# version 15
# baseline (speedup 1.0000x reference)
import contextlib
import numpy as np
import ml_dtypes

import concourse.bass as bass
import concourse.mybir as mybir
from concourse.bass_utils import run_bass_kernel_spmd

F32 = mybir.dt.float32
BF16 = mybir.dt.bfloat16
AF = mybir.ActivationFunctionType
ALU = mybir.AluOpType
BF = ml_dtypes.bfloat16

NCH = 32
T = 1024
NT = 4
TT = 256
NG = 16
HPG = 2
NB = 4
BS = 256
EPS = 1e-5

ENGS = ("sync", "gpsimd", "tensor", "scalar", "vector")


def _tables():
    base = 2.0 ** (-8.0 / 32.0)
    factor = 1.0 - 1.0 / (31.0 + 1e-5) + 1e-5
    slope = base ** (np.arange(32, dtype=np.float64) + 1.0) * factor
    r = np.arange(BS, dtype=np.float64) + 1.0
    qd = np.exp(-slope[:, None] * r[None, :])
    kd = np.exp(-slope[:, None] * (BS - r)[None, :])
    diff = r[:, None] - r[None, :]
    dd = np.where(diff >= 0,
                  np.exp(-slope[:, None, None] * diff[None]), 0.0)
    bd = np.exp(-slope * BS)

    qdb = np.broadcast_to(qd.astype(np.float32).astype(BF)[:, None, :],
                          (32, 128, 256)).copy()
    ddT = np.zeros((32, 128, 512), dtype=BF)
    for mt in range(2):
        ddT[:, :, mt * 256:(mt + 1) * 256] = np.transpose(
            dd[:, :, mt * 128:(mt + 1) * 128], (0, 2, 1)
        ).astype(np.float32).astype(BF)
    kdh = np.zeros((128, 64), dtype=np.float32)
    for h in range(32):
        for mt in range(2):
            kdh[:, h * 2 + mt] = kd[h, mt * 128:(mt + 1) * 128]
    return qdb, ddT, kdh, [float(x) for x in bd]


_QDB, _DDT, _KDH, _BD = _tables()
_E8 = np.zeros((8, 128), dtype=BF)
for _h in range(8):
    _E8[_h, 16 * _h:16 * _h + 16] = BF(8.0)



class Sem:
    def __init__(self, nc, ctx, name):
        self.h = ctx.enter_context(nc.semaphore(name))
        self.v = 0

    def inc(self, n=1):
        self.v += n
        return self.v


class Prog:
    def __init__(self):
        self.q = {e: [] for e in ENGS}

    def em(self, eng, fn):
        self.q[eng].append(fn)

    def flush(self, nc):
        with nc.Block() as block:
            @block.sync
            def _(e):
                for f in self.q["sync"]:
                    f(e)

            @block.gpsimd
            def _(e):
                for f in self.q["gpsimd"]:
                    f(e)

            @block.tensor
            def _(e):
                for f in self.q["tensor"]:
                    f(e)

            @block.scalar
            def _(e):
                for f in self.q["scalar"]:
                    f(e)

            @block.vector
            def _(e):
                for f in self.q["vector"]:
                    f(e)


def build_nc(replica_groups=None, debug=False):
    if replica_groups is None:
        replica_groups = [[0, 1], [2, 3], [4, 5], [6, 7]]
    nc = bass.Bass("TRN2", target_bir_lowering=False, debug=False)

    dram = lambda n, s, d, k: nc.dram_tensor(n, s, d, kind=k)
    x0d = dram("x0", [NT, 128, NCH * TT], F32, "ExternalInput")
    wqkvd = dram("wqkv", [96, 128, NCH * 128], F32, "ExternalInput")
    wgated = dram("wgate", [32, 128, NCH * 128], F32, "ExternalInput")
    woutd = dram("wout", [32, 128, NCH * 128], F32, "ExternalInput")
    Ad = dram("Aw", [128, NCH * 128], F32, "ExternalInput")
    Bd = dram("Bw", [128, NCH * 128], F32, "ExternalInput")
    Wrd = dram("Wrw", [128, NCH * 8], F32, "ExternalInput")
    nwd = dram("nww", [128, 32], F32, "ExternalInput")
    brd = dram("brw", [8, 1], F32, "ExternalInput")
    m0d = dram("m0w", [128, 1], F32, "ExternalInput")
    kdd = dram("kdw", [128, 64], F32, "ExternalInput")
    qdbd = dram("qdbw", [NG, 128, HPG * 256], BF16, "ExternalInput")
    ddTd = dram("ddTw", [32, 128, 512], BF16, "ExternalInput")
    Ed = dram("Ew", [8, 128], F32, "ExternalInput")
    onesd = dram("onesw", [128, 1], BF16, "ExternalInput")
    negonesd = dram("negonesw", [128, 1], BF16, "ExternalInput")
    ones1x128d = dram("ones1x128w", [1, 128], F32, "ExternalInput")
    ones1x8d = dram("ones1x8w", [1, 8], F32, "ExternalInput")
    ones8d = dram("ones8w", [8, 1], F32, "ExternalInput")
    epsd = dram("epsw", [1, 1], F32, "ExternalInput")

    yd = dram("y", [NT, 128, NCH * TT], F32, "ExternalOutput")

    x1d = dram("x1spill", [NT, 128, NCH * TT], F32, "Internal")
    x2d = dram("x2spill", [32, 128, T], F32, "Internal")
    kvsndd = dram("kvsnd", [NG, 128, HPG * 128], F32, "Internal")
    kvrcvd = dram("kvrcv", [NG, 2 * 128, HPG * 128], F32, "Internal")

    dbg = {}
    if debug:
        dbg["ln2"] = dram("dbg_ln2", [128, NCH * T], BF16, "ExternalOutput")
        dbg["attn"] = dram("dbg_attn", [128, NCH * T], BF16, "ExternalOutput")
        dbg["rstd"] = dram("dbg_rstd", [1, T], F32, "ExternalOutput")

    ctx = contextlib.ExitStack()
    with ctx:
        sb = lambda n, s, d: ctx.enter_context(nc.sbuf_tensor(n, s, d))
        sem = lambda n: Sem(nc, ctx, n)

        ln2 = sb("ln2", [128, NCH * T], BF16)
        rmsacc = sb("rmsacc", [1, T], F32)
        onesb = sb("onesb", [128, 1], BF16)
        ones1x128 = sb("ones1x128", [1, 128], F32)
        ones1x8 = sb("ones1x8", [1, 8], F32)
        ones8 = sb("ones8", [8, 1], F32)
        negonesb = sb("negonesb", [128, 1], BF16)
        nwsb = sb("nwsb", [128, 32], F32)
        brsb = sb("brsb", [8, 1], F32)
        m0sb = sb("m0sb", [128, 1], F32)
        kdsb = sb("kdsb", [128, 64], F32)
        Wrsb = sb("Wrsb", [128, NCH * 8], BF16)
        epssb = sb("epssb", [1, 1], F32)

        s_pe = sem("s_pe")
        s_act = sem("s_act")
        s_dve = sem("s_dve")
        s_pool = sem("s_pool")
        s_c0 = sem("s_c0")
        s_c0g = sem("s_c0g")
        s_dbg = sem("s_dbg")
        s_x0 = sem("s_x0")
        s_x0b = [sem("s_x0b0"), sem("s_x0b1")]
        s_x1w = sem("s_x1w")
        s_x1r = [sem("s_x1r0"), sem("s_x1r1")]
        s_x2w = [sem("s_x2w0"), sem("s_x2w1")]
        s_x2r = sem("s_x2r")
        s_x2rb = [sem("s_x2rb0"), sem("s_x2rb1")]
        s_w = [sem("s_w0"), sem("s_w1")]
        s_w4 = [sem("s_w40"), sem("s_w41")]
        s_tr = [sem("s_tr0"), sem("s_tr1")]
        s_tab = [sem("s_tab0"), sem("s_tab1")]
        s_kvs = sem("s_kvs")
        s_cc = sem("s_cc")
        s_kvr = [sem("s_kvr0"), sem("s_kvr1")]
        s_ab = sem("s_ab")
        s_y = sem("s_y")

        def emit_ffn_ln(p, tb, x_f32_src, x_b16_src, res_dst, res_sem,
                        pre_waits, second_ln_dest):
            xf, bufB, bufC, xres = tb["xf"], tb["bufB"], tb["bufC"], tb["xres"]
            Asb, Bsb = tb["Asb"], tb["Bsb"]
            b_sum, b_sq, b_bc = tb["b_sum"], tb["b_sq"], tb["b_bc"]
            b_rt, b_low, b_sex = tb["b_rt"], tb["b_low"], tb["b_sex"]
            pf = tb["pf"]
            exps, lowp = tb["exps"], tb["lowp"]
            vec = [tb["vec"][0:1, 4 * TT * i:4 * TT * (i + 1)] for i in range(2)]
            rb1 = tb["rb1"]
            rmb, rbb = tb["rmb"], tb["rbb"]
            lowf, rtf = tb["lowf"], tb["rtf"]
            negA, negWr = tb["negA"], tb["negWr"]
            Eb = tb["Eb"]
            sx, sxb = tb["sx"], tb["sxb"]
            HC = NCH // 2 * TT

            reg = dict(stats_free=0, rt_free=0, low_free=0,
                       sex_free=0, lowp_free=0, xf_free=0, xres_free=0,
                       bufC_free=0, bc_free=None,
                       bufB_free=[None, None], vec_free=[0, 0],
                       rmb_free=[0, 0])
            fwd = {}

            def wait_tag(e, tag):
                if not tag:
                    return
                eng, v = tag
                if v:
                    e.wait_ge({"pe": s_pe.h, "act": s_act.h,
                               "dve": s_dve.h}[eng], v)

            def t_colsum(e):
                mm = None
                for c in range(NCH):
                    mm = e.matmul(b_low[0:128, 0:1], Asb[:, c * 128:(c + 1) * 128],
                                  negonesb[:], start=(c == 0),
                                  stop=(c == NCH - 1))
                for c in range(NCH):
                    mm = e.matmul(b_rt[0:8, 0:1], Wrsb[:, c * 8:(c + 1) * 8],
                                  negonesb[:], start=(c == 0),
                                  stop=(c == NCH - 1))
                mm.then_inc(s_pe.h, 1)
            p.em("tensor", t_colsum)
            v_cs = s_pe.inc()

            def v_colsum(e, vp=v_cs):
                e.wait_ge(s_pe.h, vp)
                e.tensor_copy(negA[:], b_low[0:128, 0:1])
                e.tensor_copy(negWr[:], b_rt[0:8, 0:1]).then_inc(s_dve.h, 1)
            p.em("vector", v_colsum)
            reg["rt_free"] = reg["low_free"] = s_dve.inc()

            def stats_chain(src_b16, vecb, act_wait, want_mr, load_wait,
                            vec_gate):
                r_bufC = reg["bufC_free"]
                NQ = NCH // 2
                QC = NQ * TT
                sq_v = [s_act.inc() for _ in range(2)]
                st_v = [s_pe.inc() for _ in range(2)]

                def a_sqs(e, s=src_b16, rc=r_bufC, aw=act_wait, lw=load_wait,
                          sq_v=sq_v, st_v=st_v):
                    wait_tag(e, aw)
                    if lw:
                        e.wait_ge(lw[0].h, lw[1])
                    if rc:
                        e.wait_ge(s_pe.h, rc)
                    for q in range(2):
                        if q > 0:
                            e.wait_ge(s_pe.h, st_v[q - 1])
                        e.activation(bufC[:], s[:, q * QC:(q + 1) * QC],
                                     AF.Square).then_inc(s_act.h, 1)
                p.em("scalar", a_sqs)

                r_stats = reg["stats_free"]

                def t_sts(e, s=src_b16, rs=r_stats, lw=load_wait,
                          sq_v=sq_v, st_v=st_v):
                    if lw:
                        e.wait_ge(lw[0].h, lw[1])
                    if rs:
                        e.wait_ge(s_dve.h, rs)
                    for c in range(NCH):
                        e.matmul(b_sum[0:1, 0:TT], onesb[:],
                                 s[:, c * TT:(c + 1) * TT],
                                 start=(c == 0), stop=(c == NCH - 1))
                    for q in range(2):
                        e.wait_ge(s_act.h, sq_v[q])
                        mm = None
                        for c in range(NQ):
                            mm = e.matmul(b_sq[0:1, 0:TT], onesb[:],
                                          bufC[:, c * TT:(c + 1) * TT],
                                          start=(q == 0 and c == 0),
                                          stop=(q == 1 and c == NQ - 1))
                        mm.then_inc(s_pe.h, 1)
                p.em("tensor", t_sts)
                v_pe_stats = st_v[1]
                reg["bufC_free"] = v_pe_stats

                vv = [s_dve.inc() for _ in range(3)]

                def v_vec(e, vp=v_pe_stats, vv=vv, vecb=vecb, vg=vec_gate):
                    e.wait_ge(s_pe.h, vp)
                    wait_tag(e, vg)
                    e.tensor_scalar(vecb[0:1, 0:TT], b_sum[0:1, 0:TT],
                                    1.0 / 4096.0, None, ALU.mult)
                    e.tensor_scalar(vecb[0:1, TT:2 * TT], b_sq[0:1, 0:TT],
                                    1.0 / 4096.0, None,
                                    ALU.mult).then_inc(s_dve.h, 1)
                    e.wait_ge(s_dve.h, vv[0])
                    e.tensor_tensor(vecb[0:1, 2 * TT:3 * TT], vecb[0:1, 0:TT],
                                    vecb[0:1, 0:TT],
                                    ALU.mult).then_inc(s_dve.h, 1)
                    e.wait_ge(s_dve.h, vv[1])
                    e.scalar_tensor_tensor(
                        vecb[0:1, 2 * TT:3 * TT], vecb[0:1, 2 * TT:3 * TT],
                        -1.0, vecb[0:1, TT:2 * TT],
                        ALU.mult, ALU.add).then_inc(s_dve.h, 1)
                p.em("vector", v_vec)
                v_dve_var = vv[2]
                reg["stats_free"] = v_dve_var

                v_lnx = s_act.inc()

                def a_rstd(e, vd=v_dve_var, vl=v_lnx, vecb=vecb):
                    e.wait_ge(s_dve.h, vd)
                    e.activation(vecb[0:1, 2 * TT:3 * TT],
                                 vecb[0:1, 2 * TT:3 * TT], AF.Ln,
                                 bias=epssb[0:1, :]).then_inc(s_act.h, 1)
                    e.wait_ge(s_act.h, vl)
                    e.activation(vecb[0:1, 3 * TT:4 * TT],
                                 vecb[0:1, 2 * TT:3 * TT], AF.Exp,
                                 scale=-0.5).then_inc(s_act.h, 1)
                p.em("scalar", a_rstd)
                v_rstd = s_act.inc()

                if want_mr:
                    def v_mr(e, va=v_rstd, vecb=vecb):
                        e.wait_ge(s_act.h, va)
                        e.tensor_tensor(vecb[0:1, TT:2 * TT], vecb[0:1, 0:TT],
                                        vecb[0:1, 3 * TT:4 * TT],
                                        ALU.mult).then_inc(s_dve.h, 1)
                    p.em("vector", v_mr)
                    v_pre_bc = ("dve", s_dve.inc())
                    fr_off = TT
                else:
                    v_pre_bc = ("act", v_rstd)
                    fr_off = 0

                r_bc = reg["bc_free"]

                def t_bc(e, vd=v_pre_bc, rb=r_bc, fo=fr_off, vecb=vecb):
                    wait_tag(e, vd)
                    wait_tag(e, rb)
                    e.matmul(b_bc[:, 0:TT], ones1x128[:],
                             vecb[0:1, fo:fo + TT],
                             start=True, stop=False)
                    e.matmul(b_bc[:, TT:2 * TT], ones1x128[:],
                             vecb[0:1, 3 * TT:4 * TT],
                             start=False, stop=True).then_inc(s_pe.h, 1)
                p.em("tensor", t_bc)
                v_pe_bc = s_pe.inc()
                return v_pe_bc, v_pe_stats, v_rstd

            def emit_SL(t0):
                par = t0 % 2

                def sy_load(e, t0=t0,
                            pw=(pre_waits if t0 == 0 else [])):
                    for (sw, vv_) in pw:
                        e.wait_ge(sw.h, vv_)
                    rx = fwd.get(("xf", t0 - 1), 0)
                    if rx:
                        e.wait_ge(s_dve.h, rx)
                    srcs = x_f32_src(t0)
                    if not isinstance(srcs, list):
                        srcs = [(xf[:], srcs)]
                    for (dst, ap_) in srcs:
                        e.dma_start(out=dst, in_=ap_).then_inc(sx.h, 16)
                p.em("sync", sy_load)
                srcs0 = x_f32_src(t0)
                nld = len(srcs0) if isinstance(srcs0, list) else 1
                vxf = sx.inc(16 * nld)

                sxbp = sxb[par]

                def gp_load(e, t0=t0, sxbp=sxbp,
                            pw=(pre_waits if t0 == 0 else [])):
                    for (sw, vv_) in pw:
                        e.wait_ge(sw.h, vv_)
                    rb = fwd.get(("bufB", t0 - 2))
                    if rb:
                        wait_tag(e, rb)
                    srcs = x_b16_src(t0)
                    if not isinstance(srcs, list):
                        srcs = [(bufB[par][:], srcs)]
                    for (dst, ap_) in srcs:
                        e.dma_start(out=dst, in_=ap_).then_inc(sxbp.h, 16)
                p.em("gpsimd", gp_load)
                srcs0 = x_b16_src(t0)
                nld = len(srcs0) if isinstance(srcs0, list) else 1
                vxb = sxbp.inc(16 * nld)
                return dict(par=par, vxb=vxb, vxf=vxf, sxbp=sxbp)

            def emit_SS(t0, L):
                par = L["par"]
                vf_gate = reg["vec_free"][par]
                v_pe_bc, v_stats, _ = stats_chain(
                    bufB[par], vec[par], None, True,
                    (L["sxbp"], L["vxb"]), vf_gate)
                r_rmb = reg["rmb_free"][par]

                def a_bc(e, vp=v_pe_bc, rb=r_rmb, par=par):
                    e.wait_ge(s_pe.h, vp)
                    if rb:
                        e.wait_ge(s_dve.h, rb)
                    e.copy(rmb[par][:], b_bc[:, 0:TT])
                    e.copy(rbb[par][:], b_bc[:, TT:2 * TT]).then_inc(s_act.h, 1)
                p.em("scalar", a_bc)
                v_a_bc = s_act.inc()
                reg["bc_free"] = ("act", v_a_bc)
                L["v_a_bc"] = v_a_bc

            def emit_Frl(t0, L):
                par = L["par"]
                r_rt, r_low = reg["rt_free"], reg["low_free"]

                def t_rl(e, rr=r_rt, rl=r_low, vb=L["vxb"], sp=L["sxbp"],
                         par=par):
                    e.wait_ge(sp.h, vb)
                    if rr:
                        e.wait_ge(s_dve.h, rr)
                    for c in range(NCH):
                        e.matmul(b_rt[0:8, 0:TT], Wrsb[:, c * 8:(c + 1) * 8],
                                 bufB[par][:, c * TT:(c + 1) * TT],
                                 start=(c == 0), stop=(c == NCH - 1))
                    if rl:
                        e.wait_ge(s_dve.h, rl)
                    mm = None
                    for c in range(NCH):
                        mm = e.matmul(b_low[:, 0:TT],
                                      Asb[:, c * 128:(c + 1) * 128],
                                      bufB[par][:, c * TT:(c + 1) * TT],
                                      start=(c == 0), stop=(c == NCH - 1))
                    mm.then_inc(s_pe.h, 1)
                p.em("tensor", t_rl)
                L["v_pe_rl"] = s_pe.inc()

            def emit_Fmid(t0, L):
                par = L["par"]
                vfx = [s_dve.inc() for _ in range(4)]

                def v_fix(e, vp=L["v_pe_rl"], va=L["v_a_bc"], vfx=vfx,
                          par=par):
                    e.wait_ge(s_pe.h, vp)
                    e.wait_ge(s_act.h, va)
                    e.tensor_tensor(rtf[0:8, :], b_rt[0:8, 0:TT],
                                    rbb[par][0:8, :],
                                    ALU.mult).then_inc(s_dve.h, 1)
                    e.wait_ge(s_dve.h, vfx[0])
                    e.scalar_tensor_tensor(
                        rtf[0:8, :], rmb[par][0:8, :], negWr[0:8, :],
                        rtf[0:8, :], ALU.mult, ALU.add).then_inc(s_dve.h, 1)
                    e.tensor_tensor(lowf[:], b_low[:, 0:TT], rbb[par][:],
                                    ALU.mult).then_inc(s_dve.h, 1)
                    e.wait_ge(s_dve.h, vfx[2])
                    e.scalar_tensor_tensor(
                        lowf[:], rmb[par][:], negA[:], lowf[:],
                        ALU.mult, ALU.add).then_inc(s_dve.h, 1)
                p.em("vector", v_fix)
                v_fix_rt, v_fix_low = vfx[1], vfx[3]
                reg["rt_free"] = v_fix_rt
                reg["rmb_free"][par] = v_fix_low

                def a_exp(e, vd=v_fix_rt):
                    e.wait_ge(s_dve.h, vd)
                    e.activation(exps[0:8, :], rtf[0:8, :], AF.Exp,
                                 bias=brsb[0:8, :]).then_inc(s_act.h, 1)
                p.em("scalar", a_exp)
                v_act_exp = s_act.inc()

                r_sex = reg["sex_free"]

                def t_se(e, va=v_act_exp, rs=r_sex):
                    e.wait_ge(s_act.h, va)
                    if rs:
                        e.wait_ge(s_dve.h, rs)
                    e.matmul(b_sex[:, 0:TT], Eb[:], exps[0:8, :],
                             start=True, stop=True)
                    e.matmul(b_rt[0:1, TT:2 * TT], ones8[:], exps[0:8, :],
                             start=True, stop=True).then_inc(s_pe.h, 1)
                p.em("tensor", t_se)
                v_pe_se = s_pe.inc()

                def v_rinv(e, vp=v_pe_se, par=par):
                    e.wait_ge(s_pe.h, vp)
                    e.reciprocal(vec[par][0:1, 3 * TT:4 * TT],
                                 b_rt[0:1, TT:2 * TT]).then_inc(s_dve.h, 1)
                p.em("vector", v_rinv)
                v_dve_rinv = s_dve.inc()

                def t_rb8(e, vd=v_dve_rinv, par=par):
                    e.wait_ge(s_dve.h, vd)
                    e.matmul(b_rt[:, 0:TT], ones1x128[:],
                             vec[par][0:1, 3 * TT:4 * TT],
                             start=True, stop=True).then_inc(s_pe.h, 1)
                p.em("tensor", t_rb8)
                v_pe_rb8 = s_pe.inc()
                reg["vec_free"][par] = ("pe", v_pe_rb8)

                r_lowp = reg["lowp_free"]
                vm = [s_dve.inc() for _ in range(2)]

                def v_lowp(e, vfl=v_fix_low, vse=v_pe_se, vrb=v_pe_rb8,
                           rl=r_lowp, vm=vm):
                    e.wait_ge(s_dve.h, vfl)
                    e.wait_ge(s_pe.h, vse)
                    e.tensor_tensor(lowf[:], lowf[:], b_sex[:, 0:TT],
                                    ALU.mult).then_inc(s_dve.h, 1)
                    e.wait_ge(s_dve.h, vm[0])
                    e.wait_ge(s_pe.h, vrb)
                    if rl:
                        e.wait_ge(s_pe.h, rl)
                    e.tensor_tensor(lowp[:], lowf[:], b_rt[:, 0:TT],
                                    ALU.mult).then_inc(s_dve.h, 1)
                p.em("vector", v_lowp)
                lowp_ready = vm[1]
                reg["sex_free"] = vm[0]
                reg["low_free"] = vm[0]
                reg["rt_free"] = lowp_ready

                v_fp = {}
                last_f_pe = None
                r_xres = reg["xres_free"]
                for ocp in range(16):
                    bank = pf[ocp % 2]

                    def t_f(e, ocp=ocp, bank=bank, vl=lowp_ready, fp=v_fp):
                        if ocp == 0:
                            e.wait_ge(s_dve.h, vl)
                        if ocp >= 2:
                            e.wait_ge(s_dve.h, fp[ocp - 2])
                        e.matmul(bank[:, 0:TT],
                                 Bsb[:, (2 * ocp) * 128:(2 * ocp + 1) * 128],
                                 lowp[:], start=True, stop=False)
                        e.matmul(bank[:, TT:2 * TT],
                                 Bsb[:, (2 * ocp + 1) * 128:(2 * ocp + 2) * 128],
                                 lowp[:], start=False,
                                 stop=True).then_inc(s_pe.h, 1)
                    p.em("tensor", t_f)
                    v_pe_f = s_pe.inc()
                    last_f_pe = v_pe_f

                    def v_res(e, ocp=ocp, bank=bank, vp=v_pe_f, t0=t0,
                              vxf=L["vxf"], rx=r_xres):
                        e.wait_ge(s_pe.h, vp)
                        if ocp == 0:
                            e.wait_ge(sx.h, vxf)
                            if rx:
                                e.wait_ge(s_dve.h, rx)
                            if t0 > 0:
                                e.wait_ge(res_sem.h, 16 * t0)
                        e.tensor_tensor(xres[:, ocp * 512:(ocp + 1) * 512],
                                        bank[:, 0:2 * TT],
                                        xf[:, ocp * 512:(ocp + 1) * 512],
                                        ALU.add).then_inc(s_dve.h, 1)
                    p.em("vector", v_res)
                    v_fp[ocp] = s_dve.inc()
                res_done = v_fp[15]
                reg["lowp_free"] = last_f_pe
                fwd[("xf", t0)] = res_done
                L["res_done"] = res_done
                L["last_f_pe"] = last_f_pe

                def sy_spill(e, t0=t0, vd=res_done):
                    e.wait_ge(s_dve.h, vd)
                    e.dma_start(out=res_dst(t0),
                                in_=xres[:]).then_inc(res_sem.h, 16)
                p.em("sync", sy_spill)
                res_sem.inc(16)

                if second_ln_dest is None:
                    reg["xres_free"] = res_done
                    fwd[("bufB", t0)] = ("pe", L["v_pe_rl"])

            def emit_Fln2(t0, L):
                par = L["par"]

                def a_x1b(e, vd=L["res_done"], vpe=L["last_f_pe"], par=par):
                    e.wait_ge(s_dve.h, vd)
                    e.wait_ge(s_pe.h, vpe)
                    e.copy(bufB[par][:], xres[:]).then_inc(s_act.h, 1)
                p.em("scalar", a_x1b)
                v_x1b = s_act.inc()

                v_pe_bc2, v_stats2, _ = stats_chain(
                    bufB[par], vec[par], ("act", v_x1b), False,
                    (s_act, v_x1b), reg["vec_free"][par])
                fwd[("bufB", t0)] = ("pe", v_stats2)

                def a_rb(e, vp=v_pe_bc2):
                    e.wait_ge(s_pe.h, vp)
                    e.copy(rb1[:], b_bc[:, TT:2 * TT]).then_inc(s_act.h, 1)
                p.em("scalar", a_rb)
                v_act_rb = s_act.inc()

                vn = [s_dve.inc() for _ in range(2)]

                def v_norm(e, va=v_act_rb, t0=t0, vp=v_stats2, vn=vn,
                           par=par):
                    e.wait_ge(s_act.h, va)
                    e.wait_ge(s_pe.h, vp)
                    tt_ = None
                    for c in range(NCH):
                        tt_ = e.tensor_tensor(
                            second_ln_dest(c, t0),
                            xres[:, c * TT:(c + 1) * TT],
                            b_bc[:, 0:TT], ALU.subtract)
                    tt_.then_inc(s_dve.h, 1)
                    e.wait_ge(s_dve.h, vn[0])
                    tt_ = None
                    for c in range(NCH):
                        tt_ = e.tensor_tensor(
                            second_ln_dest(c, t0),
                            second_ln_dest(c, t0),
                            rb1[:], ALU.mult)
                    tt_.then_inc(s_dve.h, 1)
                p.em("vector", v_norm)
                ln2_ready = vn[1]
                reg["bc_free"] = ("dve", ln2_ready)
                reg["xres_free"] = ln2_ready
                reg["vec_free"][par] = ("pe", v_pe_bc2)

            L = {}
            L[0] = emit_SL(0)
            emit_SS(0, L[0])
            L[1] = emit_SL(1)
            emit_Frl(0, L[0])
            emit_SS(1, L[1])
            emit_Fmid(0, L[0])
            for t0 in range(1, NT):
                if t0 + 1 < NT:
                    L[t0 + 1] = emit_SL(t0 + 1)
                emit_Frl(t0, L[t0])
                if second_ln_dest is not None:
                    emit_Fln2(t0 - 1, L[t0 - 1])
                if t0 + 1 < NT:
                    emit_SS(t0 + 1, L[t0 + 1])
                emit_Fmid(t0, L[t0])
            if second_ln_dest is not None:
                emit_Fln2(NT - 1, L[NT - 1])

        def mk_ffn_bufs(cx, pref):
            sbp = lambda n, s, d: cx.enter_context(
                nc.sbuf_tensor(pref + n, s, d))
            psp = lambda n: cx.enter_context(
                nc.psum_tensor(pref + n, [128, 512], F32))
            return dict(
                xf=sbp("xf", [128, NCH * TT], F32),
                bufB=[sbp(f"bufB{i}", [128, NCH * TT], BF16) for i in range(2)],
                bufC=sbp("bufC", [128, NCH * TT // 2], BF16),
                xres=sbp("xres", [128, NCH * TT], F32),
                Asb=sbp("A", [128, NCH * 128], BF16),
                Bsb=sbp("B", [128, NCH * 128], BF16),
                exps=sbp("exps", [8, TT], F32),
                lowp=sbp("lowp", [128, TT], BF16),
                vec=sbp("vec", [1, 8 * TT], F32),
                rb1=sbp("rb1", [128, TT], BF16),
                rmb=[sbp(f"rmb{i}", [128, TT], F32) for i in range(2)],
                rbb=[sbp(f"rbb{i}", [128, TT], F32) for i in range(2)],
                lowf=sbp("lowf", [128, TT], F32),
                rtf=sbp("rtf", [8, TT], F32),
                negA=sbp("negA", [128, 1], F32),
                negWr=sbp("negWr", [8, 1], F32),
                Eb=sbp("Eb", [8, 128], F32),
                b_sum=psp("bsum"), b_sq=psp("bsq"), b_bc=psp("bbc"),
                b_rt=psp("brt"), b_low=psp("blow"), b_sex=psp("bsex"),
                pf=[psp("pf0"), psp("pf1")],
            )

        def emit_ab_loads(p, tb, pre_waits=()):
            def gp_ab(e, pw=tuple(pre_waits)):
                for (sw, vv_) in pw:
                    e.wait_ge(sw.h, vv_)
                e.dma_start(out=tb["Asb"][:], in_=Ad.ap()).then_inc(s_ab.h, 16)
                e.dma_start(out=tb["Bsb"][:], in_=Bd.ap()).then_inc(s_ab.h, 16)
                e.dma_start(out=tb["Eb"][:], in_=Ed.ap()).then_inc(s_ab.h, 16)
            p.em("gpsimd", gp_ab)
            s_ab.inc(48)
            vab = s_ab.v

            def t_wab(e, v=vab):
                e.wait_ge(s_ab.h, v)
            p.em("tensor", t_wab)

        with contextlib.ExitStack() as c1:
            p1 = Prog()

            def sy_c0(e):
                e.dma_start(out=onesb[:], in_=onesd.ap()).then_inc(s_c0.h, 16)
                e.dma_start(out=negonesb[:], in_=negonesd.ap()).then_inc(s_c0.h, 16)
                e.dma_start(out=ones1x128[:], in_=ones1x128d.ap()).then_inc(s_c0.h, 16)
                e.dma_start(out=ones1x8[:], in_=ones1x8d.ap()).then_inc(s_c0.h, 16)
                e.dma_start(out=ones8[:], in_=ones8d.ap()).then_inc(s_c0.h, 16)
                e.dma_start(out=nwsb[:], in_=nwd.ap()).then_inc(s_c0.h, 16)
                e.dma_start(out=brsb[:], in_=brd.ap()).then_inc(s_c0.h, 16)
                e.dma_start(out=m0sb[:], in_=m0d.ap()).then_inc(s_c0.h, 16)
                e.dma_start(out=kdsb[:], in_=kdd.ap()).then_inc(s_c0.h, 16)
                e.dma_start(out=epssb[:], in_=epsd.ap()).then_inc(s_c0.h, 16)
            p1.em("sync", sy_c0)
            s_c0.inc(16 * 10)

            def gp_c1(e):
                e.dma_start(out=Wrsb[:], in_=Wrd.ap()).then_inc(s_c0g.h, 16)
            p1.em("gpsimd", gp_c1)
            s_c0g.inc(16)
            CONST = s_c0.v
            CONSTG = s_c0g.v

            tb1 = mk_ffn_bufs(c1, "p1_")
            tb1["sx"], tb1["sxb"] = s_x0, s_x0b
            emit_ab_loads(p1, tb1)

            def t_wc(e, v=CONST, vg=CONSTG):
                e.wait_ge(s_c0.h, v)
                e.wait_ge(s_c0g.h, vg)
            p1.em("tensor", t_wc)

            def a_wc(e, v=CONST):
                e.wait_ge(s_c0.h, v)
            p1.em("scalar", a_wc)

            def v_wc(e, v=CONST):
                e.wait_ge(s_c0.h, v)
            p1.em("vector", v_wc)

            emit_ffn_ln(
                p1, tb1,
                x_f32_src=lambda t0: x0d.ap()[t0],
                x_b16_src=lambda t0: x0d.ap()[t0],
                res_dst=lambda t0: x1d.ap()[t0],
                res_sem=s_x1w, pre_waits=[],
                second_ln_dest=lambda c, t0: ln2[:, c * T + t0 * TT:
                                                 c * T + t0 * TT + TT])

            def sy_fin1(e, v=s_x1w.v):
                e.wait_ge(s_x1w.h, v)
            p1.em("sync", sy_fin1)

            if debug:
                def gp_d1(e, v=s_x1w.v, vd=s_dve.v):
                    e.wait_ge(s_x1w.h, v)
                    e.wait_ge(s_dve.h, vd)
                    e.dma_start(out=dbg["ln2"].ap(), in_=ln2[:]).then_inc(s_dbg.h, 16)
                p1.em("gpsimd", gp_d1)
                s_dbg.inc(16)

            p1.flush(nc)

        with contextlib.ExitStack() as cA:
            attn = cA.enter_context(nc.sbuf_tensor("attn", [128, NCH * T], BF16))
            psA = lambda n: cA.enter_context(nc.psum_tensor(n, [128, 512], F32))
            pmm = [psA(f"p2mm{i}") for i in range(4)]
            psc = [psA("p2sc0"), psA("p2sc1")]
            pcs = psA("p2cs")
            pkv = psA("p2kv")
            c2 = cA.enter_context(contextlib.ExitStack())
            sb2 = lambda n, s, d: c2.enter_context(nc.sbuf_tensor(n, s, d))
            qfm = [sb2(f"qfm{i}", [128, HPG * T], BF16) for i in range(2)]
            kfm = [sb2(f"kfm{i}", [128, HPG * T], BF16) for i in range(2)]
            ktm = [sb2(f"ktm{i}", [128, HPG * T], BF16) for i in range(2)]
            vtm = [sb2(f"vtm{i}", [128, HPG * T], BF16) for i in range(2)]
            vscr0 = sb2("vscr0", [128, HPG * T], BF16)
            vscr = [vscr0, vscr0]
            qdq = [sb2(f"qdq{i}", [128, HPG * T], BF16) for i in range(2)]
            gsig = [sb2(f"gsig{i}", [128, HPG * T], BF16) for i in range(2)]
            wbuf = [sb2(f"wbuf{i}", [128, 16 * 128], BF16) for i in range(2)]
            ddb = [sb2(f"ddb{i}", [128, 512], BF16) for i in range(2)]
            qdbb = [sb2(f"qdbb{i}", [128, HPG * 256], BF16) for i in range(2)]
            kvf = [sb2(f"kvf{i}", [128, HPG * 128], F32) for i in range(2)]
            kvb = [sb2(f"kvb{i}", [128, HPG * 3 * 128], BF16) for i in range(2)]
            kvrxb = [sb2(f"kvrxb{i}", [128, HPG * 128], BF16) for i in range(2)]
            dssb = [sb2(f"dssb{i}", [128, 512], BF16) for i in range(2)]
            sqscr = sb2("sqscr", [128, T], BF16)


            p2 = Prog()

            def v_init(e):
                e.memset(rmsacc[:], 0.0)
            p2.em("vector", v_init)

            R = dict(w_pe={}, wload=0, bp_evac={0: 0, 1: 0},
                     qkv_evac={}, tr={}, tab={}, attn_pe={}, attn_dve={},
                     evac_n={}, sc_pe={}, acc_pe={}, kvd_pe={}, ds={},
                     kvupd={}, pool_qdq={}, cc={}, kvs={}, kvr={},
                     mask={}, corr_pe={}, corr_dve={}, stats_dve={},
                     sq_act={}, g2={}, tr_after_k={}, evac_j={})

            def wsrc(g, j):
                if j < 6:
                    h = HPG * g + j // 3
                    oc = 3 * h + (j % 3)
                    return wqkvd.ap()[oc]
                return wgated.ap()[2 * g + (j - 6)]

            def emit_wloads(g):
                for j in range(8):
                    for half in range(2):
                        idx = R["wload"]
                        par = idx % 2

                        def gp_w(e, g=g, j=j, half=half, par=par, idx=idx):
                            if idx >= 2:
                                e.wait_ge(s_pe.h, R["w_pe"][idx - 2])
                            e.dma_start(
                                out=wbuf[par][:],
                                in_=wsrc(g, j)[:, half * 2048:(half + 1) * 2048]
                            ).then_inc(s_w[par].h, 16)
                        p2.em("gpsimd", gp_w)
                        s_w[par].inc(16)
                        R["wload"] += 1

            def emit_tabs(g):
                par = g % 2

                def sy_tab(e, g=g, par=par):
                    if g >= 1:
                        e.wait_ge(s_dve.h, R["ds"][(g - 1, 1, NB - 1)])
                    e.dma_start(
                        out=ddb[0][:],
                        in_=ddTd.ap()[HPG * g]).then_inc(s_tab[par].h, 16)
                    e.dma_start(
                        out=ddb[1][:],
                        in_=ddTd.ap()[HPG * g + 1]).then_inc(s_tab[par].h, 16)
                    e.dma_start(
                        out=qdbb[par][:],
                        in_=qdbd.ap()[g]).then_inc(s_tab[par].h, 16)
                p2.em("sync", sy_tab)
                s_tab[par].inc(48)
                R["tab"][g] = s_tab[par].v

            def mk_qkv_half(g, j, half):
                gp = g % 2
                widx0 = 16 * g
                bp = j % 2
                b0, b1 = pmm[2 * bp], pmm[2 * bp + 1]
                r_evac = R["bp_evac"][bp] if half == 0 else 0
                idx = widx0 + 2 * j + half
                wv = 16 * (idx // 2 + 1)

                def t_mmh(e, g=g, j=j, half=half, b0=b0, b1=b1, re_=r_evac,
                          wv=wv, idx=idx):
                    par = idx % 2
                    e.wait_ge(s_w[par].h, wv)
                    if re_:
                        e.wait_ge(s_act.h, re_)
                    mm = None
                    for ci in range(16):
                        c = half * 16 + ci
                        for t2, bank in ((0, b0), (1, b1)):
                            mm = e.matmul(
                                bank[:, 0:512],
                                wbuf[par][:, ci * 128:(ci + 1) * 128],
                                ln2[:, c * T + t2 * 512:
                                    c * T + t2 * 512 + 512],
                                start=(c == 0), stop=(c == NCH - 1))
                    mm.then_inc(s_pe.h, 1)
                R["w_pe"][idx] = s_pe.inc()
                return t_mmh

            def mk_qkv_evac(g, j):
                gp = g % 2
                bp = j % 2
                b0, b1 = pmm[2 * bp], pmm[2 * bp + 1]
                v_pe_j = R["w_pe"][16 * g + 2 * j + 1]
                if j < 6:
                    h_loc, kind = j // 3, j % 3
                else:
                    h_loc, kind = j - 6, 3

                def a_ev(e, g=g, j=j, gp=gp, b0=b0, b1=b1, vp=v_pe_j,
                         h_loc=h_loc, kind=kind):
                    e.wait_ge(s_pe.h, vp)
                    if kind == 0:
                        dst = qfm[gp]
                        if g >= 2:
                            e.wait_ge(s_pe.h, R["attn_pe"][g - 2])
                            e.wait_ge(s_pool.h, R["pool_qdq"][g - 2])
                    elif kind == 1:
                        dst = kfm[gp]
                        if g >= 2:
                            e.wait_ge(s_tr[gp].h, R["tr"][g - 2])
                    elif kind == 2:
                        dst = vscr[gp]
                        if g >= 1:
                            e.wait_ge(s_tr[(g - 1) % 2].h, R["tr"][g - 1])
                    else:
                        dst = gsig[g % 2]
                        if g >= 2:
                            e.wait_ge(s_pool.h, R["g2"][g - 2])
                    fn = AF.Sigmoid if kind == 3 else AF.Silu
                    aa = None
                    for t2 in (0, 1):
                        aa = e.activation(
                            dst[:, h_loc * T + t2 * 512:
                                h_loc * T + t2 * 512 + 512],
                            (b0 if t2 == 0 else b1)[:, 0:512], fn)
                    aa.then_inc(s_act.h, 1)
                v_evac = s_act.inc()
                R["bp_evac"][bp] = v_evac
                R["evac_j"][(g, j)] = v_evac

                tr_cl = None
                if kind in (1, 2):
                    srct = kfm[gp] if kind == 1 else vscr[gp]
                    dstt = ktm[gp] if kind == 1 else vtm[gp]

                    def sy_tr(e, gp=gp, h_loc=h_loc, srct=srct, dstt=dstt,
                              ve=v_evac, g=g):
                        e.wait_ge(s_act.h, ve)
                        if g >= 2:
                            e.wait_ge(s_pe.h, R["attn_pe"][g - 2])
                        e.dma_start_transpose(
                            dstt[:, h_loc * T:(h_loc + 1) * T].rearrange(
                                "p (j t) -> p j t", j=8),
                            srct[:, h_loc * T:(h_loc + 1) * T]
                        ).then_inc(s_tr[gp].h, 16)
                    s_tr[gp].inc(16)
                    tr_cl = sy_tr
                return a_ev, tr_cl

            def emit_pool_qdq(g):
                gp = g % 2
                tr_val = s_tr[gp].v

                def vq(e, g=g, gp=gp, tr_val=tr_val):
                    e.wait_ge(s_tab[gp].h, R["tab"][g])
                    if g >= 2:
                        e.wait_ge(s_pe.h, R["corr_pe"].get(g - 2, 0))
                    for h_loc in range(HPG):
                        h = HPG * g + h_loc
                        e.wait_ge(s_act.h, R["evac_j"][(g, 3 * h_loc)])
                        e.wait_ge(s_tr[gp].h, tr_val)
                        qv = qfm[gp][:, h_loc * T:(h_loc + 1) * T].rearrange(
                            "p (n t) -> p n t", n=NB)
                        qdv = qdbb[gp][:, h_loc * 256:(h_loc + 1) * 256][
                            :, None, :].broadcast_to([128, NB, 256])
                        ov = qdq[gp][:, h_loc * T:(h_loc + 1) * T].rearrange(
                            "p (n t) -> p n t", n=NB)
                        e.tensor_tensor(ov, qv, qdv, ALU.mult)
                        kv_ = ktm[gp][:, h_loc * T:(h_loc + 1) * T].rearrange(
                            "p (n m t) -> p n m t", n=NB, m=2)
                        kdv = kdsb[:, 2 * h:2 * h + 2][
                            :, None, :, None].broadcast_to([128, NB, 2, 128])
                        e.tensor_tensor(kv_, kv_, kdv,
                                        ALU.mult).then_inc(s_pool.h, 1)
                p2.em("vector", vq)
                s_pool.inc()
                R["pool_qdq"][g] = s_pool.inc()

            def mk_kvd(g, n):
                gp = g % 2

                def t_kvd(e, g=g, gp=gp, n=n):
                    if n == 0:
                        e.wait_ge(s_pool.h, R["pool_qdq"][g])
                        e.wait_ge(s_tr[gp].h, R["tr"][g])
                        if g >= 1:
                            e.wait_ge(s_dve.h, R["kvupd"][(g - 1, 1, NB - 1)])
                    else:
                        e.wait_ge(s_dve.h, R["kvupd"][(g, 1, n - 1)])
                    mm = None
                    for h_loc in range(HPG):
                        mtile = h_loc * 8 + 2 * n
                        for mt in range(2):
                            mm = e.matmul(
                                pkv[:, h_loc * 128:(h_loc + 1) * 128],
                                ktm[gp][:, (mtile + mt) * 128:
                                        (mtile + mt + 1) * 128],
                                vtm[gp][:, (mtile + mt) * 128:
                                        (mtile + mt + 1) * 128],
                                start=(h_loc == 0 and mt == 0),
                                stop=(h_loc == 1 and mt == 1))
                    mm.then_inc(s_pe.h, 1)
                R["kvd_pe"][(g, n)] = s_pe.inc()

                vks = []
                for h_loc in range(HPG):
                    if n < NB - 1:
                        vkv = [s_dve.inc() for _ in range(2)]
                    else:
                        vkv = [s_dve.inc()]

                    def v_kv(e, g=g, gp=gp, n=n, h_loc=h_loc, vkv=vkv):
                        e.wait_ge(s_pe.h, R["kvd_pe"][(g, n)])
                        sl = slice(h_loc * 128, (h_loc + 1) * 128)
                        if n == 0:
                            if g >= 2:
                                e.wait_ge(s_kvs.h, R["kvs"][g - 2])
                            op = e.tensor_copy(kvf[gp][:, sl], pkv[:, sl])
                        else:
                            op = e.scalar_tensor_tensor(
                                kvf[gp][:, sl], kvf[gp][:, sl],
                                _BD[HPG * g + h_loc], pkv[:, sl],
                                ALU.mult, ALU.add)
                        op.then_inc(s_dve.h, 1)
                        if n < NB - 1:
                            e.wait_ge(s_dve.h, vkv[0])
                            if g >= 2:
                                e.wait_ge(s_pe.h, R["attn_pe"][g - 2])
                            ks = slice((h_loc * 3 + n) * 128,
                                       (h_loc * 3 + n + 1) * 128)
                            e.tensor_copy(kvb[gp][:, ks],
                                          kvf[gp][:, sl]).then_inc(s_dve.h, 1)
                    vks.append(v_kv)
                    R["kvupd"][(g, h_loc, n)] = vkv[-1]
                return t_kvd, vks

            def mk_scores(g, n):
                gp = g % 2

                def t_sc(e, g=g, gp=gp, n=n):
                    for h_loc in range(HPG):
                        bank = psc[h_loc]
                        if n == 0 and h_loc == 0 and g >= 1:
                            e.wait_ge(s_dve.h, R["stats_dve"][g - 1])
                        if n > 0:
                            e.wait_ge(s_act.h,
                                      R["evac_n"][(g, h_loc, n - 1)])
                        base = h_loc * T + n * 256
                        e.matmul(bank[:, 0:256],
                                 kfm[gp][:, base:base + 128],
                                 qfm[gp][:, base:base + 256],
                                 start=True, stop=False)
                        e.matmul(bank[:, 256:512],
                                 kfm[gp][:, base + 128:base + 256],
                                 qfm[gp][:, base:base + 256],
                                 start=False,
                                 stop=True).then_inc(s_pe.h, 1)
                for h_loc in range(HPG):
                    R["sc_pe"][(g, h_loc, n)] = s_pe.inc()

                dscls = []
                for h_loc in range(HPG):
                    def v_ds(e, g=g, gp=gp, n=n, h_loc=h_loc):
                        bank = psc[h_loc]
                        e.wait_ge(s_pe.h, R["sc_pe"][(g, h_loc, n)])
                        if n > 0:
                            e.wait_ge(s_pe.h,
                                      R["acc_pe"][(g, h_loc, n - 1)])
                        elif g >= 1:
                            e.wait_ge(s_pe.h,
                                      R["acc_pe"][(g - 1, h_loc, NB - 1)])
                        e.tensor_tensor(
                            dssb[h_loc][:], bank[:, 0:512],
                            ddb[h_loc][:],
                            ALU.mult).then_inc(s_dve.h, 1)
                    dscls.append(v_ds)
                    R["ds"][(g, h_loc, n)] = s_dve.inc()
                return t_sc, dscls

            def mk_acc(g, n):
                gp = g % 2

                def t_acc(e, g=g, gp=gp, n=n):
                    for h_loc in range(HPG):
                        bank = psc[h_loc]
                        e.wait_ge(s_dve.h, R["ds"][(g, h_loc, n)])
                        base = h_loc * T + n * 256
                        mtile = h_loc * 8 + 2 * n
                        e.matmul(bank[:, 0:256],
                                 vtm[gp][:, mtile * 128:(mtile + 1) * 128],
                                 dssb[h_loc][:, 0:256],
                                 start=True, stop=False)
                        mm = e.matmul(
                            bank[:, 0:256],
                            vtm[gp][:, (mtile + 1) * 128:(mtile + 2) * 128],
                            dssb[h_loc][:, 256:512],
                            start=False, stop=(n == 0))
                        if n > 0:
                            e.wait_ge(s_dve.h, R["kvupd"][(g, h_loc, n - 1)])
                            ks = slice((h_loc * 3 + n - 1) * 128,
                                       (h_loc * 3 + n) * 128)
                            mm = e.matmul(
                                bank[:, 0:256],
                                kvb[gp][:, ks],
                                qdq[gp][:, base:base + 256],
                                start=False, stop=True)
                        mm.then_inc(s_pe.h, 1)
                for h_loc in range(HPG):
                    R["acc_pe"][(g, h_loc, n)] = s_pe.inc()

                acls = []
                for h_loc in range(HPG):
                    def a_acc(e, g=g, n=n, h_loc=h_loc):
                        bank = psc[h_loc]
                        e.wait_ge(s_pe.h, R["acc_pe"][(g, h_loc, n)])
                        h = HPG * g + h_loc
                        e.copy(attn[:, h * T + n * 256:h * T + n * 256 + 256],
                               bank[:, 0:256]).then_inc(s_act.h, 1)
                    acls.append(a_acc)
                    R["evac_n"][(g, h_loc, n)] = s_act.inc()
                return t_acc, acls

            def emit_collective(g):
                gp = g % 2
                v_kvs = s_kvs.inc(16)
                R["kvs"][g] = v_kvs
                v_cc = s_cc.inc()
                v_kvr = s_kvr[gp].inc(16)
                R["kvr"][g] = v_kvr

                def gp_cc(e, g=g, gp=gp, v_kvs=v_kvs, v_cc=v_cc):
                    e.wait_ge(s_dve.h, R["kv_chain_dve"][g])
                    e.dma_start(
                        out=kvsndd.ap()[g],
                        in_=kvf[gp][:]).then_inc(s_kvs.h, 16)
                    e.wait_ge(s_kvs.h, v_kvs)
                    e.collective_compute(
                        "AllGather", ALU.bypass,
                        replica_groups=replica_groups,
                        ins=[kvsndd.ap()[g].opt()],
                        outs=[kvrcvd.ap()[g].opt()],
                    ).then_inc(s_cc.h, 1)
                    e.wait_ge(s_cc.h, v_cc)
                    if g >= 2:
                        e.wait_ge(s_dve.h, R["mask"][g - 2])
                        e.wait_ge(s_pe.h, R["corr_pe"][g - 2])
                    e.dma_start(
                        out=kvrxb[gp][:],
                        in_=kvrcvd.ap()[g][0:128]
                    ).then_inc(s_kvr[gp].h, 16)
                p2.em("gpsimd", gp_cc)

            def mk_mask(g):
                gp = g % 2

                def v_mask(e, g=g, gp=gp):
                    e.wait_ge(s_kvr[gp].h, R["kvr"][g])
                    e.tensor_scalar(kvrxb[gp][:], kvrxb[gp][:], m0sb[:],
                                    None, ALU.mult).then_inc(s_dve.h, 1)
                R["mask"][g] = s_dve.inc()
                return v_mask

            def mk_corr_pair(g, pi, prev_add):
                gp = g % 2
                corr_banks = [pcs, psc[0]]
                pe_cls = []
                dve_cls = []
                for i in (2 * pi, 2 * pi + 1):
                    h_loc, n = i // NB, i % NB
                    bank = corr_banks[i % 2]

                    def t_corr(e, g=g, gp=gp, i=i, h_loc=h_loc, n=n,
                               bank=bank):
                        if i == 0:
                            e.wait_ge(s_dve.h, R["mask"][g])
                        if i == 1:
                            e.wait_ge(s_act.h, R["evac_n"][(g, 0, NB - 1)])
                        if i >= 2:
                            e.wait_ge(s_dve.h, prev_add[i - 2])
                        elif i < 2 and g >= 1:
                            e.wait_ge(s_dve.h, R["stats_dve"].get(g - 1, 0))
                        e.matmul(bank[:, 0:256],
                                 kvrxb[gp][:, h_loc * 128:(h_loc + 1) * 128],
                                 qdq[gp][:, h_loc * T + n * 256:
                                         h_loc * T + n * 256 + 256],
                                 start=True, stop=True).then_inc(s_pe.h, 1)
                    pe_cls.append(t_corr)
                    vpe = s_pe.inc()
                    R["corr_pe"][g] = vpe

                    def v_cadd(e, g=g, i=i, h_loc=h_loc, n=n, bank=bank,
                               vp=vpe):
                        e.wait_ge(s_pe.h, vp)
                        e.wait_ge(s_act.h, R["evac_n"][(g, h_loc, n)])
                        h = HPG * g + h_loc
                        sl = slice(h * T + n * 256, h * T + n * 256 + 256)
                        e.scalar_tensor_tensor(
                            attn[:, sl], bank[:, 0:256], _BD[h] ** n,
                            attn[:, sl], ALU.mult,
                            ALU.add).then_inc(s_dve.h, 1)
                    dve_cls.append(v_cadd)
                    prev_add[i] = s_dve.inc()
                return pe_cls, dve_cls

            def mk_stats(g):
                R["corr_dve"][g] = s_dve.v
                act_cls = []
                pe_cls = []
                for h_loc in range(HPG):
                    def a_sqr(e, g=g, h_loc=h_loc, vd=R["corr_dve"][g]):
                        e.wait_ge(s_dve.h, vd)
                        if h_loc == 1:
                            e.wait_ge(s_pe.h, R["stats_pe"][(g, 0)])
                        h = HPG * g + h_loc
                        e.activation(sqscr[:], attn[:, h * T:(h + 1) * T],
                                     AF.Square).then_inc(s_act.h, 1)
                    act_cls.append(a_sqr)
                    R["sq_act"][(g, h_loc)] = s_act.inc()

                    def t_st(e, g=g, h_loc=h_loc):
                        e.wait_ge(s_act.h, R["sq_act"][(g, h_loc)])
                        if h_loc == 0 and g >= 1:
                            e.wait_ge(s_dve.h, R["stats_dve"][g - 1])
                        mm = None
                        for t2 in (0, 1):
                            mm = e.matmul(psc[t2][0:1, 0:512], onesb[:],
                                          sqscr[:, t2 * 512:(t2 + 1) * 512],
                                          start=(h_loc == 0),
                                          stop=(h_loc == 1))
                        mm.then_inc(s_pe.h, 1)
                    pe_cls.append(t_st)
                    R["stats_pe"][(g, h_loc)] = s_pe.inc()

                def v_st(e, g=g):
                    e.wait_ge(s_pe.h, R["stats_pe"][(g, 1)])
                    mm = None
                    for t2 in (0, 1):
                        mm = e.tensor_tensor(
                            rmsacc[0:1, t2 * 512:(t2 + 1) * 512],
                            rmsacc[0:1, t2 * 512:(t2 + 1) * 512],
                            psc[t2][0:1, 0:512], ALU.add)
                    mm.then_inc(s_dve.h, 1)
                R["stats_dve"][g] = s_dve.inc()
                return act_cls, pe_cls, v_st

            def emit_g2(g):
                vg2 = [s_pool.inc() for _ in range(2)]

                def vg_g2(e, g=g, vg2=vg2):
                    e.wait_ge(s_act.h, R["sq_act"][(g, 1)])
                    aa = None
                    for h_loc in range(HPG):
                        h = HPG * g + h_loc
                        aa = e.tensor_tensor(
                            attn[:, h * T:(h + 1) * T],
                            attn[:, h * T:(h + 1) * T],
                            gsig[g % 2][:, h_loc * T:(h_loc + 1) * T],
                            ALU.mult)
                    aa.then_inc(s_pool.h, 1)
                    e.wait_ge(s_pool.h, vg2[0])
                    aa = None
                    for h_loc in range(HPG):
                        h = HPG * g + h_loc
                        aa = e.tensor_scalar(
                            attn[:, h * T:(h + 1) * T],
                            attn[:, h * T:(h + 1) * T],
                            nwsb[:, h:h + 1], None, ALU.mult)
                    aa.then_inc(s_pool.h, 1)
                p2.em("vector", vg_g2)
                R["g2"][g] = vg2[1]

            R["stats_pe"] = {}
            R["kv_chain_dve"] = {}
            emit_wloads(0)
            emit_tabs(0)
            for s in range(NG + 2):
                ga = s - 1
                have_mm = s < NG
                have_at = 0 <= ga < NG

                halves = []
                if have_mm:
                    pass

                half_idx = [0]

                def put_half():
                    if not have_mm or half_idx[0] >= 16:
                        return
                    j, hf = divmod(half_idx[0], 2)
                    p2.em("tensor", mk_qkv_half(s, j, hf))
                    if hf == 1:
                        a_ev, tr_cl = mk_qkv_evac(s, j)
                        p2.em("scalar", a_ev)
                        if tr_cl is not None:
                            p2.em("sync", tr_cl)
                    half_idx[0] += 1

                if have_at:
                    for n in range(NB):
                        t_kvd, vks = mk_kvd(ga, n)
                        p2.em("tensor", t_kvd)
                        for v in vks:
                            p2.em("vector", v)
                        put_half()
                    R["kv_chain_dve"][ga] = s_dve.v
                    emit_collective(ga)
                    for n in range(NB):
                        t_sc, dscls = mk_scores(ga, n)
                        p2.em("tensor", t_sc)
                        for v in dscls:
                            p2.em("vector", v)
                        put_half()
                        t_acc, acls = mk_acc(ga, n)
                        p2.em("tensor", t_acc)
                        for a in acls:
                            p2.em("scalar", a)
                        put_half()
                    if s < NG:
                        emit_pool_qdq(s)
                    p2.em("vector", mk_mask(ga))
                    put_half()
                    put_half()
                    put_half()
                    prev_add = {}
                    for pi in range(NB):
                        pe_cls, dve_cls = mk_corr_pair(ga, pi, prev_add)
                        for t in pe_cls:
                            p2.em("tensor", t)
                        for v in dve_cls:
                            p2.em("vector", v)
                        if pi < 2:
                            put_half()
                    act_cls, pe_cls, v_st = mk_stats(ga)
                    for a, t in zip(act_cls, pe_cls):
                        p2.em("scalar", a)
                        p2.em("tensor", t)
                    p2.em("vector", v_st)
                while have_mm and half_idx[0] < 16:
                    put_half()
                if have_mm:
                    R["qkv_evac"][s] = s_act.v
                    R["tr"][s] = s_tr[s % 2].v
                if have_at:
                    R["attn_pe"][ga] = s_pe.v
                if s < NG and not have_at:
                    emit_pool_qdq(s)
                if s + 1 < NG:
                    emit_wloads(s + 1)
                if s + 1 < NG:
                    emit_tabs(s + 1)
                if have_at:
                    emit_g2(ga)


            p2.flush(nc)

            c2.close()
            sb4 = lambda n, s, d: cA.enter_context(nc.sbuf_tensor(n, s, d))
            rbv = sb4("rbv", [128, T], F32)
            vec4 = sb4("p4_vec", [1, T], F32)
            x1sl = [sb4(f"x1sl{i}", [128, T], F32) for i in range(2)]
            tsl = [sb4(f"tsl{i}", [128, T], F32) for i in range(2)]
            x2sl = [sb4(f"x2sl{i}", [128, T], F32) for i in range(2)]
            wb4 = [sb4(f"wb4{i}", [128, NCH * 128], BF16) for i in range(2)]

            p4 = Prog()
            R4 = dict(w_pe={}, dve1={}, pool2={}, x2w={})

            v_a1 = s_act.inc()

            def a_rstd(e, v1=v_a1):
                e.activation(vec4[0:1, 0:T], rmsacc[:], AF.Ln,
                             bias=epssb[0:1, :],
                             scale=1.0 / 4096.0).then_inc(s_act.h, 1)
                e.wait_ge(s_act.h, v1)
                e.activation(rmsacc[0:1, 0:T], vec4[0:1, 0:T], AF.Exp,
                             scale=-0.5).then_inc(s_act.h, 1)
            p4.em("scalar", a_rstd)
            v_d_rstd = s_act.inc()

            def t_rb(e, vd=v_d_rstd):
                e.wait_ge(s_act.h, vd)
                e.matmul(pmm[0][:, 0:512], ones1x128[:],
                         rmsacc[0:1, 0:512], start=True, stop=True)
                e.matmul(pmm[1][:, 0:512], ones1x128[:],
                         rmsacc[0:1, 512:T],
                         start=True, stop=True).then_inc(s_pe.h, 1)
            p4.em("tensor", t_rb)
            v_p_rb = s_pe.inc()

            def v_rb(e, vp=v_p_rb):
                e.wait_ge(s_pe.h, vp)
                e.tensor_copy(rbv[:, 0:512], pmm[0][:, 0:512])
                e.tensor_copy(rbv[:, 512:T],
                              pmm[1][:, 0:512]).then_inc(s_dve.h, 1)
            p4.em("vector", v_rb)
            v_rb_done = s_dve.inc()
            if debug:
                def gp_d2(e, v=v_rb_done):
                    e.wait_ge(s_dve.h, v)
                    e.dma_start(out=dbg["attn"].ap(),
                                in_=attn[:]).then_inc(s_dbg.h, 16)
                    e.dma_start(out=dbg["rstd"].ap(),
                                in_=rmsacc[0:1, 0:T]).then_inc(s_dbg.h, 16)
                p4.em("gpsimd", gp_d2)
                s_dbg.inc(32)
            for oc in range(32):
                par = oc % 2

                def gp_w4(e, oc=oc, par=par):
                    if oc >= 2:
                        e.wait_ge(s_pe.h, R4["w_pe"][oc - 2])
                    e.dma_start(out=wb4[par][:],
                                in_=woutd.ap()[oc]).then_inc(s_w4[par].h, 16)
                p4.em("gpsimd", gp_w4)
                s_w4[par].inc(16)
                vw4 = s_w4[par].v

                def sy_x1(e, oc=oc, par=par):
                    if oc == 0:
                        e.wait_ge(s_x1w.h, 16 * NT)
                    if oc >= 2:
                        e.wait_ge(s_dve.h, R4["pool2"][oc - 2])
                    for n in range(NT):
                        e.dma_start(
                            out=x1sl[par][:, n * TT:(n + 1) * TT],
                            in_=x1d.ap()[n][:, oc * TT:(oc + 1) * TT]
                        ).then_inc(s_x1r[par].h, 16)
                p4.em("sync", sy_x1)
                s_x1r[par].inc(16 * NT)
                vx1 = s_x1r[par].v

                b0, b1 = pmm[2 * par], pmm[2 * par + 1]

                def t_out(e, oc=oc, par=par, b0=b0, b1=b1, vw=vw4,
                          vrb=v_rb_done):
                    e.wait_ge(s_w4[par].h, vw)
                    if oc < 2:
                        e.wait_ge(s_dve.h, vrb)
                    if oc >= 2:
                        e.wait_ge(s_dve.h, R4["dve1"][oc - 2])
                    mm = None
                    for c in range(NCH):
                        for t2, bank in ((0, b0), (1, b1)):
                            mm = e.matmul(
                                bank[:, 0:512],
                                wb4[par][:, c * 128:(c + 1) * 128],
                                attn[:, c * T + t2 * 512:c * T + t2 * 512 + 512],
                                start=(c == 0), stop=(c == NCH - 1))
                    mm.then_inc(s_pe.h, 1)
                p4.em("tensor", t_out)
                R4["w_pe"][oc] = s_pe.inc()

                v12 = [s_dve.inc() for _ in range(2)]

                def v_o1(e, oc=oc, par=par, b0=b0, b1=b1,
                         vp=R4["w_pe"][oc], vx=vx1, v12=v12):
                    e.wait_ge(s_pe.h, vp)
                    mm = None
                    for t2, bank in ((0, b0), (1, b1)):
                        mm = e.tensor_tensor(
                            tsl[par][:, t2 * 512:(t2 + 1) * 512],
                            bank[:, 0:512], rbv[:, t2 * 512:(t2 + 1) * 512],
                            ALU.mult)
                    mm.then_inc(s_dve.h, 1)
                    e.wait_ge(s_dve.h, v12[0])
                    e.wait_ge(s_x1r[par].h, vx)
                    if oc >= 2:
                        e.wait_ge(s_x2w[par].h, R4["x2w"][oc - 2])
                    e.tensor_tensor(x2sl[par][:], tsl[par][:], x1sl[par][:],
                                    ALU.add).then_inc(s_dve.h, 1)
                p4.em("vector", v_o1)
                R4["dve1"][oc] = v12[0]
                R4["pool2"][oc] = v12[1]

                def sy_x2(e, oc=oc, par=par, vp=v12[1]):
                    e.wait_ge(s_dve.h, vp)
                    e.dma_start(out=x2d.ap()[oc],
                                in_=x2sl[par][:]).then_inc(s_x2w[par].h, 16)
                p4.em("sync", sy_x2)
                s_x2w[par].inc(16)
                R4["x2w"][oc] = s_x2w[par].v



            def sy_fin4(e):
                e.wait_ge(s_x2w[0].h, 16 * 16)
                e.wait_ge(s_x2w[1].h, 16 * 16)
            p4.em("sync", sy_fin4)

            p4.flush(nc)

        with contextlib.ExitStack() as c4:
            p4b = Prog()
            tb4 = mk_ffn_bufs(c4, "p4b_")
            tb4["sx"], tb4["sxb"] = s_x2r, s_x2rb
            emit_ab_loads(p4b, tb4,
                          [(s_x2w[0], 16 * 16), (s_x2w[1], 16 * 16)])

            def x2src_f(t0):
                return [(tb4["xf"][:, c * TT:(c + 1) * TT],
                         x2d.ap()[c][:, t0 * TT:(t0 + 1) * TT])
                        for c in range(NCH)]

            def x2src_b(t0):
                return [(tb4["bufB"][t0 % 2][:, c * TT:(c + 1) * TT],
                         x2d.ap()[c][:, t0 * TT:(t0 + 1) * TT])
                        for c in range(NCH)]

            emit_ffn_ln(
                p4b, tb4,
                x_f32_src=x2src_f,
                x_b16_src=x2src_b,
                res_dst=lambda t0: yd.ap()[t0],
                res_sem=s_y,
                pre_waits=[(s_x2w[0], 16 * 16), (s_x2w[1], 16 * 16)] +
                          ([(s_dbg, s_dbg.v)] if debug else []),
                second_ln_dest=None)

            def gp_fin(e, v=s_y.v, vd=s_dbg.v):
                e.wait_ge(s_y.h, v)
                if vd:
                    e.wait_ge(s_dbg.h, vd)
            p4b.em("gpsimd", gp_fin)
            p4b.flush(nc)

    return nc


def _prep_shared(L0_attn_qkv_w, L0_attn_out_w, L0_attn_gate_w, L0_attn_norm_w,
                 L0_ffn_A, L0_ffn_B, L0_ffn_Wr, L0_ffn_br):
    f32 = np.float32

    def wtiles(w, no):
        return np.ascontiguousarray(
            w.reshape(no, 128, NCH, 128).transpose(0, 3, 2, 1).reshape(
                no, 128, NCH * 128)).astype(f32)

    d = {}
    d["wqkv"] = wtiles(np.asarray(L0_attn_qkv_w), 96)
    d["wgate"] = wtiles(np.asarray(L0_attn_gate_w), 32)
    d["wout"] = wtiles(np.asarray(L0_attn_out_w), 32)
    A_flat = np.asarray(L0_ffn_A).reshape(128, 4096)
    d["Aw"] = np.ascontiguousarray(
        A_flat.T.reshape(NCH, 128, 128).transpose(1, 0, 2).reshape(
            128, NCH * 128)).astype(f32)
    B_flat = np.asarray(L0_ffn_B).transpose(0, 2, 1).reshape(128, 4096)
    d["Bw"] = np.ascontiguousarray(B_flat).astype(f32)
    Wr = np.asarray(L0_ffn_Wr)
    d["Wrw"] = np.ascontiguousarray(
        Wr.T.reshape(NCH, 128, 8).transpose(1, 0, 2).reshape(
            128, NCH * 8)).astype(f32)
    d["nww"] = np.ascontiguousarray(
        np.asarray(L0_attn_norm_w).reshape(NCH, 128).T).astype(f32)
    d["brw"] = np.asarray(L0_ffn_br).reshape(8, 1).astype(f32)
    d["kdw"] = _KDH
    d["qdbw"] = np.ascontiguousarray(
        _QDB.reshape(NG, HPG, 128, 256).transpose(0, 2, 1, 3).reshape(
            NG, 128, HPG * 256))
    d["ddTw"] = _DDT
    d["Ew"] = _E8.astype(np.float32)
    d["onesw"] = np.ones((128, 1), dtype=BF)
    d["negonesw"] = np.full((128, 1), -1.0, dtype=BF)
    d["ones1x128w"] = np.ones((1, 128), dtype=f32)
    d["ones1x8w"] = np.ones((1, 8), dtype=f32)
    d["ones8w"] = np.ones((8, 1), dtype=f32)
    d["epsw"] = np.full((1, 1), EPS, dtype=f32)
    return d


def _prep_x(x, core):
    b, g = core // 2, core % 2
    xs = np.asarray(x)[b, g * T:(g + 1) * T, :]
    t = xs.T.reshape(NCH, 128, NT, TT).transpose(2, 1, 0, 3)
    return np.ascontiguousarray(t.reshape(NT, 128, NCH * TT)).astype(np.float32)


def _unprep_y(y):
    t = np.asarray(y).reshape(NT, 128, NCH, TT).transpose(2, 1, 0, 3)
    return t.reshape(4096, T).T


def kernel(L0_attn_hidden_states, L0_attn_qkv_w, L0_attn_out_w,
           L0_attn_gate_w, L0_attn_norm_w, L0_ffn_A, L0_ffn_B, L0_ffn_Wr,
           L0_ffn_br):
    shared = _prep_shared(L0_attn_qkv_w, L0_attn_out_w, L0_attn_gate_w,
                          L0_attn_norm_w, L0_ffn_A, L0_ffn_B, L0_ffn_Wr,
                          L0_ffn_br)
    in_maps = []
    for c in range(8):
        m = dict(shared)
        m["x0"] = _prep_x(L0_attn_hidden_states, c)
        m["m0w"] = np.full((128, 1), float(c % 2), np.float32)
        in_maps.append(m)
    nc = build_nc()
    res = run_bass_kernel_spmd(nc, in_maps, core_ids=list(range(8)))
    out = np.zeros((4, 2048, 4096), dtype=np.float32)
    for c in range(8):
        b, g = c // 2, c % 2
        out[b, g * T:(g + 1) * T, :] = _unprep_y(res.results[c]["y"])
    return out


if __name__ == "__main__":
    nc = build_nc()
    n_inst = sum(len(b.instructions) for b in nc.m.functions[0].blocks)
    print("built ok, instructions:", n_inst)


# version 17
# speedup vs baseline: 1.1474x; 1.1474x over previous
import contextlib
import numpy as np
import ml_dtypes

import concourse.bass as bass
import concourse.mybir as mybir
from concourse.bass_utils import run_bass_kernel_spmd

F32 = mybir.dt.float32
BF16 = mybir.dt.bfloat16
AF = mybir.ActivationFunctionType
ALU = mybir.AluOpType
BF = ml_dtypes.bfloat16

NCH = 32
T = 1024
NT = 4
TT = 256
NG = 16
HPG = 2
NB = 4
BS = 256
EPS = 1e-5

ENGS = ("sync", "gpsimd", "tensor", "scalar", "vector")


def _tables():
    base = 2.0 ** (-8.0 / 32.0)
    factor = 1.0 - 1.0 / (31.0 + 1e-5) + 1e-5
    slope = base ** (np.arange(32, dtype=np.float64) + 1.0) * factor
    r = np.arange(BS, dtype=np.float64) + 1.0
    qd = np.exp(-slope[:, None] * r[None, :])
    kd = np.exp(-slope[:, None] * (BS - r)[None, :])
    diff = r[:, None] - r[None, :]
    dd = np.where(diff >= 0,
                  np.exp(-slope[:, None, None] * diff[None]), 0.0)
    bd = np.exp(-slope * BS)

    qdb = np.broadcast_to(qd.astype(np.float32).astype(BF)[:, None, :],
                          (32, 128, 256)).copy()
    ddT = np.zeros((32, 128, 512), dtype=BF)
    for mt in range(2):
        ddT[:, :, mt * 256:(mt + 1) * 256] = np.transpose(
            dd[:, :, mt * 128:(mt + 1) * 128], (0, 2, 1)
        ).astype(np.float32).astype(BF)
    kdh = np.zeros((128, 64), dtype=np.float32)
    for h in range(32):
        for mt in range(2):
            kdh[:, h * 2 + mt] = kd[h, mt * 128:(mt + 1) * 128]
    return qdb, ddT, kdh, [float(x) for x in bd]


_QDB, _DDT, _KDH, _BD = _tables()
_E8 = np.zeros((8, 128), dtype=BF)
for _h in range(8):
    _E8[_h, 16 * _h:16 * _h + 16] = BF(8.0)



class Sem:
    def __init__(self, nc, ctx, name):
        self.h = ctx.enter_context(nc.semaphore(name))
        self.v = 0

    def inc(self, n=1):
        self.v += n
        return self.v


class Prog:
    def __init__(self):
        self.q = {e: [] for e in ENGS}

    def em(self, eng, fn):
        self.q[eng].append(fn)

    def flush(self, nc):
        with nc.Block() as block:
            @block.sync
            def _(e):
                for f in self.q["sync"]:
                    f(e)

            @block.gpsimd
            def _(e):
                for f in self.q["gpsimd"]:
                    f(e)

            @block.tensor
            def _(e):
                for f in self.q["tensor"]:
                    f(e)

            @block.scalar
            def _(e):
                for f in self.q["scalar"]:
                    f(e)

            @block.vector
            def _(e):
                for f in self.q["vector"]:
                    f(e)


def build_nc(replica_groups=None, debug=False):
    if replica_groups is None:
        replica_groups = [[0, 1], [2, 3], [4, 5], [6, 7]]
    nc = bass.Bass("TRN2", target_bir_lowering=False, debug=False)

    dram = lambda n, s, d, k: nc.dram_tensor(n, s, d, kind=k)
    x0d = dram("x0", [NT, 128, NCH * TT], F32, "ExternalInput")
    wqkvd = dram("wqkv", [96, 128, NCH * 128], F32, "ExternalInput")
    wgated = dram("wgate", [32, 128, NCH * 128], F32, "ExternalInput")
    woutd = dram("wout", [32, 128, NCH * 128], F32, "ExternalInput")
    Ad = dram("Aw", [128, NCH * 128], F32, "ExternalInput")
    Bd = dram("Bw", [128, NCH * 128], F32, "ExternalInput")
    Wrd = dram("Wrw", [128, NCH * 8], F32, "ExternalInput")
    nwd = dram("nww", [128, 32], F32, "ExternalInput")
    brd = dram("brw", [8, 1], F32, "ExternalInput")
    m0d = dram("m0w", [128, 1], F32, "ExternalInput")
    kdd = dram("kdw", [128, 64], F32, "ExternalInput")
    qdbd = dram("qdbw", [NG, 128, HPG * 256], BF16, "ExternalInput")
    ddTd = dram("ddTw", [32, 128, 512], BF16, "ExternalInput")
    Ed = dram("Ew", [8, 128], F32, "ExternalInput")
    onesd = dram("onesw", [128, 1], BF16, "ExternalInput")
    negonesd = dram("negonesw", [128, 1], BF16, "ExternalInput")
    ones1x128d = dram("ones1x128w", [1, 128], F32, "ExternalInput")
    ones1x8d = dram("ones1x8w", [1, 8], F32, "ExternalInput")
    ones8d = dram("ones8w", [8, 1], F32, "ExternalInput")
    epsd = dram("epsw", [1, 1], F32, "ExternalInput")

    yd = dram("y", [NT, 128, NCH * TT], F32, "ExternalOutput")

    x1d = dram("x1spill", [NT, 128, NCH * TT], F32, "Internal")
    x2d = dram("x2spill", [32, 128, T], F32, "Internal")
    kvsndd = dram("kvsnd", [NG, 128, HPG * 128], F32, "Internal")
    kvrcvd = dram("kvrcv", [NG, 2 * 128, HPG * 128], F32, "Internal")

    dbg = {}
    if debug:
        dbg["ln2"] = dram("dbg_ln2", [128, NCH * T], BF16, "ExternalOutput")
        dbg["attn"] = dram("dbg_attn", [128, NCH * T], BF16, "ExternalOutput")
        dbg["rstd"] = dram("dbg_rstd", [1, T], F32, "ExternalOutput")

    ctx = contextlib.ExitStack()
    with ctx:
        sb = lambda n, s, d: ctx.enter_context(nc.sbuf_tensor(n, s, d))
        sem = lambda n: Sem(nc, ctx, n)

        ln2 = sb("ln2", [128, NCH * T], BF16)
        rmsacc = sb("rmsacc", [1, T], F32)
        onesb = sb("onesb", [128, 1], BF16)
        ones1x128 = sb("ones1x128", [1, 128], F32)
        ones1x8 = sb("ones1x8", [1, 8], F32)
        ones8 = sb("ones8", [8, 1], F32)
        negonesb = sb("negonesb", [128, 1], BF16)
        nwsb = sb("nwsb", [128, 32], F32)
        brsb = sb("brsb", [8, 1], F32)
        m0sb = sb("m0sb", [128, 1], F32)
        kdsb = sb("kdsb", [128, 64], F32)
        Wrsb = sb("Wrsb", [128, NCH * 8], BF16)
        epssb = sb("epssb", [1, 1], F32)

        s_pe = sem("s_pe")
        s_act = sem("s_act")
        s_dve = sem("s_dve")
        s_pool = sem("s_pool")
        s_c0 = sem("s_c0")
        s_c0g = sem("s_c0g")
        s_dbg = sem("s_dbg")
        s_x0 = sem("s_x0")
        s_x0b = [sem("s_x0b0"), sem("s_x0b1")]
        s_x1w = sem("s_x1w")
        s_x1r = [sem("s_x1r0"), sem("s_x1r1")]
        s_x2w = [sem("s_x2w0"), sem("s_x2w1")]
        s_x2r = sem("s_x2r")
        s_x2rb = [sem("s_x2rb0"), sem("s_x2rb1")]
        s_w = [sem("s_w0"), sem("s_w1")]
        s_w4 = [sem("s_w40"), sem("s_w41")]
        s_tr = [sem("s_tr0"), sem("s_tr1")]
        s_tab = [sem("s_tab0"), sem("s_tab1")]
        s_kvs = sem("s_kvs")
        s_cc = sem("s_cc")
        s_kvr = [sem("s_kvr0"), sem("s_kvr1")]
        s_ab = sem("s_ab")
        s_y = sem("s_y")

        def emit_ffn_ln(p, tb, x_f32_src, x_b16_src, res_dst, res_sem,
                        pre_waits, second_ln_dest):
            xf, bufB, bufC, xres = tb["xf"], tb["bufB"], tb["bufC"], tb["xres"]
            Asb, Bsb = tb["Asb"], tb["Bsb"]
            b_sum, b_sq, b_bc = tb["b_sum"], tb["b_sq"], tb["b_bc"]
            b_rt, b_low, b_sex = tb["b_rt"], tb["b_low"], tb["b_sex"]
            pf = tb["pf"]
            exps, lowp = tb["exps"], tb["lowp"]
            vec = [tb["vec"][0:1, 4 * TT * i:4 * TT * (i + 1)] for i in range(2)]
            rb1 = tb["rb1"]
            rmb, rbb = tb["rmb"], tb["rbb"]
            lowf, rtf = tb["lowf"], tb["rtf"]
            negA, negWr = tb["negA"], tb["negWr"]
            Eb = tb["Eb"]
            sx, sxb = tb["sx"], tb["sxb"]
            HC = NCH // 2 * TT

            reg = dict(stats_free=0, rt_free=0, low_free=0,
                       sex_free=0, lowp_free=0, xf_free=0, xres_free=0,
                       bufC_free=0, bc_free=None,
                       bufB_free=[None, None], vec_free=[0, 0],
                       rmb_free=[0, 0])
            fwd = {}

            def wait_tag(e, tag):
                if not tag:
                    return
                eng, v = tag
                if v:
                    e.wait_ge({"pe": s_pe.h, "act": s_act.h,
                               "dve": s_dve.h}[eng], v)

            def t_colsum(e):
                mm = None
                for c in range(NCH):
                    mm = e.matmul(b_low[0:128, 0:1], Asb[:, c * 128:(c + 1) * 128],
                                  negonesb[:], start=(c == 0),
                                  stop=(c == NCH - 1))
                for c in range(NCH):
                    mm = e.matmul(b_rt[0:8, 0:1], Wrsb[:, c * 8:(c + 1) * 8],
                                  negonesb[:], start=(c == 0),
                                  stop=(c == NCH - 1))
                mm.then_inc(s_pe.h, 1)
            p.em("tensor", t_colsum)
            v_cs = s_pe.inc()

            def v_colsum(e, vp=v_cs):
                e.wait_ge(s_pe.h, vp)
                e.tensor_copy(negA[:], b_low[0:128, 0:1])
                e.tensor_copy(negWr[:], b_rt[0:8, 0:1]).then_inc(s_dve.h, 1)
            p.em("vector", v_colsum)
            reg["rt_free"] = reg["low_free"] = s_dve.inc()

            def stats_chain(src_b16, vecb, act_wait, want_mr, load_wait,
                            vec_gate):
                r_bufC = reg["bufC_free"]
                NQ = NCH // 2
                QC = NQ * TT
                sq_v = [s_act.inc() for _ in range(2)]
                st_v = [s_pe.inc() for _ in range(2)]

                def a_sqs(e, s=src_b16, rc=r_bufC, aw=act_wait, lw=load_wait,
                          sq_v=sq_v, st_v=st_v):
                    wait_tag(e, aw)
                    if lw:
                        e.wait_ge(lw[0].h, lw[1])
                    if rc:
                        e.wait_ge(s_pe.h, rc)
                    for q in range(2):
                        if q > 0:
                            e.wait_ge(s_pe.h, st_v[q - 1])
                        e.activation(bufC[:], s[:, q * QC:(q + 1) * QC],
                                     AF.Square).then_inc(s_act.h, 1)
                p.em("scalar", a_sqs)

                r_stats = reg["stats_free"]

                def t_sts(e, s=src_b16, rs=r_stats, lw=load_wait,
                          sq_v=sq_v, st_v=st_v):
                    if lw:
                        e.wait_ge(lw[0].h, lw[1])
                    if rs:
                        e.wait_ge(s_dve.h, rs)
                    for c in range(NCH):
                        e.matmul(b_sum[0:1, 0:TT], onesb[:],
                                 s[:, c * TT:(c + 1) * TT],
                                 start=(c == 0), stop=(c == NCH - 1))
                    for q in range(2):
                        e.wait_ge(s_act.h, sq_v[q])
                        mm = None
                        for c in range(NQ):
                            mm = e.matmul(b_sq[0:1, 0:TT], onesb[:],
                                          bufC[:, c * TT:(c + 1) * TT],
                                          start=(q == 0 and c == 0),
                                          stop=(q == 1 and c == NQ - 1))
                        mm.then_inc(s_pe.h, 1)
                p.em("tensor", t_sts)
                v_pe_stats = st_v[1]
                reg["bufC_free"] = v_pe_stats

                vv = [s_dve.inc() for _ in range(3)]

                def v_vec(e, vp=v_pe_stats, vv=vv, vecb=vecb, vg=vec_gate):
                    e.wait_ge(s_pe.h, vp)
                    wait_tag(e, vg)
                    e.tensor_scalar(vecb[0:1, 0:TT], b_sum[0:1, 0:TT],
                                    1.0 / 4096.0, None, ALU.mult)
                    e.tensor_scalar(vecb[0:1, TT:2 * TT], b_sq[0:1, 0:TT],
                                    1.0 / 4096.0, None,
                                    ALU.mult).then_inc(s_dve.h, 1)
                    e.wait_ge(s_dve.h, vv[0])
                    e.tensor_tensor(vecb[0:1, 2 * TT:3 * TT], vecb[0:1, 0:TT],
                                    vecb[0:1, 0:TT],
                                    ALU.mult).then_inc(s_dve.h, 1)
                    e.wait_ge(s_dve.h, vv[1])
                    e.scalar_tensor_tensor(
                        vecb[0:1, 2 * TT:3 * TT], vecb[0:1, 2 * TT:3 * TT],
                        -1.0, vecb[0:1, TT:2 * TT],
                        ALU.mult, ALU.add).then_inc(s_dve.h, 1)
                p.em("vector", v_vec)
                v_dve_var = vv[2]
                reg["stats_free"] = v_dve_var

                v_lnx = s_act.inc()

                def a_rstd(e, vd=v_dve_var, vl=v_lnx, vecb=vecb):
                    e.wait_ge(s_dve.h, vd)
                    e.activation(vecb[0:1, 2 * TT:3 * TT],
                                 vecb[0:1, 2 * TT:3 * TT], AF.Ln,
                                 bias=epssb[0:1, :]).then_inc(s_act.h, 1)
                    e.wait_ge(s_act.h, vl)
                    e.activation(vecb[0:1, 3 * TT:4 * TT],
                                 vecb[0:1, 2 * TT:3 * TT], AF.Exp,
                                 scale=-0.5).then_inc(s_act.h, 1)
                p.em("scalar", a_rstd)
                v_rstd = s_act.inc()

                if want_mr:
                    def v_mr(e, va=v_rstd, vecb=vecb):
                        e.wait_ge(s_act.h, va)
                        e.tensor_tensor(vecb[0:1, TT:2 * TT], vecb[0:1, 0:TT],
                                        vecb[0:1, 3 * TT:4 * TT],
                                        ALU.mult).then_inc(s_dve.h, 1)
                    p.em("vector", v_mr)
                    v_pre_bc = ("dve", s_dve.inc())
                    fr_off = TT
                else:
                    v_pre_bc = ("act", v_rstd)
                    fr_off = 0

                r_bc = reg["bc_free"]

                def t_bc(e, vd=v_pre_bc, rb=r_bc, fo=fr_off, vecb=vecb):
                    wait_tag(e, vd)
                    wait_tag(e, rb)
                    e.matmul(b_bc[:, 0:TT], ones1x128[:],
                             vecb[0:1, fo:fo + TT],
                             start=True, stop=False)
                    e.matmul(b_bc[:, TT:2 * TT], ones1x128[:],
                             vecb[0:1, 3 * TT:4 * TT],
                             start=False, stop=True).then_inc(s_pe.h, 1)
                p.em("tensor", t_bc)
                v_pe_bc = s_pe.inc()
                return v_pe_bc, v_pe_stats, v_rstd

            def emit_SL(t0):
                par = t0 % 2

                def sy_load(e, t0=t0,
                            pw=(pre_waits if t0 == 0 else [])):
                    for (sw, vv_) in pw:
                        e.wait_ge(sw.h, vv_)
                    rx = fwd.get(("xf", t0 - 1), 0)
                    if rx:
                        e.wait_ge(s_dve.h, rx)
                    srcs = x_f32_src(t0)
                    if not isinstance(srcs, list):
                        srcs = [(xf[:], srcs)]
                    for (dst, ap_) in srcs:
                        e.dma_start(out=dst, in_=ap_).then_inc(sx.h, 16)
                p.em("sync", sy_load)
                srcs0 = x_f32_src(t0)
                nld = len(srcs0) if isinstance(srcs0, list) else 1
                vxf = sx.inc(16 * nld)

                sxbp = sxb[par]

                def gp_load(e, t0=t0, sxbp=sxbp,
                            pw=(pre_waits if t0 == 0 else [])):
                    for (sw, vv_) in pw:
                        e.wait_ge(sw.h, vv_)
                    rb = fwd.get(("bufB", t0 - 2))
                    if rb:
                        wait_tag(e, rb)
                    srcs = x_b16_src(t0)
                    if not isinstance(srcs, list):
                        srcs = [(bufB[par][:], srcs)]
                    for (dst, ap_) in srcs:
                        e.dma_start(out=dst, in_=ap_).then_inc(sxbp.h, 16)
                p.em("gpsimd", gp_load)
                srcs0 = x_b16_src(t0)
                nld = len(srcs0) if isinstance(srcs0, list) else 1
                vxb = sxbp.inc(16 * nld)
                return dict(par=par, vxb=vxb, vxf=vxf, sxbp=sxbp)

            def emit_SS(t0, L):
                par = L["par"]
                vf_gate = reg["vec_free"][par]
                v_pe_bc, v_stats, _ = stats_chain(
                    bufB[par], vec[par], None, True,
                    (L["sxbp"], L["vxb"]), vf_gate)
                r_rmb = reg["rmb_free"][par]

                def a_bc(e, vp=v_pe_bc, rb=r_rmb, par=par):
                    e.wait_ge(s_pe.h, vp)
                    if rb:
                        e.wait_ge(s_dve.h, rb)
                    e.copy(rmb[par][:], b_bc[:, 0:TT])
                    e.copy(rbb[par][:], b_bc[:, TT:2 * TT]).then_inc(s_act.h, 1)
                p.em("scalar", a_bc)
                v_a_bc = s_act.inc()
                reg["bc_free"] = ("act", v_a_bc)
                L["v_a_bc"] = v_a_bc

            def emit_Frl(t0, L):
                par = L["par"]
                r_rt, r_low = reg["rt_free"], reg["low_free"]

                def t_rl(e, rr=r_rt, rl=r_low, vb=L["vxb"], sp=L["sxbp"],
                         par=par):
                    e.wait_ge(sp.h, vb)
                    if rr:
                        e.wait_ge(s_dve.h, rr)
                    for c in range(NCH):
                        e.matmul(b_rt[0:8, 0:TT], Wrsb[:, c * 8:(c + 1) * 8],
                                 bufB[par][:, c * TT:(c + 1) * TT],
                                 start=(c == 0), stop=(c == NCH - 1))
                    if rl:
                        e.wait_ge(s_dve.h, rl)
                    mm = None
                    for c in range(NCH):
                        mm = e.matmul(b_low[:, 0:TT],
                                      Asb[:, c * 128:(c + 1) * 128],
                                      bufB[par][:, c * TT:(c + 1) * TT],
                                      start=(c == 0), stop=(c == NCH - 1))
                    mm.then_inc(s_pe.h, 1)
                p.em("tensor", t_rl)
                L["v_pe_rl"] = s_pe.inc()

            def emit_Fmid(t0, L):
                par = L["par"]
                vfx = [s_dve.inc() for _ in range(4)]

                def v_fix(e, vp=L["v_pe_rl"], va=L["v_a_bc"], vfx=vfx,
                          par=par):
                    e.wait_ge(s_pe.h, vp)
                    e.wait_ge(s_act.h, va)
                    e.tensor_tensor(rtf[0:8, :], b_rt[0:8, 0:TT],
                                    rbb[par][0:8, :],
                                    ALU.mult).then_inc(s_dve.h, 1)
                    e.wait_ge(s_dve.h, vfx[0])
                    e.scalar_tensor_tensor(
                        rtf[0:8, :], rmb[par][0:8, :], negWr[0:8, :],
                        rtf[0:8, :], ALU.mult, ALU.add).then_inc(s_dve.h, 1)
                    e.tensor_tensor(lowf[:], b_low[:, 0:TT], rbb[par][:],
                                    ALU.mult).then_inc(s_dve.h, 1)
                    e.wait_ge(s_dve.h, vfx[2])
                    e.scalar_tensor_tensor(
                        lowf[:], rmb[par][:], negA[:], lowf[:],
                        ALU.mult, ALU.add).then_inc(s_dve.h, 1)
                p.em("vector", v_fix)
                v_fix_rt, v_fix_low = vfx[1], vfx[3]
                reg["rt_free"] = v_fix_rt
                reg["rmb_free"][par] = v_fix_low

                def a_exp(e, vd=v_fix_rt):
                    e.wait_ge(s_dve.h, vd)
                    e.activation(exps[0:8, :], rtf[0:8, :], AF.Exp,
                                 bias=brsb[0:8, :]).then_inc(s_act.h, 1)
                p.em("scalar", a_exp)
                v_act_exp = s_act.inc()

                r_sex = reg["sex_free"]

                def t_se(e, va=v_act_exp, rs=r_sex):
                    e.wait_ge(s_act.h, va)
                    if rs:
                        e.wait_ge(s_dve.h, rs)
                    e.matmul(b_sex[:, 0:TT], Eb[:], exps[0:8, :],
                             start=True, stop=True)
                    e.matmul(b_rt[0:1, TT:2 * TT], ones8[:], exps[0:8, :],
                             start=True, stop=True).then_inc(s_pe.h, 1)
                p.em("tensor", t_se)
                v_pe_se = s_pe.inc()

                def v_rinv(e, vp=v_pe_se, par=par):
                    e.wait_ge(s_pe.h, vp)
                    e.reciprocal(vec[par][0:1, 3 * TT:4 * TT],
                                 b_rt[0:1, TT:2 * TT]).then_inc(s_dve.h, 1)
                p.em("vector", v_rinv)
                v_dve_rinv = s_dve.inc()

                def t_rb8(e, vd=v_dve_rinv, par=par):
                    e.wait_ge(s_dve.h, vd)
                    e.matmul(b_rt[:, 0:TT], ones1x128[:],
                             vec[par][0:1, 3 * TT:4 * TT],
                             start=True, stop=True).then_inc(s_pe.h, 1)
                p.em("tensor", t_rb8)
                v_pe_rb8 = s_pe.inc()
                reg["vec_free"][par] = ("pe", v_pe_rb8)

                r_lowp = reg["lowp_free"]
                vm = [s_dve.inc() for _ in range(2)]

                def v_lowp(e, vfl=v_fix_low, vse=v_pe_se, vrb=v_pe_rb8,
                           rl=r_lowp, vm=vm):
                    e.wait_ge(s_dve.h, vfl)
                    e.wait_ge(s_pe.h, vse)
                    e.tensor_tensor(lowf[:], lowf[:], b_sex[:, 0:TT],
                                    ALU.mult).then_inc(s_dve.h, 1)
                    e.wait_ge(s_dve.h, vm[0])
                    e.wait_ge(s_pe.h, vrb)
                    if rl:
                        e.wait_ge(s_pe.h, rl)
                    e.tensor_tensor(lowp[:], lowf[:], b_rt[:, 0:TT],
                                    ALU.mult).then_inc(s_dve.h, 1)
                p.em("vector", v_lowp)
                lowp_ready = vm[1]
                reg["sex_free"] = vm[0]
                reg["low_free"] = vm[0]
                reg["rt_free"] = lowp_ready

                v_fp = {}
                last_f_pe = None
                r_xres = reg["xres_free"]
                for ocp in range(16):
                    bank = pf[ocp % 2]

                    def t_f(e, ocp=ocp, bank=bank, vl=lowp_ready, fp=v_fp):
                        if ocp == 0:
                            e.wait_ge(s_dve.h, vl)
                        if ocp >= 2:
                            e.wait_ge(s_dve.h, fp[ocp - 2])
                        e.matmul(bank[:, 0:TT],
                                 Bsb[:, (2 * ocp) * 128:(2 * ocp + 1) * 128],
                                 lowp[:], start=True, stop=False)
                        e.matmul(bank[:, TT:2 * TT],
                                 Bsb[:, (2 * ocp + 1) * 128:(2 * ocp + 2) * 128],
                                 lowp[:], start=False,
                                 stop=True).then_inc(s_pe.h, 1)
                    p.em("tensor", t_f)
                    v_pe_f = s_pe.inc()
                    last_f_pe = v_pe_f

                    def v_res(e, ocp=ocp, bank=bank, vp=v_pe_f, t0=t0,
                              vxf=L["vxf"], rx=r_xres):
                        e.wait_ge(s_pe.h, vp)
                        if ocp == 0:
                            e.wait_ge(sx.h, vxf)
                            if rx:
                                e.wait_ge(s_dve.h, rx)
                            if t0 > 0:
                                e.wait_ge(res_sem.h, 16 * t0)
                        e.tensor_tensor(xres[:, ocp * 512:(ocp + 1) * 512],
                                        bank[:, 0:2 * TT],
                                        xf[:, ocp * 512:(ocp + 1) * 512],
                                        ALU.add).then_inc(s_dve.h, 1)
                    p.em("vector", v_res)
                    v_fp[ocp] = s_dve.inc()
                res_done = v_fp[15]
                reg["lowp_free"] = last_f_pe
                fwd[("xf", t0)] = res_done
                L["res_done"] = res_done
                L["last_f_pe"] = last_f_pe

                def sy_spill(e, t0=t0, vd=res_done):
                    e.wait_ge(s_dve.h, vd)
                    e.dma_start(out=res_dst(t0),
                                in_=xres[:]).then_inc(res_sem.h, 16)
                p.em("sync", sy_spill)
                res_sem.inc(16)

                if second_ln_dest is None:
                    reg["xres_free"] = res_done
                    fwd[("bufB", t0)] = ("pe", L["v_pe_rl"])

            def emit_Fln2(t0, L):
                par = L["par"]

                def a_x1b(e, vd=L["res_done"], vpe=L["last_f_pe"], par=par):
                    e.wait_ge(s_dve.h, vd)
                    e.wait_ge(s_pe.h, vpe)
                    e.copy(bufB[par][:], xres[:]).then_inc(s_act.h, 1)
                p.em("scalar", a_x1b)
                v_x1b = s_act.inc()

                v_pe_bc2, v_stats2, _ = stats_chain(
                    bufB[par], vec[par], ("act", v_x1b), False,
                    (s_act, v_x1b), reg["vec_free"][par])
                fwd[("bufB", t0)] = ("pe", v_stats2)

                def a_rb(e, vp=v_pe_bc2):
                    e.wait_ge(s_pe.h, vp)
                    e.copy(rb1[:], b_bc[:, TT:2 * TT]).then_inc(s_act.h, 1)
                p.em("scalar", a_rb)
                v_act_rb = s_act.inc()

                vn = [s_dve.inc() for _ in range(2)]

                def v_norm(e, va=v_act_rb, t0=t0, vp=v_stats2, vn=vn,
                           par=par):
                    e.wait_ge(s_act.h, va)
                    e.wait_ge(s_pe.h, vp)
                    tt_ = None
                    for c in range(NCH):
                        tt_ = e.tensor_tensor(
                            second_ln_dest(c, t0),
                            xres[:, c * TT:(c + 1) * TT],
                            b_bc[:, 0:TT], ALU.subtract)
                    tt_.then_inc(s_dve.h, 1)
                    e.wait_ge(s_dve.h, vn[0])
                    tt_ = None
                    for c in range(NCH):
                        tt_ = e.tensor_tensor(
                            second_ln_dest(c, t0),
                            second_ln_dest(c, t0),
                            rb1[:], ALU.mult)
                    tt_.then_inc(s_dve.h, 1)
                p.em("vector", v_norm)
                ln2_ready = vn[1]
                reg["bc_free"] = ("dve", ln2_ready)
                reg["xres_free"] = ln2_ready
                reg["vec_free"][par] = ("pe", v_pe_bc2)

            L = {}
            L[0] = emit_SL(0)
            emit_SS(0, L[0])
            L[1] = emit_SL(1)
            emit_Frl(0, L[0])
            emit_SS(1, L[1])
            emit_Fmid(0, L[0])
            for t0 in range(1, NT):
                if t0 + 1 < NT:
                    L[t0 + 1] = emit_SL(t0 + 1)
                emit_Frl(t0, L[t0])
                if second_ln_dest is not None:
                    emit_Fln2(t0 - 1, L[t0 - 1])
                if t0 + 1 < NT:
                    emit_SS(t0 + 1, L[t0 + 1])
                emit_Fmid(t0, L[t0])
            if second_ln_dest is not None:
                emit_Fln2(NT - 1, L[NT - 1])

        def mk_ffn_bufs(cx, pref):
            sbp = lambda n, s, d: cx.enter_context(
                nc.sbuf_tensor(pref + n, s, d))
            psp = lambda n: cx.enter_context(
                nc.psum_tensor(pref + n, [128, 512], F32))
            return dict(
                xf=sbp("xf", [128, NCH * TT], F32),
                bufB=[sbp(f"bufB{i}", [128, NCH * TT], BF16) for i in range(2)],
                bufC=sbp("bufC", [128, NCH * TT // 2], BF16),
                xres=sbp("xres", [128, NCH * TT], F32),
                Asb=sbp("A", [128, NCH * 128], BF16),
                Bsb=sbp("B", [128, NCH * 128], BF16),
                exps=sbp("exps", [8, TT], F32),
                lowp=sbp("lowp", [128, TT], BF16),
                vec=sbp("vec", [1, 8 * TT], F32),
                rb1=sbp("rb1", [128, TT], BF16),
                rmb=[sbp(f"rmb{i}", [128, TT], F32) for i in range(2)],
                rbb=[sbp(f"rbb{i}", [128, TT], F32) for i in range(2)],
                lowf=sbp("lowf", [128, TT], F32),
                rtf=sbp("rtf", [8, TT], F32),
                negA=sbp("negA", [128, 1], F32),
                negWr=sbp("negWr", [8, 1], F32),
                Eb=sbp("Eb", [8, 128], F32),
                b_sum=psp("bsum"), b_sq=psp("bsq"), b_bc=psp("bbc"),
                b_rt=psp("brt"), b_low=psp("blow"), b_sex=psp("bsex"),
                pf=[psp("pf0"), psp("pf1")],
            )

        def emit_ab_loads(p, tb, pre_waits=()):
            def gp_ab(e, pw=tuple(pre_waits)):
                for (sw, vv_) in pw:
                    e.wait_ge(sw.h, vv_)
                e.dma_start(out=tb["Asb"][:], in_=Ad.ap()).then_inc(s_ab.h, 16)
                e.dma_start(out=tb["Bsb"][:], in_=Bd.ap()).then_inc(s_ab.h, 16)
                e.dma_start(out=tb["Eb"][:], in_=Ed.ap()).then_inc(s_ab.h, 16)
            p.em("gpsimd", gp_ab)
            s_ab.inc(48)
            vab = s_ab.v

            def t_wab(e, v=vab):
                e.wait_ge(s_ab.h, v)
            p.em("tensor", t_wab)

        with contextlib.ExitStack() as c1:
            p1 = Prog()

            def sy_c0(e):
                e.dma_start(out=onesb[:], in_=onesd.ap()).then_inc(s_c0.h, 16)
                e.dma_start(out=negonesb[:], in_=negonesd.ap()).then_inc(s_c0.h, 16)
                e.dma_start(out=ones1x128[:], in_=ones1x128d.ap()).then_inc(s_c0.h, 16)
                e.dma_start(out=ones1x8[:], in_=ones1x8d.ap()).then_inc(s_c0.h, 16)
                e.dma_start(out=ones8[:], in_=ones8d.ap()).then_inc(s_c0.h, 16)
                e.dma_start(out=nwsb[:], in_=nwd.ap()).then_inc(s_c0.h, 16)
                e.dma_start(out=brsb[:], in_=brd.ap()).then_inc(s_c0.h, 16)
                e.dma_start(out=m0sb[:], in_=m0d.ap()).then_inc(s_c0.h, 16)
                e.dma_start(out=kdsb[:], in_=kdd.ap()).then_inc(s_c0.h, 16)
                e.dma_start(out=epssb[:], in_=epsd.ap()).then_inc(s_c0.h, 16)
            p1.em("sync", sy_c0)
            s_c0.inc(16 * 10)

            def gp_c1(e):
                e.dma_start(out=Wrsb[:], in_=Wrd.ap()).then_inc(s_c0g.h, 16)
            p1.em("gpsimd", gp_c1)
            s_c0g.inc(16)
            CONST = s_c0.v
            CONSTG = s_c0g.v

            tb1 = mk_ffn_bufs(c1, "p1_")
            tb1["sx"], tb1["sxb"] = s_x0, s_x0b
            emit_ab_loads(p1, tb1)

            def t_wc(e, v=CONST, vg=CONSTG):
                e.wait_ge(s_c0.h, v)
                e.wait_ge(s_c0g.h, vg)
            p1.em("tensor", t_wc)

            def a_wc(e, v=CONST):
                e.wait_ge(s_c0.h, v)
            p1.em("scalar", a_wc)

            def v_wc(e, v=CONST):
                e.wait_ge(s_c0.h, v)
            p1.em("vector", v_wc)

            emit_ffn_ln(
                p1, tb1,
                x_f32_src=lambda t0: x0d.ap()[t0],
                x_b16_src=lambda t0: x0d.ap()[t0],
                res_dst=lambda t0: x1d.ap()[t0],
                res_sem=s_x1w, pre_waits=[],
                second_ln_dest=lambda c, t0: ln2[:, c * T + t0 * TT:
                                                 c * T + t0 * TT + TT])

            def sy_fin1(e, v=s_x1w.v):
                e.wait_ge(s_x1w.h, v)
            p1.em("sync", sy_fin1)

            if debug:
                def gp_d1(e, v=s_x1w.v, vd=s_dve.v):
                    e.wait_ge(s_x1w.h, v)
                    e.wait_ge(s_dve.h, vd)
                    e.dma_start(out=dbg["ln2"].ap(), in_=ln2[:]).then_inc(s_dbg.h, 16)
                p1.em("gpsimd", gp_d1)
                s_dbg.inc(16)

            p1.flush(nc)

        with contextlib.ExitStack() as cA:
            attn = cA.enter_context(nc.sbuf_tensor("attn", [128, NCH * T], BF16))
            psA = lambda n: cA.enter_context(nc.psum_tensor(n, [128, 512], F32))
            pmm = [psA(f"p2mm{i}") for i in range(4)]
            psc = [psA("p2sc0"), psA("p2sc1")]
            pcs = psA("p2cs")
            pkv = psA("p2kv")
            c2 = cA.enter_context(contextlib.ExitStack())
            sb2 = lambda n, s, d: c2.enter_context(nc.sbuf_tensor(n, s, d))
            qfm = [sb2(f"qfm{i}", [128, HPG * T], BF16) for i in range(2)]
            kfm = [sb2(f"kfm{i}", [128, HPG * T], BF16) for i in range(2)]
            ktm = [sb2(f"ktm{i}", [128, HPG * T], BF16) for i in range(2)]
            vtm = [sb2(f"vtm{i}", [128, HPG * T], BF16) for i in range(2)]
            vscr0 = sb2("vscr0", [128, HPG * T], BF16)
            vscr = [vscr0, vscr0]
            qdq = [sb2(f"qdq{i}", [128, HPG * T], BF16) for i in range(2)]
            gsig = [sb2(f"gsig{i}", [128, HPG * T], BF16) for i in range(2)]
            wbuf = [sb2(f"wbuf{i}", [128, 16 * 128], BF16) for i in range(2)]
            ddb = [sb2(f"ddb{i}", [128, 512], BF16) for i in range(2)]
            qdbb = [sb2(f"qdbb{i}", [128, HPG * 256], BF16) for i in range(2)]
            kvf = [sb2(f"kvf{i}", [128, HPG * 128], F32) for i in range(2)]
            kvb = [sb2(f"kvb{i}", [128, HPG * 3 * 128], BF16) for i in range(2)]
            kvrxb = [sb2(f"kvrxb{i}", [128, HPG * 128], BF16) for i in range(2)]
            dssb = [sb2(f"dssb{i}", [128, 512], BF16) for i in range(2)]
            sqscr = sb2("sqscr", [128, T], BF16)


            p2 = Prog()

            def v_init(e):
                e.memset(rmsacc[:], 0.0)
            p2.em("vector", v_init)

            R = dict(w_pe={}, wload=0, bp_evac={0: 0, 1: 0},
                     qkv_evac={}, tr={}, tab={}, attn_pe={}, attn_dve={},
                     evac_n={}, sc_pe={}, acc_pe={}, kvd_pe={}, ds={},
                     kvupd={}, pool_qdq={}, cc={}, kvs={}, kvr={},
                     mask={}, corr_pe={}, corr_dve={}, stats_dve={},
                     sq_act={}, g2={}, evac_j={},
                     tr_kv4={}, pool_kdk={}, kv_chain_dve={})

            JMAP = [(0, 1), (0, 2), (1, 1), (1, 2), (0, 0), (1, 0),
                    (0, 3), (1, 3)]

            def wsrc(g, j):
                h_loc, kind = JMAP[j]
                if kind == 3:
                    return wgated.ap()[2 * g + h_loc]
                h = HPG * g + h_loc
                return wqkvd.ap()[3 * h + kind]

            def emit_wloads(g):
                for j in range(8):
                    for half in range(2):
                        idx = R["wload"]
                        par = idx % 2

                        def gp_w(e, g=g, j=j, half=half, par=par, idx=idx):
                            if idx >= 2:
                                e.wait_ge(s_pe.h, R["w_pe"][idx - 2])
                            e.dma_start(
                                out=wbuf[par][:],
                                in_=wsrc(g, j)[:, half * 2048:(half + 1) * 2048]
                            ).then_inc(s_w[par].h, 16)
                        p2.em("gpsimd", gp_w)
                        s_w[par].inc(16)
                        R["wload"] += 1

            def emit_tabs(g):
                par = g % 2

                def sy_tab(e, g=g, par=par):
                    if g >= 1:
                        e.wait_ge(s_dve.h, R["ds"][(g - 1, 1, NB - 1)])
                    e.dma_start(
                        out=ddb[0][:],
                        in_=ddTd.ap()[HPG * g]).then_inc(s_tab[par].h, 16)
                    e.dma_start(
                        out=ddb[1][:],
                        in_=ddTd.ap()[HPG * g + 1]).then_inc(s_tab[par].h, 16)
                    e.dma_start(
                        out=qdbb[par][:],
                        in_=qdbd.ap()[g]).then_inc(s_tab[par].h, 16)
                p2.em("sync", sy_tab)
                s_tab[par].inc(48)
                R["tab"][g] = s_tab[par].v

            def mk_qkv_half(g, j, half):
                gp = g % 2
                widx0 = 16 * g
                bp = j % 2
                b0, b1 = pmm[2 * bp], pmm[2 * bp + 1]
                r_evac = R["bp_evac"][bp] if half == 0 else 0
                idx = widx0 + 2 * j + half
                wv = 16 * (idx // 2 + 1)

                def t_mmh(e, g=g, j=j, half=half, b0=b0, b1=b1, re_=r_evac,
                          wv=wv, idx=idx):
                    par = idx % 2
                    e.wait_ge(s_w[par].h, wv)
                    if re_:
                        e.wait_ge(s_act.h, re_)
                    mm = None
                    for ci in range(16):
                        c = half * 16 + ci
                        for t2, bank in ((0, b0), (1, b1)):
                            mm = e.matmul(
                                bank[:, 0:512],
                                wbuf[par][:, ci * 128:(ci + 1) * 128],
                                ln2[:, c * T + t2 * 512:
                                    c * T + t2 * 512 + 512],
                                start=(c == 0), stop=(c == NCH - 1))
                    mm.then_inc(s_pe.h, 1)
                R["w_pe"][idx] = s_pe.inc()
                return t_mmh

            def mk_qkv_evac(g, j):
                gp = g % 2
                bp = j % 2
                b0, b1 = pmm[2 * bp], pmm[2 * bp + 1]
                v_pe_j = R["w_pe"][16 * g + 2 * j + 1]
                h_loc, kind = JMAP[j]

                def a_ev(e, g=g, j=j, gp=gp, b0=b0, b1=b1, vp=v_pe_j,
                         h_loc=h_loc, kind=kind):
                    e.wait_ge(s_pe.h, vp)
                    if kind == 0:
                        dst = qfm[gp]
                        if g >= 2:
                            e.wait_ge(s_pe.h, R["attn_pe"][g - 2])
                            e.wait_ge(s_pool.h, R["pool_qdq"][g - 2])
                    elif kind == 1:
                        dst = kfm[gp]
                        if g >= 2:
                            e.wait_ge(s_tr[gp].h, R["tr"][g - 2])
                    elif kind == 2:
                        dst = vscr[gp]
                        if g >= 1:
                            e.wait_ge(s_tr[(g - 1) % 2].h, R["tr"][g - 1])
                    else:
                        dst = gsig[g % 2]
                        if g >= 2:
                            e.wait_ge(s_pool.h, R["g2"][g - 2])
                    fn = AF.Sigmoid if kind == 3 else AF.Silu
                    aa = None
                    for t2 in (0, 1):
                        aa = e.activation(
                            dst[:, h_loc * T + t2 * 512:
                                h_loc * T + t2 * 512 + 512],
                            (b0 if t2 == 0 else b1)[:, 0:512], fn)
                    aa.then_inc(s_act.h, 1)
                v_evac = s_act.inc()
                R["bp_evac"][bp] = v_evac
                R["evac_j"][(g, j)] = v_evac

                tr_cl = None
                if kind in (1, 2):
                    srct = kfm[gp] if kind == 1 else vscr[gp]
                    dstt = ktm[gp] if kind == 1 else vtm[gp]

                    def sy_tr(e, gp=gp, h_loc=h_loc, srct=srct, dstt=dstt,
                              ve=v_evac, g=g):
                        e.wait_ge(s_act.h, ve)
                        if g >= 2:
                            e.wait_ge(s_pe.h, R["attn_pe"][g - 2])
                        e.dma_start_transpose(
                            dstt[:, h_loc * T:(h_loc + 1) * T].rearrange(
                                "p (j t) -> p j t", j=8),
                            srct[:, h_loc * T:(h_loc + 1) * T]
                        ).then_inc(s_tr[gp].h, 16)
                    s_tr[gp].inc(16)
                    tr_cl = sy_tr
                    if j == 3:
                        R["tr_kv4"][g] = s_tr[gp].v
                return a_ev, tr_cl

            def emit_kdk(g):
                gp = g % 2

                def vk(e, g=g, gp=gp):
                    e.wait_ge(s_tab[gp].h, R["tab"][g])
                    e.wait_ge(s_tr[gp].h, R["tr_kv4"][g])
                    for h_loc in range(HPG):
                        h = HPG * g + h_loc
                        kv_ = ktm[gp][:, h_loc * T:(h_loc + 1) * T].rearrange(
                            "p (n m t) -> p n m t", n=NB, m=2)
                        kdv = kdsb[:, 2 * h:2 * h + 2][
                            :, None, :, None].broadcast_to([128, NB, 2, 128])
                        e.tensor_tensor(kv_, kv_, kdv,
                                        ALU.mult).then_inc(s_pool.h, 1)
                p2.em("vector", vk)
                s_pool.inc()
                R["pool_kdk"][g] = s_pool.inc()

            def emit_qdq(g):
                gp = g % 2

                def vq(e, g=g, gp=gp):
                    e.wait_ge(s_tab[gp].h, R["tab"][g])
                    if g >= 2:
                        e.wait_ge(s_pe.h, R["corr_pe"].get(g - 2, 0))
                    for h_loc in range(HPG):
                        e.wait_ge(s_act.h, R["evac_j"][(g, 4 + h_loc)])
                        qv = qfm[gp][:, h_loc * T:(h_loc + 1) * T].rearrange(
                            "p (n t) -> p n t", n=NB)
                        qdv = qdbb[gp][:, h_loc * 256:(h_loc + 1) * 256][
                            :, None, :].broadcast_to([128, NB, 256])
                        ov = qdq[gp][:, h_loc * T:(h_loc + 1) * T].rearrange(
                            "p (n t) -> p n t", n=NB)
                        e.tensor_tensor(ov, qv, qdv,
                                        ALU.mult).then_inc(s_pool.h, 1)
                p2.em("vector", vq)
                s_pool.inc()
                R["pool_qdq"][g] = s_pool.inc()

            def mk_kvd(g, n):
                gp = g % 2

                def t_kvd(e, g=g, gp=gp, n=n):
                    if n == 0:
                        e.wait_ge(s_pool.h, R["pool_kdk"][g])
                        e.wait_ge(s_tr[gp].h, R["tr_kv4"][g])
                        if g >= 1:
                            e.wait_ge(s_dve.h, R["kvupd"][(g - 1, 1, NB - 1)])
                    else:
                        e.wait_ge(s_dve.h, R["kvupd"][(g, 1, n - 1)])
                    mm = None
                    for h_loc in range(HPG):
                        mtile = h_loc * 8 + 2 * n
                        for mt in range(2):
                            mm = e.matmul(
                                pkv[:, h_loc * 128:(h_loc + 1) * 128],
                                ktm[gp][:, (mtile + mt) * 128:
                                        (mtile + mt + 1) * 128],
                                vtm[gp][:, (mtile + mt) * 128:
                                        (mtile + mt + 1) * 128],
                                start=(h_loc == 0 and mt == 0),
                                stop=(h_loc == 1 and mt == 1))
                    mm.then_inc(s_pe.h, 1)
                R["kvd_pe"][(g, n)] = s_pe.inc()

                vks = []
                for h_loc in range(HPG):
                    if n < NB - 1:
                        vkv = [s_dve.inc() for _ in range(2)]
                    else:
                        vkv = [s_dve.inc()]

                    def v_kv(e, g=g, gp=gp, n=n, h_loc=h_loc, vkv=vkv):
                        e.wait_ge(s_pe.h, R["kvd_pe"][(g, n)])
                        sl = slice(h_loc * 128, (h_loc + 1) * 128)
                        if n == 0:
                            if g >= 2:
                                e.wait_ge(s_kvs.h, R["kvs"][g - 2])
                            op = e.tensor_copy(kvf[gp][:, sl], pkv[:, sl])
                        else:
                            op = e.scalar_tensor_tensor(
                                kvf[gp][:, sl], kvf[gp][:, sl],
                                _BD[HPG * g + h_loc], pkv[:, sl],
                                ALU.mult, ALU.add)
                        op.then_inc(s_dve.h, 1)
                        if n < NB - 1:
                            e.wait_ge(s_dve.h, vkv[0])
                            if g >= 2:
                                e.wait_ge(s_pe.h, R["attn_pe"][g - 2])
                            ks = slice((h_loc * 3 + n) * 128,
                                       (h_loc * 3 + n + 1) * 128)
                            e.tensor_copy(kvb[gp][:, ks],
                                          kvf[gp][:, sl]).then_inc(s_dve.h, 1)
                    vks.append(v_kv)
                    R["kvupd"][(g, h_loc, n)] = vkv[-1]
                return t_kvd, vks

            def mk_scores(g, n):
                gp = g % 2

                def t_sc(e, g=g, gp=gp, n=n):
                    for h_loc in range(HPG):
                        bank = psc[h_loc]
                        if n == 0 and h_loc == 0:
                            e.wait_ge(s_act.h, R["evac_j"][(g, 5)])
                            if g >= 1:
                                e.wait_ge(s_dve.h, R["stats_dve"][g - 1])
                        if n > 0:
                            e.wait_ge(s_act.h,
                                      R["evac_n"][(g, h_loc, n - 1)])
                        base = h_loc * T + n * 256
                        e.matmul(bank[:, 0:256],
                                 kfm[gp][:, base:base + 128],
                                 qfm[gp][:, base:base + 256],
                                 start=True, stop=False)
                        e.matmul(bank[:, 256:512],
                                 kfm[gp][:, base + 128:base + 256],
                                 qfm[gp][:, base:base + 256],
                                 start=False,
                                 stop=True).then_inc(s_pe.h, 1)
                for h_loc in range(HPG):
                    R["sc_pe"][(g, h_loc, n)] = s_pe.inc()

                dscls = []
                for h_loc in range(HPG):
                    def v_ds(e, g=g, gp=gp, n=n, h_loc=h_loc):
                        bank = psc[h_loc]
                        e.wait_ge(s_pe.h, R["sc_pe"][(g, h_loc, n)])
                        if n > 0:
                            e.wait_ge(s_pe.h,
                                      R["acc_pe"][(g, h_loc, n - 1)])
                        elif g >= 1:
                            e.wait_ge(s_pe.h,
                                      R["acc_pe"][(g - 1, h_loc, NB - 1)])
                        e.tensor_tensor(
                            dssb[h_loc][:], bank[:, 0:512],
                            ddb[h_loc][:],
                            ALU.mult).then_inc(s_dve.h, 1)
                    dscls.append(v_ds)
                    R["ds"][(g, h_loc, n)] = s_dve.inc()
                return t_sc, dscls

            def mk_acc(g, n):
                gp = g % 2

                def t_acc(e, g=g, gp=gp, n=n):
                    if n == 0:
                        e.wait_ge(s_tr[gp].h, R["tr_kv4"][g])
                    if n == 1:
                        e.wait_ge(s_pool.h, R["pool_qdq"][g])
                    for h_loc in range(HPG):
                        bank = psc[h_loc]
                        e.wait_ge(s_dve.h, R["ds"][(g, h_loc, n)])
                        base = h_loc * T + n * 256
                        mtile = h_loc * 8 + 2 * n
                        e.matmul(bank[:, 0:256],
                                 vtm[gp][:, mtile * 128:(mtile + 1) * 128],
                                 dssb[h_loc][:, 0:256],
                                 start=True, stop=False)
                        mm = e.matmul(
                            bank[:, 0:256],
                            vtm[gp][:, (mtile + 1) * 128:(mtile + 2) * 128],
                            dssb[h_loc][:, 256:512],
                            start=False, stop=(n == 0))
                        if n > 0:
                            e.wait_ge(s_dve.h, R["kvupd"][(g, h_loc, n - 1)])
                            ks = slice((h_loc * 3 + n - 1) * 128,
                                       (h_loc * 3 + n) * 128)
                            mm = e.matmul(
                                bank[:, 0:256],
                                kvb[gp][:, ks],
                                qdq[gp][:, base:base + 256],
                                start=False, stop=True)
                        mm.then_inc(s_pe.h, 1)
                for h_loc in range(HPG):
                    R["acc_pe"][(g, h_loc, n)] = s_pe.inc()

                acls = []
                for h_loc in range(HPG):
                    def a_acc(e, g=g, n=n, h_loc=h_loc):
                        bank = psc[h_loc]
                        e.wait_ge(s_pe.h, R["acc_pe"][(g, h_loc, n)])
                        h = HPG * g + h_loc
                        e.copy(attn[:, h * T + n * 256:h * T + n * 256 + 256],
                               bank[:, 0:256]).then_inc(s_act.h, 1)
                    acls.append(a_acc)
                    R["evac_n"][(g, h_loc, n)] = s_act.inc()
                return t_acc, acls

            def emit_collective(g):
                gp = g % 2
                v_kvs = s_kvs.inc(16)
                R["kvs"][g] = v_kvs
                v_cc = s_cc.inc()
                v_kvr = s_kvr[gp].inc(16)
                R["kvr"][g] = v_kvr

                def gp_cc(e, g=g, gp=gp, v_kvs=v_kvs, v_cc=v_cc):
                    e.wait_ge(s_dve.h, R["kv_chain_dve"][g])
                    e.dma_start(
                        out=kvsndd.ap()[g],
                        in_=kvf[gp][:]).then_inc(s_kvs.h, 16)
                    e.wait_ge(s_kvs.h, v_kvs)
                    e.collective_compute(
                        "AllGather", ALU.bypass,
                        replica_groups=replica_groups,
                        ins=[kvsndd.ap()[g].opt()],
                        outs=[kvrcvd.ap()[g].opt()],
                    ).then_inc(s_cc.h, 1)
                    e.wait_ge(s_cc.h, v_cc)
                    if g >= 2:
                        e.wait_ge(s_dve.h, R["mask"][g - 2])
                        e.wait_ge(s_pe.h, R["corr_pe"][g - 2])
                    e.dma_start(
                        out=kvrxb[gp][:],
                        in_=kvrcvd.ap()[g][0:128]
                    ).then_inc(s_kvr[gp].h, 16)
                p2.em("gpsimd", gp_cc)

            def mk_mask(g):
                gp = g % 2

                def v_mask(e, g=g, gp=gp):
                    e.wait_ge(s_kvr[gp].h, R["kvr"][g])
                    e.tensor_scalar(kvrxb[gp][:], kvrxb[gp][:], m0sb[:],
                                    None, ALU.mult).then_inc(s_dve.h, 1)
                R["mask"][g] = s_dve.inc()
                return v_mask

            def mk_corr_pair(g, pi, prev_add):
                gp = g % 2
                corr_banks = [pcs, psc[0]]
                pe_cls = []
                dve_cls = []
                for i in (2 * pi, 2 * pi + 1):
                    h_loc, n = i // NB, i % NB
                    bank = corr_banks[i % 2]

                    def t_corr(e, g=g, gp=gp, i=i, h_loc=h_loc, n=n,
                               bank=bank):
                        if i == 0:
                            e.wait_ge(s_dve.h, R["mask"][g])
                        if i == 1:
                            e.wait_ge(s_act.h, R["evac_n"][(g, 0, NB - 1)])
                        if i >= 2:
                            e.wait_ge(s_dve.h, prev_add[i - 2])
                        elif i < 2 and g >= 1:
                            e.wait_ge(s_dve.h, R["stats_dve"].get(g - 1, 0))
                        e.matmul(bank[:, 0:256],
                                 kvrxb[gp][:, h_loc * 128:(h_loc + 1) * 128],
                                 qdq[gp][:, h_loc * T + n * 256:
                                         h_loc * T + n * 256 + 256],
                                 start=True, stop=True).then_inc(s_pe.h, 1)
                    pe_cls.append(t_corr)
                    vpe = s_pe.inc()
                    R["corr_pe"][g] = vpe

                    def v_cadd(e, g=g, i=i, h_loc=h_loc, n=n, bank=bank,
                               vp=vpe):
                        e.wait_ge(s_pe.h, vp)
                        e.wait_ge(s_act.h, R["evac_n"][(g, h_loc, n)])
                        h = HPG * g + h_loc
                        sl = slice(h * T + n * 256, h * T + n * 256 + 256)
                        e.scalar_tensor_tensor(
                            attn[:, sl], bank[:, 0:256], _BD[h] ** n,
                            attn[:, sl], ALU.mult,
                            ALU.add).then_inc(s_dve.h, 1)
                    dve_cls.append(v_cadd)
                    prev_add[i] = s_dve.inc()
                return pe_cls, dve_cls

            def mk_stats(g):
                R["corr_dve"][g] = s_dve.v
                act_cls = []
                pe_cls = []
                for h_loc in range(HPG):
                    def a_sqr(e, g=g, h_loc=h_loc, vd=R["corr_dve"][g]):
                        e.wait_ge(s_dve.h, vd)
                        if h_loc == 1:
                            e.wait_ge(s_pe.h, R["stats_pe"][(g, 0)])
                        h = HPG * g + h_loc
                        e.activation(sqscr[:], attn[:, h * T:(h + 1) * T],
                                     AF.Square).then_inc(s_act.h, 1)
                    act_cls.append(a_sqr)
                    R["sq_act"][(g, h_loc)] = s_act.inc()

                    def t_st(e, g=g, h_loc=h_loc):
                        e.wait_ge(s_act.h, R["sq_act"][(g, h_loc)])
                        if h_loc == 0 and g >= 1:
                            e.wait_ge(s_dve.h, R["stats_dve"][g - 1])
                        mm = None
                        for t2 in (0, 1):
                            mm = e.matmul(psc[t2][0:1, 0:512], onesb[:],
                                          sqscr[:, t2 * 512:(t2 + 1) * 512],
                                          start=(h_loc == 0),
                                          stop=(h_loc == 1))
                        mm.then_inc(s_pe.h, 1)
                    pe_cls.append(t_st)
                    R["stats_pe"][(g, h_loc)] = s_pe.inc()

                def v_st(e, g=g):
                    e.wait_ge(s_pe.h, R["stats_pe"][(g, 1)])
                    mm = None
                    for t2 in (0, 1):
                        mm = e.tensor_tensor(
                            rmsacc[0:1, t2 * 512:(t2 + 1) * 512],
                            rmsacc[0:1, t2 * 512:(t2 + 1) * 512],
                            psc[t2][0:1, 0:512], ALU.add)
                    mm.then_inc(s_dve.h, 1)
                R["stats_dve"][g] = s_dve.inc()
                return act_cls, pe_cls, v_st

            def emit_g2(g):
                vg2 = [s_pool.inc() for _ in range(2)]

                def vg_g2(e, g=g, vg2=vg2):
                    e.wait_ge(s_act.h, R["sq_act"][(g, 1)])
                    aa = None
                    for h_loc in range(HPG):
                        h = HPG * g + h_loc
                        aa = e.tensor_tensor(
                            attn[:, h * T:(h + 1) * T],
                            attn[:, h * T:(h + 1) * T],
                            gsig[g % 2][:, h_loc * T:(h_loc + 1) * T],
                            ALU.mult)
                    aa.then_inc(s_pool.h, 1)
                    e.wait_ge(s_pool.h, vg2[0])
                    aa = None
                    for h_loc in range(HPG):
                        h = HPG * g + h_loc
                        aa = e.tensor_scalar(
                            attn[:, h * T:(h + 1) * T],
                            attn[:, h * T:(h + 1) * T],
                            nwsb[:, h:h + 1], None, ALU.mult)
                    aa.then_inc(s_pool.h, 1)
                p2.em("vector", vg_g2)
                R["g2"][g] = vg2[1]

            R["stats_pe"] = {}
            emit_wloads(0)
            emit_tabs(0)
            for s in range(NG + 1):
                ga = s - 1
                have_mm = s < NG
                have_at = 0 <= ga < NG
                half_idx = [0]

                def put_half():
                    if not have_mm or half_idx[0] >= 16:
                        return
                    j, hf = divmod(half_idx[0], 2)
                    p2.em("tensor", mk_qkv_half(s, j, hf))
                    if hf == 1:
                        a_ev, tr_cl = mk_qkv_evac(s, j)
                        p2.em("scalar", a_ev)
                        if tr_cl is not None:
                            p2.em("sync", tr_cl)
                    half_idx[0] += 1

                if have_at:
                    for n in range(NB):
                        t_sc, dscls = mk_scores(ga, n)
                        p2.em("tensor", t_sc)
                        for v in dscls:
                            p2.em("vector", v)
                        put_half()
                        t_acc, acls = mk_acc(ga, n)
                        p2.em("tensor", t_acc)
                        for a in acls:
                            p2.em("scalar", a)
                        put_half()
                    p2.em("vector", mk_mask(ga))
                    prev_add = {}
                    for pi in range(NB):
                        pe_cls, dve_cls = mk_corr_pair(ga, pi, prev_add)
                        for t in pe_cls:
                            p2.em("tensor", t)
                        for v in dve_cls:
                            p2.em("vector", v)
                        put_half()
                    act_cls, pe_cls, v_st = mk_stats(ga)
                    for a, t in zip(act_cls, pe_cls):
                        p2.em("scalar", a)
                        p2.em("tensor", t)
                    p2.em("vector", v_st)
                if have_mm:
                    if not have_at:
                        while half_idx[0] < 8:
                            put_half()
                    emit_kdk(s)
                    for n in range(NB):
                        t_kvd, vks = mk_kvd(s, n)
                        p2.em("tensor", t_kvd)
                        for v in vks:
                            p2.em("vector", v)
                        put_half()
                    while half_idx[0] < 16:
                        put_half()
                    R["qkv_evac"][s] = s_act.v
                    R["tr"][s] = s_tr[s % 2].v
                    R["kv_chain_dve"][s] = s_dve.v
                    emit_collective(s)
                    emit_qdq(s)
                    if s + 1 < NG:
                        emit_wloads(s + 1)
                        emit_tabs(s + 1)
                if have_at:
                    R["attn_pe"][ga] = s_pe.v
                    emit_g2(ga)

            p2.flush(nc)

            c2.close()
            sb4 = lambda n, s, d: cA.enter_context(nc.sbuf_tensor(n, s, d))
            rbv = sb4("rbv", [128, T], F32)
            vec4 = sb4("p4_vec", [1, T], F32)
            x1sl = [sb4(f"x1sl{i}", [128, T], F32) for i in range(2)]
            tsl = [sb4(f"tsl{i}", [128, T], F32) for i in range(2)]
            x2sl = [sb4(f"x2sl{i}", [128, T], F32) for i in range(2)]
            wb4 = [sb4(f"wb4{i}", [128, NCH * 128], BF16) for i in range(2)]

            p4 = Prog()
            R4 = dict(w_pe={}, dve1={}, pool2={}, x2w={})

            v_a1 = s_act.inc()

            def a_rstd(e, v1=v_a1):
                e.activation(vec4[0:1, 0:T], rmsacc[:], AF.Ln,
                             bias=epssb[0:1, :],
                             scale=1.0 / 4096.0).then_inc(s_act.h, 1)
                e.wait_ge(s_act.h, v1)
                e.activation(rmsacc[0:1, 0:T], vec4[0:1, 0:T], AF.Exp,
                             scale=-0.5).then_inc(s_act.h, 1)
            p4.em("scalar", a_rstd)
            v_d_rstd = s_act.inc()

            def t_rb(e, vd=v_d_rstd):
                e.wait_ge(s_act.h, vd)
                e.matmul(pmm[0][:, 0:512], ones1x128[:],
                         rmsacc[0:1, 0:512], start=True, stop=True)
                e.matmul(pmm[1][:, 0:512], ones1x128[:],
                         rmsacc[0:1, 512:T],
                         start=True, stop=True).then_inc(s_pe.h, 1)
            p4.em("tensor", t_rb)
            v_p_rb = s_pe.inc()

            def v_rb(e, vp=v_p_rb):
                e.wait_ge(s_pe.h, vp)
                e.tensor_copy(rbv[:, 0:512], pmm[0][:, 0:512])
                e.tensor_copy(rbv[:, 512:T],
                              pmm[1][:, 0:512]).then_inc(s_dve.h, 1)
            p4.em("vector", v_rb)
            v_rb_done = s_dve.inc()
            if debug:
                def gp_d2(e, v=v_rb_done):
                    e.wait_ge(s_dve.h, v)
                    e.dma_start(out=dbg["attn"].ap(),
                                in_=attn[:]).then_inc(s_dbg.h, 16)
                    e.dma_start(out=dbg["rstd"].ap(),
                                in_=rmsacc[0:1, 0:T]).then_inc(s_dbg.h, 16)
                p4.em("gpsimd", gp_d2)
                s_dbg.inc(32)
            for oc in range(32):
                par = oc % 2

                def gp_w4(e, oc=oc, par=par):
                    if oc >= 2:
                        e.wait_ge(s_pe.h, R4["w_pe"][oc - 2])
                    e.dma_start(out=wb4[par][:],
                                in_=woutd.ap()[oc]).then_inc(s_w4[par].h, 16)
                p4.em("gpsimd", gp_w4)
                s_w4[par].inc(16)
                vw4 = s_w4[par].v

                def sy_x1(e, oc=oc, par=par):
                    if oc == 0:
                        e.wait_ge(s_x1w.h, 16 * NT)
                    if oc >= 2:
                        e.wait_ge(s_dve.h, R4["pool2"][oc - 2])
                    for n in range(NT):
                        e.dma_start(
                            out=x1sl[par][:, n * TT:(n + 1) * TT],
                            in_=x1d.ap()[n][:, oc * TT:(oc + 1) * TT]
                        ).then_inc(s_x1r[par].h, 16)
                p4.em("sync", sy_x1)
                s_x1r[par].inc(16 * NT)
                vx1 = s_x1r[par].v

                b0, b1 = pmm[2 * par], pmm[2 * par + 1]

                def t_out(e, oc=oc, par=par, b0=b0, b1=b1, vw=vw4,
                          vrb=v_rb_done):
                    e.wait_ge(s_w4[par].h, vw)
                    if oc < 2:
                        e.wait_ge(s_dve.h, vrb)
                    if oc >= 2:
                        e.wait_ge(s_dve.h, R4["dve1"][oc - 2])
                    mm = None
                    for c in range(NCH):
                        for t2, bank in ((0, b0), (1, b1)):
                            mm = e.matmul(
                                bank[:, 0:512],
                                wb4[par][:, c * 128:(c + 1) * 128],
                                attn[:, c * T + t2 * 512:c * T + t2 * 512 + 512],
                                start=(c == 0), stop=(c == NCH - 1))
                    mm.then_inc(s_pe.h, 1)
                p4.em("tensor", t_out)
                R4["w_pe"][oc] = s_pe.inc()

                v12 = [s_dve.inc() for _ in range(2)]

                def v_o1(e, oc=oc, par=par, b0=b0, b1=b1,
                         vp=R4["w_pe"][oc], vx=vx1, v12=v12):
                    e.wait_ge(s_pe.h, vp)
                    mm = None
                    for t2, bank in ((0, b0), (1, b1)):
                        mm = e.tensor_tensor(
                            tsl[par][:, t2 * 512:(t2 + 1) * 512],
                            bank[:, 0:512], rbv[:, t2 * 512:(t2 + 1) * 512],
                            ALU.mult)
                    mm.then_inc(s_dve.h, 1)
                    e.wait_ge(s_dve.h, v12[0])
                    e.wait_ge(s_x1r[par].h, vx)
                    if oc >= 2:
                        e.wait_ge(s_x2w[par].h, R4["x2w"][oc - 2])
                    e.tensor_tensor(x2sl[par][:], tsl[par][:], x1sl[par][:],
                                    ALU.add).then_inc(s_dve.h, 1)
                p4.em("vector", v_o1)
                R4["dve1"][oc] = v12[0]
                R4["pool2"][oc] = v12[1]

                def sy_x2(e, oc=oc, par=par, vp=v12[1]):
                    e.wait_ge(s_dve.h, vp)
                    e.dma_start(out=x2d.ap()[oc],
                                in_=x2sl[par][:]).then_inc(s_x2w[par].h, 16)
                p4.em("sync", sy_x2)
                s_x2w[par].inc(16)
                R4["x2w"][oc] = s_x2w[par].v



            def sy_fin4(e):
                e.wait_ge(s_x2w[0].h, 16 * 16)
                e.wait_ge(s_x2w[1].h, 16 * 16)
            p4.em("sync", sy_fin4)

            p4.flush(nc)

        with contextlib.ExitStack() as c4:
            p4b = Prog()
            tb4 = mk_ffn_bufs(c4, "p4b_")
            tb4["sx"], tb4["sxb"] = s_x2r, s_x2rb
            emit_ab_loads(p4b, tb4,
                          [(s_x2w[0], 16 * 16), (s_x2w[1], 16 * 16)])

            def x2src_f(t0):
                return [(tb4["xf"][:, c * TT:(c + 1) * TT],
                         x2d.ap()[c][:, t0 * TT:(t0 + 1) * TT])
                        for c in range(NCH)]

            def x2src_b(t0):
                return [(tb4["bufB"][t0 % 2][:, c * TT:(c + 1) * TT],
                         x2d.ap()[c][:, t0 * TT:(t0 + 1) * TT])
                        for c in range(NCH)]

            emit_ffn_ln(
                p4b, tb4,
                x_f32_src=x2src_f,
                x_b16_src=x2src_b,
                res_dst=lambda t0: yd.ap()[t0],
                res_sem=s_y,
                pre_waits=[(s_x2w[0], 16 * 16), (s_x2w[1], 16 * 16)] +
                          ([(s_dbg, s_dbg.v)] if debug else []),
                second_ln_dest=None)

            def gp_fin(e, v=s_y.v, vd=s_dbg.v):
                e.wait_ge(s_y.h, v)
                if vd:
                    e.wait_ge(s_dbg.h, vd)
            p4b.em("gpsimd", gp_fin)
            p4b.flush(nc)

    return nc


def _prep_shared(L0_attn_qkv_w, L0_attn_out_w, L0_attn_gate_w, L0_attn_norm_w,
                 L0_ffn_A, L0_ffn_B, L0_ffn_Wr, L0_ffn_br):
    f32 = np.float32

    def wtiles(w, no):
        return np.ascontiguousarray(
            w.reshape(no, 128, NCH, 128).transpose(0, 3, 2, 1).reshape(
                no, 128, NCH * 128)).astype(f32)

    d = {}
    d["wqkv"] = wtiles(np.asarray(L0_attn_qkv_w), 96)
    d["wgate"] = wtiles(np.asarray(L0_attn_gate_w), 32)
    d["wout"] = wtiles(np.asarray(L0_attn_out_w), 32)
    A_flat = np.asarray(L0_ffn_A).reshape(128, 4096)
    d["Aw"] = np.ascontiguousarray(
        A_flat.T.reshape(NCH, 128, 128).transpose(1, 0, 2).reshape(
            128, NCH * 128)).astype(f32)
    B_flat = np.asarray(L0_ffn_B).transpose(0, 2, 1).reshape(128, 4096)
    d["Bw"] = np.ascontiguousarray(B_flat).astype(f32)
    Wr = np.asarray(L0_ffn_Wr)
    d["Wrw"] = np.ascontiguousarray(
        Wr.T.reshape(NCH, 128, 8).transpose(1, 0, 2).reshape(
            128, NCH * 8)).astype(f32)
    d["nww"] = np.ascontiguousarray(
        np.asarray(L0_attn_norm_w).reshape(NCH, 128).T).astype(f32)
    d["brw"] = np.asarray(L0_ffn_br).reshape(8, 1).astype(f32)
    d["kdw"] = _KDH
    d["qdbw"] = np.ascontiguousarray(
        _QDB.reshape(NG, HPG, 128, 256).transpose(0, 2, 1, 3).reshape(
            NG, 128, HPG * 256))
    d["ddTw"] = _DDT
    d["Ew"] = _E8.astype(np.float32)
    d["onesw"] = np.ones((128, 1), dtype=BF)
    d["negonesw"] = np.full((128, 1), -1.0, dtype=BF)
    d["ones1x128w"] = np.ones((1, 128), dtype=f32)
    d["ones1x8w"] = np.ones((1, 8), dtype=f32)
    d["ones8w"] = np.ones((8, 1), dtype=f32)
    d["epsw"] = np.full((1, 1), EPS, dtype=f32)
    return d


def _prep_x(x, core):
    b, g = core // 2, core % 2
    xs = np.asarray(x)[b, g * T:(g + 1) * T, :]
    t = xs.T.reshape(NCH, 128, NT, TT).transpose(2, 1, 0, 3)
    return np.ascontiguousarray(t.reshape(NT, 128, NCH * TT)).astype(np.float32)


def _unprep_y(y):
    t = np.asarray(y).reshape(NT, 128, NCH, TT).transpose(2, 1, 0, 3)
    return t.reshape(4096, T).T


def kernel(L0_attn_hidden_states, L0_attn_qkv_w, L0_attn_out_w,
           L0_attn_gate_w, L0_attn_norm_w, L0_ffn_A, L0_ffn_B, L0_ffn_Wr,
           L0_ffn_br):
    shared = _prep_shared(L0_attn_qkv_w, L0_attn_out_w, L0_attn_gate_w,
                          L0_attn_norm_w, L0_ffn_A, L0_ffn_B, L0_ffn_Wr,
                          L0_ffn_br)
    in_maps = []
    for c in range(8):
        m = dict(shared)
        m["x0"] = _prep_x(L0_attn_hidden_states, c)
        m["m0w"] = np.full((128, 1), float(c % 2), np.float32)
        in_maps.append(m)
    nc = build_nc()
    res = run_bass_kernel_spmd(nc, in_maps, core_ids=list(range(8)))
    out = np.zeros((4, 2048, 4096), dtype=np.float32)
    for c in range(8):
        b, g = c // 2, c % 2
        out[b, g * T:(g + 1) * T, :] = _unprep_y(res.results[c]["y"])
    return out


if __name__ == "__main__":
    nc = build_nc()
    n_inst = sum(len(b.instructions) for b in nc.m.functions[0].blocks)
    print("built ok, instructions:", n_inst)


# version 18 (final)
# speedup vs baseline: 1.2172x; 1.0609x over previous
import contextlib
import numpy as np
import ml_dtypes

import concourse.bass as bass
import concourse.mybir as mybir
from concourse.bass_utils import run_bass_kernel_spmd

F32 = mybir.dt.float32
BF16 = mybir.dt.bfloat16
AF = mybir.ActivationFunctionType
ALU = mybir.AluOpType
BF = ml_dtypes.bfloat16

NCH = 32
T = 1024
NT = 4
TT = 256
NG = 16
HPG = 2
NB = 4
BS = 256
EPS = 1e-5

ENGS = ("sync", "gpsimd", "tensor", "scalar", "vector")


def _tables():
    base = 2.0 ** (-8.0 / 32.0)
    factor = 1.0 - 1.0 / (31.0 + 1e-5) + 1e-5
    slope = base ** (np.arange(32, dtype=np.float64) + 1.0) * factor
    r = np.arange(BS, dtype=np.float64) + 1.0
    qd = np.exp(-slope[:, None] * r[None, :])
    kd = np.exp(-slope[:, None] * (BS - r)[None, :])
    diff = r[:, None] - r[None, :]
    dd = np.where(diff >= 0,
                  np.exp(-slope[:, None, None] * diff[None]), 0.0)
    bd = np.exp(-slope * BS)

    qdb = np.broadcast_to(qd.astype(np.float32).astype(BF)[:, None, :],
                          (32, 128, 256)).copy()
    ddT = np.zeros((32, 128, 512), dtype=BF)
    for mt in range(2):
        ddT[:, :, mt * 256:(mt + 1) * 256] = np.transpose(
            dd[:, :, mt * 128:(mt + 1) * 128], (0, 2, 1)
        ).astype(np.float32).astype(BF)
    kdh = np.zeros((128, 64), dtype=np.float32)
    for h in range(32):
        for mt in range(2):
            kdh[:, h * 2 + mt] = kd[h, mt * 128:(mt + 1) * 128]
    return qdb, ddT, kdh, [float(x) for x in bd]


_QDB, _DDT, _KDH, _BD = _tables()
_E8 = np.zeros((8, 128), dtype=BF)
for _h in range(8):
    _E8[_h, 16 * _h:16 * _h + 16] = BF(8.0)



class Sem:
    def __init__(self, nc, ctx, name):
        self.h = ctx.enter_context(nc.semaphore(name))
        self.v = 0

    def inc(self, n=1):
        self.v += n
        return self.v


class Prog:
    def __init__(self):
        self.q = {e: [] for e in ENGS}

    def em(self, eng, fn):
        self.q[eng].append(fn)

    def flush(self, nc):
        with nc.Block() as block:
            @block.sync
            def _(e):
                for f in self.q["sync"]:
                    f(e)

            @block.gpsimd
            def _(e):
                for f in self.q["gpsimd"]:
                    f(e)

            @block.tensor
            def _(e):
                for f in self.q["tensor"]:
                    f(e)

            @block.scalar
            def _(e):
                for f in self.q["scalar"]:
                    f(e)

            @block.vector
            def _(e):
                for f in self.q["vector"]:
                    f(e)


def build_nc(replica_groups=None, debug=False):
    if replica_groups is None:
        replica_groups = [[0, 1], [2, 3], [4, 5], [6, 7]]
    nc = bass.Bass("TRN2", target_bir_lowering=False, debug=False)

    dram = lambda n, s, d, k: nc.dram_tensor(n, s, d, kind=k)
    x0d = dram("x0", [NT, 128, NCH * TT], F32, "ExternalInput")
    wqkvd = dram("wqkv", [96, 128, NCH * 128], F32, "ExternalInput")
    wgated = dram("wgate", [32, 128, NCH * 128], F32, "ExternalInput")
    woutd = dram("wout", [32, 128, NCH * 128], F32, "ExternalInput")
    Ad = dram("Aw", [128, NCH * 128], F32, "ExternalInput")
    Bd = dram("Bw", [128, NCH * 128], F32, "ExternalInput")
    Wrd = dram("Wrw", [128, NCH * 8], F32, "ExternalInput")
    nwd = dram("nww", [128, 32], F32, "ExternalInput")
    brd = dram("brw", [8, 1], F32, "ExternalInput")
    m0d = dram("m0w", [128, 1], F32, "ExternalInput")
    kdd = dram("kdw", [128, 64], F32, "ExternalInput")
    qdbd = dram("qdbw", [NG, 128, HPG * 256], BF16, "ExternalInput")
    ddTd = dram("ddTw", [32, 128, 512], BF16, "ExternalInput")
    Ed = dram("Ew", [8, 128], F32, "ExternalInput")
    onesd = dram("onesw", [128, 1], BF16, "ExternalInput")
    negonesd = dram("negonesw", [128, 1], BF16, "ExternalInput")
    ones1x128d = dram("ones1x128w", [1, 128], F32, "ExternalInput")
    ones1x8d = dram("ones1x8w", [1, 8], F32, "ExternalInput")
    ones8d = dram("ones8w", [8, 1], F32, "ExternalInput")
    epsd = dram("epsw", [1, 1], F32, "ExternalInput")

    yd = dram("y", [NT, 128, NCH * TT], F32, "ExternalOutput")

    x1d = dram("x1spill", [NT, 128, NCH * TT], F32, "Internal")
    x2d = dram("x2spill", [32, 128, T], F32, "Internal")
    kvsndd = dram("kvsnd", [NG, 128, HPG * 128], F32, "Internal")
    kvrcvd = dram("kvrcv", [NG, 2 * 128, HPG * 128], F32, "Internal")

    dbg = {}
    if debug:
        dbg["ln2"] = dram("dbg_ln2", [128, NCH * T], BF16, "ExternalOutput")
        dbg["attn"] = dram("dbg_attn", [128, NCH * T], BF16, "ExternalOutput")
        dbg["rstd"] = dram("dbg_rstd", [1, T], F32, "ExternalOutput")

    ctx = contextlib.ExitStack()
    with ctx:
        sb = lambda n, s, d: ctx.enter_context(nc.sbuf_tensor(n, s, d))
        sem = lambda n: Sem(nc, ctx, n)

        ln2 = sb("ln2", [128, NCH * T], BF16)
        rmsacc = sb("rmsacc", [1, T], F32)
        onesb = sb("onesb", [128, 1], BF16)
        ones1x128 = sb("ones1x128", [1, 128], F32)
        ones1x8 = sb("ones1x8", [1, 8], F32)
        ones8 = sb("ones8", [8, 1], F32)
        negonesb = sb("negonesb", [128, 1], BF16)
        nwsb = sb("nwsb", [128, 32], F32)
        brsb = sb("brsb", [8, 1], F32)
        m0sb = sb("m0sb", [128, 1], F32)
        kdsb = sb("kdsb", [128, 64], F32)
        Wrsb = sb("Wrsb", [128, NCH * 8], BF16)
        epssb = sb("epssb", [1, 1], F32)

        s_pe = sem("s_pe")
        s_act = sem("s_act")
        s_dve = sem("s_dve")
        s_pool = sem("s_pool")
        s_c0 = sem("s_c0")
        s_c0g = sem("s_c0g")
        s_dbg = sem("s_dbg")
        s_x0 = sem("s_x0")
        s_x0b = [sem("s_x0b0"), sem("s_x0b1")]
        s_x1w = sem("s_x1w")
        s_x1r = [sem("s_x1r0"), sem("s_x1r1")]
        s_x2w = [sem("s_x2w0"), sem("s_x2w1")]
        s_x2r = sem("s_x2r")
        s_x2rb = [sem("s_x2rb0"), sem("s_x2rb1")]
        s_w = [sem(f"s_w{i}") for i in range(3)]
        s_w4 = [sem("s_w40"), sem("s_w41")]
        s_tr = [sem("s_tr0"), sem("s_tr1")]
        s_tab = [sem("s_tab0"), sem("s_tab1")]
        s_kvs = sem("s_kvs")
        s_cc = sem("s_cc")
        s_kvr = [sem("s_kvr0"), sem("s_kvr1")]
        s_ab = sem("s_ab")
        s_y = sem("s_y")

        def emit_ffn_ln(p, tb, x_f32_src, x_b16_src, res_dst, res_sem,
                        pre_waits, second_ln_dest):
            xf, bufB, bufC, xres = tb["xf"], tb["bufB"], tb["bufC"], tb["xres"]
            Asb, Bsb = tb["Asb"], tb["Bsb"]
            b_sum, b_sq, b_bc = tb["b_sum"], tb["b_sq"], tb["b_bc"]
            b_rt, b_low, b_sex = tb["b_rt"], tb["b_low"], tb["b_sex"]
            pf = tb["pf"]
            exps, lowp = tb["exps"], tb["lowp"]
            vec = [tb["vec"][0:1, 4 * TT * i:4 * TT * (i + 1)] for i in range(2)]
            rb1 = tb["rb1"]
            rmb, rbb = tb["rmb"], tb["rbb"]
            lowf, rtf = tb["lowf"], tb["rtf"]
            negA, negWr = tb["negA"], tb["negWr"]
            Eb = tb["Eb"]
            sx, sxb = tb["sx"], tb["sxb"]
            HC = NCH // 2 * TT

            reg = dict(stats_free=0, rt_free=0, low_free=0,
                       sex_free=0, lowp_free=0, xf_free=0, xres_free=0,
                       bufC_free=0, bc_free=None,
                       bufB_free=[None, None], vec_free=[0, 0],
                       rmb_free=[0, 0])
            fwd = {}

            def wait_tag(e, tag):
                if not tag:
                    return
                eng, v = tag
                if v:
                    e.wait_ge({"pe": s_pe.h, "act": s_act.h,
                               "dve": s_dve.h}[eng], v)

            def t_colsum(e):
                mm = None
                for c in range(NCH):
                    mm = e.matmul(b_low[0:128, 0:1], Asb[:, c * 128:(c + 1) * 128],
                                  negonesb[:], start=(c == 0),
                                  stop=(c == NCH - 1))
                for c in range(NCH):
                    mm = e.matmul(b_rt[0:8, 0:1], Wrsb[:, c * 8:(c + 1) * 8],
                                  negonesb[:], start=(c == 0),
                                  stop=(c == NCH - 1))
                mm.then_inc(s_pe.h, 1)
            p.em("tensor", t_colsum)
            v_cs = s_pe.inc()

            def v_colsum(e, vp=v_cs):
                e.wait_ge(s_pe.h, vp)
                e.tensor_copy(negA[:], b_low[0:128, 0:1])
                e.tensor_copy(negWr[:], b_rt[0:8, 0:1]).then_inc(s_dve.h, 1)
            p.em("vector", v_colsum)
            reg["rt_free"] = reg["low_free"] = s_dve.inc()

            def stats_chain(src_b16, vecb, act_wait, want_mr, load_wait,
                            vec_gate):
                r_bufC = reg["bufC_free"]
                NQ = NCH // 2
                QC = NQ * TT
                sq_v = [s_act.inc() for _ in range(2)]
                st_v = [s_pe.inc() for _ in range(2)]

                def a_sqs(e, s=src_b16, rc=r_bufC, aw=act_wait, lw=load_wait,
                          sq_v=sq_v, st_v=st_v):
                    wait_tag(e, aw)
                    if lw:
                        e.wait_ge(lw[0].h, lw[1])
                    if rc:
                        e.wait_ge(s_pe.h, rc)
                    for q in range(2):
                        if q > 0:
                            e.wait_ge(s_pe.h, st_v[q - 1])
                        e.activation(bufC[:], s[:, q * QC:(q + 1) * QC],
                                     AF.Square).then_inc(s_act.h, 1)
                p.em("scalar", a_sqs)

                r_stats = reg["stats_free"]

                def t_sts(e, s=src_b16, rs=r_stats, lw=load_wait,
                          sq_v=sq_v, st_v=st_v):
                    if lw:
                        e.wait_ge(lw[0].h, lw[1])
                    if rs:
                        e.wait_ge(s_dve.h, rs)
                    for c in range(NCH):
                        e.matmul(b_sum[0:1, 0:TT], onesb[:],
                                 s[:, c * TT:(c + 1) * TT],
                                 start=(c == 0), stop=(c == NCH - 1))
                    for q in range(2):
                        e.wait_ge(s_act.h, sq_v[q])
                        mm = None
                        for c in range(NQ):
                            mm = e.matmul(b_sq[0:1, 0:TT], onesb[:],
                                          bufC[:, c * TT:(c + 1) * TT],
                                          start=(q == 0 and c == 0),
                                          stop=(q == 1 and c == NQ - 1))
                        mm.then_inc(s_pe.h, 1)
                p.em("tensor", t_sts)
                v_pe_stats = st_v[1]
                reg["bufC_free"] = v_pe_stats

                vv = [s_dve.inc() for _ in range(3)]

                def v_vec(e, vp=v_pe_stats, vv=vv, vecb=vecb, vg=vec_gate):
                    e.wait_ge(s_pe.h, vp)
                    wait_tag(e, vg)
                    e.tensor_scalar(vecb[0:1, 0:TT], b_sum[0:1, 0:TT],
                                    1.0 / 4096.0, None, ALU.mult)
                    e.tensor_scalar(vecb[0:1, TT:2 * TT], b_sq[0:1, 0:TT],
                                    1.0 / 4096.0, None,
                                    ALU.mult).then_inc(s_dve.h, 1)
                    e.wait_ge(s_dve.h, vv[0])
                    e.tensor_tensor(vecb[0:1, 2 * TT:3 * TT], vecb[0:1, 0:TT],
                                    vecb[0:1, 0:TT],
                                    ALU.mult).then_inc(s_dve.h, 1)
                    e.wait_ge(s_dve.h, vv[1])
                    e.scalar_tensor_tensor(
                        vecb[0:1, 2 * TT:3 * TT], vecb[0:1, 2 * TT:3 * TT],
                        -1.0, vecb[0:1, TT:2 * TT],
                        ALU.mult, ALU.add).then_inc(s_dve.h, 1)
                p.em("vector", v_vec)
                v_dve_var = vv[2]
                reg["stats_free"] = v_dve_var

                v_lnx = s_act.inc()

                def a_rstd(e, vd=v_dve_var, vl=v_lnx, vecb=vecb):
                    e.wait_ge(s_dve.h, vd)
                    e.activation(vecb[0:1, 2 * TT:3 * TT],
                                 vecb[0:1, 2 * TT:3 * TT], AF.Ln,
                                 bias=epssb[0:1, :]).then_inc(s_act.h, 1)
                    e.wait_ge(s_act.h, vl)
                    e.activation(vecb[0:1, 3 * TT:4 * TT],
                                 vecb[0:1, 2 * TT:3 * TT], AF.Exp,
                                 scale=-0.5).then_inc(s_act.h, 1)
                p.em("scalar", a_rstd)
                v_rstd = s_act.inc()

                if want_mr:
                    def v_mr(e, va=v_rstd, vecb=vecb):
                        e.wait_ge(s_act.h, va)
                        e.tensor_tensor(vecb[0:1, TT:2 * TT], vecb[0:1, 0:TT],
                                        vecb[0:1, 3 * TT:4 * TT],
                                        ALU.mult).then_inc(s_dve.h, 1)
                    p.em("vector", v_mr)
                    v_pre_bc = ("dve", s_dve.inc())
                    fr_off = TT
                else:
                    v_pre_bc = ("act", v_rstd)
                    fr_off = 0

                r_bc = reg["bc_free"]

                def t_bc(e, vd=v_pre_bc, rb=r_bc, fo=fr_off, vecb=vecb):
                    wait_tag(e, vd)
                    wait_tag(e, rb)
                    e.matmul(b_bc[:, 0:TT], ones1x128[:],
                             vecb[0:1, fo:fo + TT],
                             start=True, stop=False)
                    e.matmul(b_bc[:, TT:2 * TT], ones1x128[:],
                             vecb[0:1, 3 * TT:4 * TT],
                             start=False, stop=True).then_inc(s_pe.h, 1)
                p.em("tensor", t_bc)
                v_pe_bc = s_pe.inc()
                return v_pe_bc, v_pe_stats, v_rstd

            def emit_SL(t0):
                par = t0 % 2

                def sy_load(e, t0=t0,
                            pw=(pre_waits if t0 == 0 else [])):
                    for (sw, vv_) in pw:
                        e.wait_ge(sw.h, vv_)
                    rx = fwd.get(("xf", t0 - 1), 0)
                    if rx:
                        e.wait_ge(s_dve.h, rx)
                    srcs = x_f32_src(t0)
                    if not isinstance(srcs, list):
                        srcs = [(xf[:], srcs)]
                    for (dst, ap_) in srcs:
                        e.dma_start(out=dst, in_=ap_).then_inc(sx.h, 16)
                p.em("sync", sy_load)
                srcs0 = x_f32_src(t0)
                nld = len(srcs0) if isinstance(srcs0, list) else 1
                vxf = sx.inc(16 * nld)

                sxbp = sxb[par]

                def gp_load(e, t0=t0, sxbp=sxbp,
                            pw=(pre_waits if t0 == 0 else [])):
                    for (sw, vv_) in pw:
                        e.wait_ge(sw.h, vv_)
                    rb = fwd.get(("bufB", t0 - 2))
                    if rb:
                        wait_tag(e, rb)
                    srcs = x_b16_src(t0)
                    if not isinstance(srcs, list):
                        srcs = [(bufB[par][:], srcs)]
                    for (dst, ap_) in srcs:
                        e.dma_start(out=dst, in_=ap_).then_inc(sxbp.h, 16)
                p.em("gpsimd", gp_load)
                srcs0 = x_b16_src(t0)
                nld = len(srcs0) if isinstance(srcs0, list) else 1
                vxb = sxbp.inc(16 * nld)
                return dict(par=par, vxb=vxb, vxf=vxf, sxbp=sxbp)

            def emit_SS(t0, L):
                par = L["par"]
                vf_gate = reg["vec_free"][par]
                v_pe_bc, v_stats, _ = stats_chain(
                    bufB[par], vec[par], None, True,
                    (L["sxbp"], L["vxb"]), vf_gate)
                r_rmb = reg["rmb_free"][par]

                def a_bc(e, vp=v_pe_bc, rb=r_rmb, par=par):
                    e.wait_ge(s_pe.h, vp)
                    if rb:
                        e.wait_ge(s_dve.h, rb)
                    e.copy(rmb[par][:], b_bc[:, 0:TT])
                    e.copy(rbb[par][:], b_bc[:, TT:2 * TT]).then_inc(s_act.h, 1)
                p.em("scalar", a_bc)
                v_a_bc = s_act.inc()
                reg["bc_free"] = ("act", v_a_bc)
                L["v_a_bc"] = v_a_bc

            def emit_Frl(t0, L):
                par = L["par"]
                r_rt, r_low = reg["rt_free"], reg["low_free"]

                def t_rl(e, rr=r_rt, rl=r_low, vb=L["vxb"], sp=L["sxbp"],
                         par=par):
                    e.wait_ge(sp.h, vb)
                    if rr:
                        e.wait_ge(s_dve.h, rr)
                    for c in range(NCH):
                        e.matmul(b_rt[0:8, 0:TT], Wrsb[:, c * 8:(c + 1) * 8],
                                 bufB[par][:, c * TT:(c + 1) * TT],
                                 start=(c == 0), stop=(c == NCH - 1))
                    if rl:
                        e.wait_ge(s_dve.h, rl)
                    mm = None
                    for c in range(NCH):
                        mm = e.matmul(b_low[:, 0:TT],
                                      Asb[:, c * 128:(c + 1) * 128],
                                      bufB[par][:, c * TT:(c + 1) * TT],
                                      start=(c == 0), stop=(c == NCH - 1))
                    mm.then_inc(s_pe.h, 1)
                p.em("tensor", t_rl)
                L["v_pe_rl"] = s_pe.inc()

            def emit_Fmid(t0, L):
                par = L["par"]
                vfx = [s_dve.inc() for _ in range(4)]

                def v_fix(e, vp=L["v_pe_rl"], va=L["v_a_bc"], vfx=vfx,
                          par=par):
                    e.wait_ge(s_pe.h, vp)
                    e.wait_ge(s_act.h, va)
                    e.tensor_tensor(rtf[0:8, :], b_rt[0:8, 0:TT],
                                    rbb[par][0:8, :],
                                    ALU.mult).then_inc(s_dve.h, 1)
                    e.wait_ge(s_dve.h, vfx[0])
                    e.scalar_tensor_tensor(
                        rtf[0:8, :], rmb[par][0:8, :], negWr[0:8, :],
                        rtf[0:8, :], ALU.mult, ALU.add).then_inc(s_dve.h, 1)
                    e.tensor_tensor(lowf[:], b_low[:, 0:TT], rbb[par][:],
                                    ALU.mult).then_inc(s_dve.h, 1)
                    e.wait_ge(s_dve.h, vfx[2])
                    e.scalar_tensor_tensor(
                        lowf[:], rmb[par][:], negA[:], lowf[:],
                        ALU.mult, ALU.add).then_inc(s_dve.h, 1)
                p.em("vector", v_fix)
                v_fix_rt, v_fix_low = vfx[1], vfx[3]
                reg["rt_free"] = v_fix_rt
                reg["rmb_free"][par] = v_fix_low

                def a_exp(e, vd=v_fix_rt):
                    e.wait_ge(s_dve.h, vd)
                    e.activation(exps[0:8, :], rtf[0:8, :], AF.Exp,
                                 bias=brsb[0:8, :]).then_inc(s_act.h, 1)
                p.em("scalar", a_exp)
                v_act_exp = s_act.inc()

                r_sex = reg["sex_free"]

                def t_se(e, va=v_act_exp, rs=r_sex):
                    e.wait_ge(s_act.h, va)
                    if rs:
                        e.wait_ge(s_dve.h, rs)
                    e.matmul(b_sex[:, 0:TT], Eb[:], exps[0:8, :],
                             start=True, stop=True)
                    e.matmul(b_rt[0:1, TT:2 * TT], ones8[:], exps[0:8, :],
                             start=True, stop=True).then_inc(s_pe.h, 1)
                p.em("tensor", t_se)
                v_pe_se = s_pe.inc()

                def v_rinv(e, vp=v_pe_se, par=par):
                    e.wait_ge(s_pe.h, vp)
                    e.reciprocal(vec[par][0:1, 3 * TT:4 * TT],
                                 b_rt[0:1, TT:2 * TT]).then_inc(s_dve.h, 1)
                p.em("vector", v_rinv)
                v_dve_rinv = s_dve.inc()

                def t_rb8(e, vd=v_dve_rinv, par=par):
                    e.wait_ge(s_dve.h, vd)
                    e.matmul(b_rt[:, 0:TT], ones1x128[:],
                             vec[par][0:1, 3 * TT:4 * TT],
                             start=True, stop=True).then_inc(s_pe.h, 1)
                p.em("tensor", t_rb8)
                v_pe_rb8 = s_pe.inc()
                reg["vec_free"][par] = ("pe", v_pe_rb8)

                r_lowp = reg["lowp_free"]
                vm = [s_dve.inc() for _ in range(2)]

                def v_lowp(e, vfl=v_fix_low, vse=v_pe_se, vrb=v_pe_rb8,
                           rl=r_lowp, vm=vm):
                    e.wait_ge(s_dve.h, vfl)
                    e.wait_ge(s_pe.h, vse)
                    e.tensor_tensor(lowf[:], lowf[:], b_sex[:, 0:TT],
                                    ALU.mult).then_inc(s_dve.h, 1)
                    e.wait_ge(s_dve.h, vm[0])
                    e.wait_ge(s_pe.h, vrb)
                    if rl:
                        e.wait_ge(s_pe.h, rl)
                    e.tensor_tensor(lowp[:], lowf[:], b_rt[:, 0:TT],
                                    ALU.mult).then_inc(s_dve.h, 1)
                p.em("vector", v_lowp)
                lowp_ready = vm[1]
                reg["sex_free"] = vm[0]
                reg["low_free"] = vm[0]
                reg["rt_free"] = lowp_ready

                v_fp = {}
                last_f_pe = None
                r_xres = reg["xres_free"]
                for ocp in range(16):
                    bank = pf[ocp % 2]

                    def t_f(e, ocp=ocp, bank=bank, vl=lowp_ready, fp=v_fp):
                        if ocp == 0:
                            e.wait_ge(s_dve.h, vl)
                        if ocp >= 2:
                            e.wait_ge(s_dve.h, fp[ocp - 2])
                        e.matmul(bank[:, 0:TT],
                                 Bsb[:, (2 * ocp) * 128:(2 * ocp + 1) * 128],
                                 lowp[:], start=True, stop=False)
                        e.matmul(bank[:, TT:2 * TT],
                                 Bsb[:, (2 * ocp + 1) * 128:(2 * ocp + 2) * 128],
                                 lowp[:], start=False,
                                 stop=True).then_inc(s_pe.h, 1)
                    p.em("tensor", t_f)
                    v_pe_f = s_pe.inc()
                    last_f_pe = v_pe_f

                    def v_res(e, ocp=ocp, bank=bank, vp=v_pe_f, t0=t0,
                              vxf=L["vxf"], rx=r_xres):
                        e.wait_ge(s_pe.h, vp)
                        if ocp == 0:
                            e.wait_ge(sx.h, vxf)
                            if rx:
                                e.wait_ge(s_dve.h, rx)
                            if t0 > 0:
                                e.wait_ge(res_sem.h, 16 * t0)
                        e.tensor_tensor(xres[:, ocp * 512:(ocp + 1) * 512],
                                        bank[:, 0:2 * TT],
                                        xf[:, ocp * 512:(ocp + 1) * 512],
                                        ALU.add).then_inc(s_dve.h, 1)
                    p.em("vector", v_res)
                    v_fp[ocp] = s_dve.inc()
                res_done = v_fp[15]
                reg["lowp_free"] = last_f_pe
                fwd[("xf", t0)] = res_done
                L["res_done"] = res_done
                L["last_f_pe"] = last_f_pe

                def sy_spill(e, t0=t0, vd=res_done):
                    e.wait_ge(s_dve.h, vd)
                    e.dma_start(out=res_dst(t0),
                                in_=xres[:]).then_inc(res_sem.h, 16)
                p.em("sync", sy_spill)
                res_sem.inc(16)

                if second_ln_dest is None:
                    reg["xres_free"] = res_done
                    fwd[("bufB", t0)] = ("pe", L["v_pe_rl"])

            def emit_Fln2(t0, L):
                par = L["par"]

                def a_x1b(e, vd=L["res_done"], vpe=L["last_f_pe"], par=par):
                    e.wait_ge(s_dve.h, vd)
                    e.wait_ge(s_pe.h, vpe)
                    e.copy(bufB[par][:], xres[:]).then_inc(s_act.h, 1)
                p.em("scalar", a_x1b)
                v_x1b = s_act.inc()

                v_pe_bc2, v_stats2, _ = stats_chain(
                    bufB[par], vec[par], ("act", v_x1b), False,
                    (s_act, v_x1b), reg["vec_free"][par])
                fwd[("bufB", t0)] = ("pe", v_stats2)

                def a_rb(e, vp=v_pe_bc2):
                    e.wait_ge(s_pe.h, vp)
                    e.copy(rb1[:], b_bc[:, TT:2 * TT]).then_inc(s_act.h, 1)
                p.em("scalar", a_rb)
                v_act_rb = s_act.inc()

                vn = [s_dve.inc() for _ in range(2)]

                def v_norm(e, va=v_act_rb, t0=t0, vp=v_stats2, vn=vn,
                           par=par):
                    e.wait_ge(s_act.h, va)
                    e.wait_ge(s_pe.h, vp)
                    tt_ = None
                    for c in range(NCH):
                        tt_ = e.tensor_tensor(
                            second_ln_dest(c, t0),
                            xres[:, c * TT:(c + 1) * TT],
                            b_bc[:, 0:TT], ALU.subtract)
                    tt_.then_inc(s_dve.h, 1)
                    e.wait_ge(s_dve.h, vn[0])
                    tt_ = None
                    for c in range(NCH):
                        tt_ = e.tensor_tensor(
                            second_ln_dest(c, t0),
                            second_ln_dest(c, t0),
                            rb1[:], ALU.mult)
                    tt_.then_inc(s_dve.h, 1)
                p.em("vector", v_norm)
                ln2_ready = vn[1]
                reg["bc_free"] = ("dve", ln2_ready)
                reg["xres_free"] = ln2_ready
                reg["vec_free"][par] = ("pe", v_pe_bc2)

            L = {}
            L[0] = emit_SL(0)
            emit_SS(0, L[0])
            L[1] = emit_SL(1)
            emit_Frl(0, L[0])
            emit_SS(1, L[1])
            emit_Fmid(0, L[0])
            for t0 in range(1, NT):
                if t0 + 1 < NT:
                    L[t0 + 1] = emit_SL(t0 + 1)
                emit_Frl(t0, L[t0])
                if second_ln_dest is not None:
                    emit_Fln2(t0 - 1, L[t0 - 1])
                if t0 + 1 < NT:
                    emit_SS(t0 + 1, L[t0 + 1])
                emit_Fmid(t0, L[t0])
            if second_ln_dest is not None:
                emit_Fln2(NT - 1, L[NT - 1])

        def mk_ffn_bufs(cx, pref):
            sbp = lambda n, s, d: cx.enter_context(
                nc.sbuf_tensor(pref + n, s, d))
            psp = lambda n: cx.enter_context(
                nc.psum_tensor(pref + n, [128, 512], F32))
            return dict(
                xf=sbp("xf", [128, NCH * TT], F32),
                bufB=[sbp(f"bufB{i}", [128, NCH * TT], BF16) for i in range(2)],
                bufC=sbp("bufC", [128, NCH * TT // 2], BF16),
                xres=sbp("xres", [128, NCH * TT], F32),
                Asb=sbp("A", [128, NCH * 128], BF16),
                Bsb=sbp("B", [128, NCH * 128], BF16),
                exps=sbp("exps", [8, TT], F32),
                lowp=sbp("lowp", [128, TT], BF16),
                vec=sbp("vec", [1, 8 * TT], F32),
                rb1=sbp("rb1", [128, TT], BF16),
                rmb=[sbp(f"rmb{i}", [128, TT], F32) for i in range(2)],
                rbb=[sbp(f"rbb{i}", [128, TT], F32) for i in range(2)],
                lowf=sbp("lowf", [128, TT], F32),
                rtf=sbp("rtf", [8, TT], F32),
                negA=sbp("negA", [128, 1], F32),
                negWr=sbp("negWr", [8, 1], F32),
                Eb=sbp("Eb", [8, 128], F32),
                b_sum=psp("bsum"), b_sq=psp("bsq"), b_bc=psp("bbc"),
                b_rt=psp("brt"), b_low=psp("blow"), b_sex=psp("bsex"),
                pf=[psp("pf0"), psp("pf1")],
            )

        def emit_ab_loads(p, tb, pre_waits=()):
            def gp_ab(e, pw=tuple(pre_waits)):
                for (sw, vv_) in pw:
                    e.wait_ge(sw.h, vv_)
                e.dma_start(out=tb["Asb"][:], in_=Ad.ap()).then_inc(s_ab.h, 16)
                e.dma_start(out=tb["Bsb"][:], in_=Bd.ap()).then_inc(s_ab.h, 16)
                e.dma_start(out=tb["Eb"][:], in_=Ed.ap()).then_inc(s_ab.h, 16)
            p.em("gpsimd", gp_ab)
            s_ab.inc(48)
            vab = s_ab.v

            def t_wab(e, v=vab):
                e.wait_ge(s_ab.h, v)
            p.em("tensor", t_wab)

        with contextlib.ExitStack() as c1:
            p1 = Prog()

            def sy_c0(e):
                e.dma_start(out=onesb[:], in_=onesd.ap()).then_inc(s_c0.h, 16)
                e.dma_start(out=negonesb[:], in_=negonesd.ap()).then_inc(s_c0.h, 16)
                e.dma_start(out=ones1x128[:], in_=ones1x128d.ap()).then_inc(s_c0.h, 16)
                e.dma_start(out=ones1x8[:], in_=ones1x8d.ap()).then_inc(s_c0.h, 16)
                e.dma_start(out=ones8[:], in_=ones8d.ap()).then_inc(s_c0.h, 16)
                e.dma_start(out=nwsb[:], in_=nwd.ap()).then_inc(s_c0.h, 16)
                e.dma_start(out=brsb[:], in_=brd.ap()).then_inc(s_c0.h, 16)
                e.dma_start(out=m0sb[:], in_=m0d.ap()).then_inc(s_c0.h, 16)
                e.dma_start(out=kdsb[:], in_=kdd.ap()).then_inc(s_c0.h, 16)
                e.dma_start(out=epssb[:], in_=epsd.ap()).then_inc(s_c0.h, 16)
            p1.em("sync", sy_c0)
            s_c0.inc(16 * 10)

            def gp_c1(e):
                e.dma_start(out=Wrsb[:], in_=Wrd.ap()).then_inc(s_c0g.h, 16)
            p1.em("gpsimd", gp_c1)
            s_c0g.inc(16)
            CONST = s_c0.v
            CONSTG = s_c0g.v

            tb1 = mk_ffn_bufs(c1, "p1_")
            tb1["sx"], tb1["sxb"] = s_x0, s_x0b
            emit_ab_loads(p1, tb1)

            def t_wc(e, v=CONST, vg=CONSTG):
                e.wait_ge(s_c0.h, v)
                e.wait_ge(s_c0g.h, vg)
            p1.em("tensor", t_wc)

            def a_wc(e, v=CONST):
                e.wait_ge(s_c0.h, v)
            p1.em("scalar", a_wc)

            def v_wc(e, v=CONST):
                e.wait_ge(s_c0.h, v)
            p1.em("vector", v_wc)

            emit_ffn_ln(
                p1, tb1,
                x_f32_src=lambda t0: x0d.ap()[t0],
                x_b16_src=lambda t0: x0d.ap()[t0],
                res_dst=lambda t0: x1d.ap()[t0],
                res_sem=s_x1w, pre_waits=[],
                second_ln_dest=lambda c, t0: ln2[:, c * T + t0 * TT:
                                                 c * T + t0 * TT + TT])

            def sy_fin1(e, v=s_x1w.v):
                e.wait_ge(s_x1w.h, v)
            p1.em("sync", sy_fin1)

            if debug:
                def gp_d1(e, v=s_x1w.v, vd=s_dve.v):
                    e.wait_ge(s_x1w.h, v)
                    e.wait_ge(s_dve.h, vd)
                    e.dma_start(out=dbg["ln2"].ap(), in_=ln2[:]).then_inc(s_dbg.h, 16)
                p1.em("gpsimd", gp_d1)
                s_dbg.inc(16)

            p1.flush(nc)

        with contextlib.ExitStack() as cA:
            attn = cA.enter_context(nc.sbuf_tensor("attn", [128, NCH * T], BF16))
            psA = lambda n: cA.enter_context(nc.psum_tensor(n, [128, 512], F32))
            pmm = [psA(f"p2mm{i}") for i in range(4)]
            psc = [psA("p2sc0"), psA("p2sc1")]
            pcs = psA("p2cs")
            pkv = psA("p2kv")
            c2 = cA.enter_context(contextlib.ExitStack())
            sb2 = lambda n, s, d: c2.enter_context(nc.sbuf_tensor(n, s, d))
            qfm = [sb2(f"qfm{i}", [128, HPG * T], BF16) for i in range(2)]
            kfm = [sb2(f"kfm{i}", [128, HPG * T], BF16) for i in range(2)]
            ktm = [sb2(f"ktm{i}", [128, HPG * T], BF16) for i in range(2)]
            vtm = [sb2(f"vtm{i}", [128, HPG * T], BF16) for i in range(2)]
            qdq = [sb2(f"qdq{i}", [128, HPG * T], BF16) for i in range(2)]
            gsig = [sb2(f"gsig{i}", [128, HPG * T], BF16) for i in range(2)]
            wbuf = [sb2(f"wbuf{i}", [128, 16 * 128], BF16) for i in range(3)]
            ddb = [sb2(f"ddb{i}", [128, 512], BF16) for i in range(2)]
            qdbb = [sb2(f"qdbb{i}", [128, HPG * 256], BF16) for i in range(2)]
            kvf = [sb2(f"kvf{i}", [128, HPG * 128], F32) for i in range(2)]
            kvb = [sb2(f"kvb{i}", [128, HPG * 3 * 128], BF16) for i in range(2)]
            kvrxb = [sb2(f"kvrxb{i}", [128, HPG * 128], BF16) for i in range(2)]
            dssb = [sb2(f"dssb{i}", [128, 512], BF16) for i in range(2)]
            sqscr = sb2("sqscr", [128, T], BF16)


            p2 = Prog()

            def v_init(e):
                e.memset(rmsacc[:], 0.0)
            p2.em("vector", v_init)

            R = dict(w_pe={}, wload=0, bp_evac={0: 0, 1: 0},
                     qkv_evac={}, tr={}, tab={}, attn_pe={}, attn_dve={},
                     evac_n={}, sc_pe={}, acc_pe={}, kvd_pe={}, ds={},
                     kvupd={}, pool_qdq={}, cc={}, kvs={}, kvr={},
                     mask={}, corr_pe={}, corr_dve={}, stats_dve={},
                     sq_act={}, g2={}, evac_j={},
                     tr_kv4={}, pool_kdk={}, kv_chain_dve={})

            JMAP = [(0, 1), (0, 2), (1, 1), (1, 2), (0, 0), (1, 0),
                    (0, 3), (1, 3)]

            def wsrc(g, j):
                h_loc, kind = JMAP[j]
                if kind == 3:
                    return wgated.ap()[2 * g + h_loc]
                h = HPG * g + h_loc
                return wqkvd.ap()[3 * h + kind]

            def emit_wloads(g, lo=0, hi=16):
                for t in range(lo, hi):
                    j, half = divmod(t, 2)
                    if True:
                        idx = R["wload"]
                        par = idx % 3

                        def gp_w(e, g=g, j=j, half=half, par=par, idx=idx):
                            if idx >= 3:
                                e.wait_ge(s_pe.h, R["w_pe"][idx - 3])
                            e.dma_start(
                                out=wbuf[par][:],
                                in_=wsrc(g, j)[:, half * 2048:(half + 1) * 2048]
                            ).then_inc(s_w[par].h, 16)
                        p2.em("gpsimd", gp_w)
                        s_w[par].inc(16)
                        R["wload"] += 1

            def emit_tabs(g):
                par = g % 2

                def sy_tab(e, g=g, par=par):
                    if g >= 1:
                        e.wait_ge(s_dve.h, R["ds"][(g - 1, 1, NB - 1)])
                    e.dma_start(
                        out=ddb[0][:],
                        in_=ddTd.ap()[HPG * g]).then_inc(s_tab[par].h, 16)
                    e.dma_start(
                        out=ddb[1][:],
                        in_=ddTd.ap()[HPG * g + 1]).then_inc(s_tab[par].h, 16)
                    e.dma_start(
                        out=qdbb[par][:],
                        in_=qdbd.ap()[g]).then_inc(s_tab[par].h, 16)
                p2.em("sync", sy_tab)
                s_tab[par].inc(48)
                R["tab"][g] = s_tab[par].v

            def mk_qkv_half(g, j, half):
                gp = g % 2
                widx0 = 16 * g
                bp = j % 2
                b0, b1 = pmm[2 * bp], pmm[2 * bp + 1]
                r_evac = R["bp_evac"][bp] if half == 0 else 0
                idx = widx0 + 2 * j + half
                wv = 16 * (idx // 3 + 1)

                def t_mmh(e, g=g, j=j, half=half, b0=b0, b1=b1, re_=r_evac,
                          wv=wv, idx=idx):
                    par = idx % 3
                    e.wait_ge(s_w[par].h, wv)
                    if re_:
                        e.wait_ge(s_act.h, re_)
                    mm = None
                    for ci in range(16):
                        c = half * 16 + ci
                        for t2, bank in ((0, b0), (1, b1)):
                            mm = e.matmul(
                                bank[:, 0:512],
                                wbuf[par][:, ci * 128:(ci + 1) * 128],
                                ln2[:, c * T + t2 * 512:
                                    c * T + t2 * 512 + 512],
                                start=(c == 0), stop=(c == NCH - 1))
                    mm.then_inc(s_pe.h, 1)
                R["w_pe"][idx] = s_pe.inc()
                return t_mmh

            def mk_qkv_evac(g, j):
                gp = g % 2
                bp = j % 2
                b0, b1 = pmm[2 * bp], pmm[2 * bp + 1]
                v_pe_j = R["w_pe"][16 * g + 2 * j + 1]
                h_loc, kind = JMAP[j]

                def a_ev(e, g=g, j=j, gp=gp, b0=b0, b1=b1, vp=v_pe_j,
                         h_loc=h_loc, kind=kind):
                    e.wait_ge(s_pe.h, vp)
                    if kind == 0:
                        dst = qfm[gp]
                        if g >= 2:
                            e.wait_ge(s_pe.h, R["attn_pe"][g - 2])
                            e.wait_ge(s_pool.h, R["pool_qdq"][g - 2])
                    elif kind == 1:
                        dst = kfm[gp]
                        if g >= 2:
                            e.wait_ge(s_tr[gp].h, R["tr"][g - 2])
                    elif kind == 2:
                        dst = qdq[gp]
                        if g >= 2:
                            e.wait_ge(s_tr[gp].h, R["tr"][g - 2])
                    else:
                        dst = gsig[g % 2]
                        if g >= 2:
                            e.wait_ge(s_pool.h, R["g2"][g - 2])
                    fn = AF.Sigmoid if kind == 3 else AF.Silu
                    aa = None
                    for t2 in (0, 1):
                        aa = e.activation(
                            dst[:, h_loc * T + t2 * 512:
                                h_loc * T + t2 * 512 + 512],
                            (b0 if t2 == 0 else b1)[:, 0:512], fn)
                    aa.then_inc(s_act.h, 1)
                v_evac = s_act.inc()
                R["bp_evac"][bp] = v_evac
                R["evac_j"][(g, j)] = v_evac

                tr_cl = None
                if kind in (1, 2):
                    srct = kfm[gp] if kind == 1 else qdq[gp]
                    dstt = ktm[gp] if kind == 1 else vtm[gp]

                    def sy_tr(e, gp=gp, h_loc=h_loc, srct=srct, dstt=dstt,
                              ve=v_evac, g=g):
                        e.wait_ge(s_act.h, ve)
                        if g >= 2:
                            e.wait_ge(s_pe.h, R["attn_pe"][g - 2])
                        e.dma_start_transpose(
                            dstt[:, h_loc * T:(h_loc + 1) * T].rearrange(
                                "p (j t) -> p j t", j=8),
                            srct[:, h_loc * T:(h_loc + 1) * T]
                        ).then_inc(s_tr[gp].h, 16)
                    s_tr[gp].inc(16)
                    tr_cl = sy_tr
                    if j == 3:
                        R["tr_kv4"][g] = s_tr[gp].v
                return a_ev, tr_cl

            def emit_kdk(g):
                gp = g % 2

                def vk(e, g=g, gp=gp):
                    e.wait_ge(s_tab[gp].h, R["tab"][g])
                    e.wait_ge(s_tr[gp].h, R["tr_kv4"][g])
                    for h_loc in range(HPG):
                        h = HPG * g + h_loc
                        kv_ = ktm[gp][:, h_loc * T:(h_loc + 1) * T].rearrange(
                            "p (n m t) -> p n m t", n=NB, m=2)
                        kdv = kdsb[:, 2 * h:2 * h + 2][
                            :, None, :, None].broadcast_to([128, NB, 2, 128])
                        e.tensor_tensor(kv_, kv_, kdv,
                                        ALU.mult).then_inc(s_pool.h, 1)
                p2.em("vector", vk)
                s_pool.inc()
                R["pool_kdk"][g] = s_pool.inc()

            def emit_qdq(g):
                gp = g % 2

                def vq(e, g=g, gp=gp):
                    e.wait_ge(s_tab[gp].h, R["tab"][g])
                    e.wait_ge(s_tr[gp].h, R["tr_kv4"][g])
                    if g >= 2:
                        e.wait_ge(s_pe.h, R["corr_pe"].get(g - 2, 0))
                    for h_loc in range(HPG):
                        e.wait_ge(s_act.h, R["evac_j"][(g, 4 + h_loc)])
                        qv = qfm[gp][:, h_loc * T:(h_loc + 1) * T].rearrange(
                            "p (n t) -> p n t", n=NB)
                        qdv = qdbb[gp][:, h_loc * 256:(h_loc + 1) * 256][
                            :, None, :].broadcast_to([128, NB, 256])
                        ov = qdq[gp][:, h_loc * T:(h_loc + 1) * T].rearrange(
                            "p (n t) -> p n t", n=NB)
                        e.tensor_tensor(ov, qv, qdv,
                                        ALU.mult).then_inc(s_pool.h, 1)
                p2.em("vector", vq)
                s_pool.inc()
                R["pool_qdq"][g] = s_pool.inc()

            def mk_kvd(g, n):
                gp = g % 2

                def t_kvd(e, g=g, gp=gp, n=n):
                    if n == 0:
                        e.wait_ge(s_pool.h, R["pool_kdk"][g])
                        e.wait_ge(s_tr[gp].h, R["tr_kv4"][g])
                        if g >= 1:
                            e.wait_ge(s_dve.h, R["kvupd"][(g - 1, 1, NB - 1)])
                    else:
                        e.wait_ge(s_dve.h, R["kvupd"][(g, 1, n - 1)])
                    mm = None
                    for h_loc in range(HPG):
                        mtile = h_loc * 8 + 2 * n
                        for mt in range(2):
                            mm = e.matmul(
                                pkv[:, h_loc * 128:(h_loc + 1) * 128],
                                ktm[gp][:, (mtile + mt) * 128:
                                        (mtile + mt + 1) * 128],
                                vtm[gp][:, (mtile + mt) * 128:
                                        (mtile + mt + 1) * 128],
                                start=(h_loc == 0 and mt == 0),
                                stop=(h_loc == 1 and mt == 1))
                    mm.then_inc(s_pe.h, 1)
                R["kvd_pe"][(g, n)] = s_pe.inc()

                vks = []
                for h_loc in range(HPG):
                    if n < NB - 1:
                        vkv = [s_dve.inc() for _ in range(2)]
                    else:
                        vkv = [s_dve.inc()]

                    def v_kv(e, g=g, gp=gp, n=n, h_loc=h_loc, vkv=vkv):
                        e.wait_ge(s_pe.h, R["kvd_pe"][(g, n)])
                        sl = slice(h_loc * 128, (h_loc + 1) * 128)
                        if n == 0:
                            if g >= 2:
                                e.wait_ge(s_kvs.h, R["kvs"][g - 2])
                            op = e.tensor_copy(kvf[gp][:, sl], pkv[:, sl])
                        else:
                            op = e.scalar_tensor_tensor(
                                kvf[gp][:, sl], kvf[gp][:, sl],
                                _BD[HPG * g + h_loc], pkv[:, sl],
                                ALU.mult, ALU.add)
                        op.then_inc(s_dve.h, 1)
                        if n < NB - 1:
                            e.wait_ge(s_dve.h, vkv[0])
                            if g >= 2:
                                e.wait_ge(s_pe.h, R["attn_pe"][g - 2])
                            ks = slice((h_loc * 3 + n) * 128,
                                       (h_loc * 3 + n + 1) * 128)
                            e.tensor_copy(kvb[gp][:, ks],
                                          kvf[gp][:, sl]).then_inc(s_dve.h, 1)
                    vks.append(v_kv)
                    R["kvupd"][(g, h_loc, n)] = vkv[-1]
                return t_kvd, vks

            def mk_scores(g, n):
                gp = g % 2

                def t_sc(e, g=g, gp=gp, n=n):
                    for h_loc in range(HPG):
                        bank = psc[h_loc]
                        if n == 0 and h_loc == 0:
                            e.wait_ge(s_act.h, R["evac_j"][(g, 5)])
                            if g >= 1:
                                e.wait_ge(s_dve.h, R["stats_dve"][g - 1])
                        if n > 0:
                            e.wait_ge(s_act.h,
                                      R["evac_n"][(g, h_loc, n - 1)])
                        base = h_loc * T + n * 256
                        e.matmul(bank[:, 0:256],
                                 kfm[gp][:, base:base + 128],
                                 qfm[gp][:, base:base + 256],
                                 start=True, stop=False)
                        e.matmul(bank[:, 256:512],
                                 kfm[gp][:, base + 128:base + 256],
                                 qfm[gp][:, base:base + 256],
                                 start=False,
                                 stop=True).then_inc(s_pe.h, 1)
                for h_loc in range(HPG):
                    R["sc_pe"][(g, h_loc, n)] = s_pe.inc()

                dscls = []
                for h_loc in range(HPG):
                    def v_ds(e, g=g, gp=gp, n=n, h_loc=h_loc):
                        bank = psc[h_loc]
                        e.wait_ge(s_pe.h, R["sc_pe"][(g, h_loc, n)])
                        if n > 0:
                            e.wait_ge(s_pe.h,
                                      R["acc_pe"][(g, h_loc, n - 1)])
                        elif g >= 1:
                            e.wait_ge(s_pe.h,
                                      R["acc_pe"][(g - 1, h_loc, NB - 1)])
                        e.tensor_tensor(
                            dssb[h_loc][:], bank[:, 0:512],
                            ddb[h_loc][:],
                            ALU.mult).then_inc(s_dve.h, 1)
                    dscls.append(v_ds)
                    R["ds"][(g, h_loc, n)] = s_dve.inc()
                return t_sc, dscls

            def mk_acc(g, n):
                gp = g % 2

                def t_acc(e, g=g, gp=gp, n=n):
                    if n == 0:
                        e.wait_ge(s_tr[gp].h, R["tr_kv4"][g])
                    if n == 1:
                        e.wait_ge(s_pool.h, R["pool_qdq"][g])
                    for h_loc in range(HPG):
                        bank = psc[h_loc]
                        e.wait_ge(s_dve.h, R["ds"][(g, h_loc, n)])
                        base = h_loc * T + n * 256
                        mtile = h_loc * 8 + 2 * n
                        e.matmul(bank[:, 0:256],
                                 vtm[gp][:, mtile * 128:(mtile + 1) * 128],
                                 dssb[h_loc][:, 0:256],
                                 start=True, stop=False)
                        mm = e.matmul(
                            bank[:, 0:256],
                            vtm[gp][:, (mtile + 1) * 128:(mtile + 2) * 128],
                            dssb[h_loc][:, 256:512],
                            start=False, stop=(n == 0))
                        if n > 0:
                            e.wait_ge(s_dve.h, R["kvupd"][(g, h_loc, n - 1)])
                            ks = slice((h_loc * 3 + n - 1) * 128,
                                       (h_loc * 3 + n) * 128)
                            mm = e.matmul(
                                bank[:, 0:256],
                                kvb[gp][:, ks],
                                qdq[gp][:, base:base + 256],
                                start=False, stop=True)
                        mm.then_inc(s_pe.h, 1)
                for h_loc in range(HPG):
                    R["acc_pe"][(g, h_loc, n)] = s_pe.inc()

                acls = []
                for h_loc in range(HPG):
                    def a_acc(e, g=g, n=n, h_loc=h_loc):
                        bank = psc[h_loc]
                        e.wait_ge(s_pe.h, R["acc_pe"][(g, h_loc, n)])
                        h = HPG * g + h_loc
                        e.copy(attn[:, h * T + n * 256:h * T + n * 256 + 256],
                               bank[:, 0:256]).then_inc(s_act.h, 1)
                    acls.append(a_acc)
                    R["evac_n"][(g, h_loc, n)] = s_act.inc()
                return t_acc, acls

            def emit_collective(g):
                gp = g % 2
                v_kvs = s_kvs.inc(16)
                R["kvs"][g] = v_kvs
                v_cc = s_cc.inc()
                v_kvr = s_kvr[gp].inc(16)
                R["kvr"][g] = v_kvr

                def gp_cc(e, g=g, gp=gp, v_kvs=v_kvs, v_cc=v_cc):
                    e.wait_ge(s_dve.h, R["kv_chain_dve"][g])
                    e.dma_start(
                        out=kvsndd.ap()[g],
                        in_=kvf[gp][:]).then_inc(s_kvs.h, 16)
                    e.wait_ge(s_kvs.h, v_kvs)
                    e.collective_compute(
                        "AllGather", ALU.bypass,
                        replica_groups=replica_groups,
                        ins=[kvsndd.ap()[g].opt()],
                        outs=[kvrcvd.ap()[g].opt()],
                    ).then_inc(s_cc.h, 1)
                    e.wait_ge(s_cc.h, v_cc)
                    if g >= 2:
                        e.wait_ge(s_dve.h, R["mask"][g - 2])
                        e.wait_ge(s_pe.h, R["corr_pe"][g - 2])
                    e.dma_start(
                        out=kvrxb[gp][:],
                        in_=kvrcvd.ap()[g][0:128]
                    ).then_inc(s_kvr[gp].h, 16)
                p2.em("gpsimd", gp_cc)

            def mk_mask(g):
                gp = g % 2

                def v_mask(e, g=g, gp=gp):
                    e.wait_ge(s_kvr[gp].h, R["kvr"][g])
                    e.tensor_scalar(kvrxb[gp][:], kvrxb[gp][:], m0sb[:],
                                    None, ALU.mult).then_inc(s_dve.h, 1)
                R["mask"][g] = s_dve.inc()
                return v_mask

            def mk_corr_pair(g, pi, prev_add):
                gp = g % 2
                corr_banks = [pcs, psc[0]]
                pe_cls = []
                dve_cls = []
                for i in (2 * pi, 2 * pi + 1):
                    h_loc, n = i // NB, i % NB
                    bank = corr_banks[i % 2]

                    def t_corr(e, g=g, gp=gp, i=i, h_loc=h_loc, n=n,
                               bank=bank):
                        if i == 0:
                            e.wait_ge(s_dve.h, R["mask"][g])
                        if i == 1:
                            e.wait_ge(s_act.h, R["evac_n"][(g, 0, NB - 1)])
                        if i >= 2:
                            e.wait_ge(s_dve.h, prev_add[i - 2])
                        elif i < 2 and g >= 1:
                            e.wait_ge(s_dve.h, R["stats_dve"].get(g - 1, 0))
                        e.matmul(bank[:, 0:256],
                                 kvrxb[gp][:, h_loc * 128:(h_loc + 1) * 128],
                                 qdq[gp][:, h_loc * T + n * 256:
                                         h_loc * T + n * 256 + 256],
                                 start=True, stop=True).then_inc(s_pe.h, 1)
                    pe_cls.append(t_corr)
                    vpe = s_pe.inc()
                    R["corr_pe"][g] = vpe

                    def v_cadd(e, g=g, i=i, h_loc=h_loc, n=n, bank=bank,
                               vp=vpe):
                        e.wait_ge(s_pe.h, vp)
                        e.wait_ge(s_act.h, R["evac_n"][(g, h_loc, n)])
                        h = HPG * g + h_loc
                        sl = slice(h * T + n * 256, h * T + n * 256 + 256)
                        e.scalar_tensor_tensor(
                            attn[:, sl], bank[:, 0:256], _BD[h] ** n,
                            attn[:, sl], ALU.mult,
                            ALU.add).then_inc(s_dve.h, 1)
                    dve_cls.append(v_cadd)
                    prev_add[i] = s_dve.inc()
                return pe_cls, dve_cls

            def mk_stats(g):
                R["corr_dve"][g] = s_dve.v
                act_cls = []
                pe_cls = []
                for h_loc in range(HPG):
                    def a_sqr(e, g=g, h_loc=h_loc, vd=R["corr_dve"][g]):
                        e.wait_ge(s_dve.h, vd)
                        if h_loc == 1:
                            e.wait_ge(s_pe.h, R["stats_pe"][(g, 0)])
                        h = HPG * g + h_loc
                        e.activation(sqscr[:], attn[:, h * T:(h + 1) * T],
                                     AF.Square).then_inc(s_act.h, 1)
                    act_cls.append(a_sqr)
                    R["sq_act"][(g, h_loc)] = s_act.inc()

                    def t_st(e, g=g, h_loc=h_loc):
                        e.wait_ge(s_act.h, R["sq_act"][(g, h_loc)])
                        if h_loc == 0 and g >= 1:
                            e.wait_ge(s_dve.h, R["stats_dve"][g - 1])
                        mm = None
                        for t2 in (0, 1):
                            mm = e.matmul(psc[t2][0:1, 0:512], onesb[:],
                                          sqscr[:, t2 * 512:(t2 + 1) * 512],
                                          start=(h_loc == 0),
                                          stop=(h_loc == 1))
                        mm.then_inc(s_pe.h, 1)
                    pe_cls.append(t_st)
                    R["stats_pe"][(g, h_loc)] = s_pe.inc()

                def v_st(e, g=g):
                    e.wait_ge(s_pe.h, R["stats_pe"][(g, 1)])
                    mm = None
                    for t2 in (0, 1):
                        mm = e.tensor_tensor(
                            rmsacc[0:1, t2 * 512:(t2 + 1) * 512],
                            rmsacc[0:1, t2 * 512:(t2 + 1) * 512],
                            psc[t2][0:1, 0:512], ALU.add)
                    mm.then_inc(s_dve.h, 1)
                R["stats_dve"][g] = s_dve.inc()
                return act_cls, pe_cls, v_st

            def emit_g2(g):
                vg2 = [s_pool.inc() for _ in range(2)]

                def vg_g2(e, g=g, vg2=vg2):
                    e.wait_ge(s_act.h, R["sq_act"][(g, 1)])
                    aa = None
                    for h_loc in range(HPG):
                        h = HPG * g + h_loc
                        aa = e.tensor_tensor(
                            attn[:, h * T:(h + 1) * T],
                            attn[:, h * T:(h + 1) * T],
                            gsig[g % 2][:, h_loc * T:(h_loc + 1) * T],
                            ALU.mult)
                    aa.then_inc(s_pool.h, 1)
                    e.wait_ge(s_pool.h, vg2[0])
                    aa = None
                    for h_loc in range(HPG):
                        h = HPG * g + h_loc
                        aa = e.tensor_scalar(
                            attn[:, h * T:(h + 1) * T],
                            attn[:, h * T:(h + 1) * T],
                            nwsb[:, h:h + 1], None, ALU.mult)
                    aa.then_inc(s_pool.h, 1)
                p2.em("vector", vg_g2)
                R["g2"][g] = vg2[1]

            R["stats_pe"] = {}
            emit_wloads(0)
            emit_tabs(0)
            for s in range(NG + 1):
                ga = s - 1
                have_mm = s < NG
                have_at = 0 <= ga < NG
                half_idx = [0]

                def put_half():
                    if not have_mm or half_idx[0] >= 16:
                        return
                    j, hf = divmod(half_idx[0], 2)
                    p2.em("tensor", mk_qkv_half(s, j, hf))
                    if hf == 1:
                        a_ev, tr_cl = mk_qkv_evac(s, j)
                        p2.em("scalar", a_ev)
                        if tr_cl is not None:
                            p2.em("sync", tr_cl)
                    half_idx[0] += 1

                if have_at:
                    for n in range(NB):
                        t_sc, dscls = mk_scores(ga, n)
                        p2.em("tensor", t_sc)
                        for v in dscls:
                            p2.em("vector", v)
                        put_half()
                        t_acc, acls = mk_acc(ga, n)
                        p2.em("tensor", t_acc)
                        for a in acls:
                            p2.em("scalar", a)
                        put_half()
                    p2.em("vector", mk_mask(ga))
                    prev_add = {}
                    for pi in range(NB):
                        pe_cls, dve_cls = mk_corr_pair(ga, pi, prev_add)
                        for t in pe_cls:
                            p2.em("tensor", t)
                        for v in dve_cls:
                            p2.em("vector", v)
                        put_half()
                    act_cls, pe_cls, v_st = mk_stats(ga)
                    for a, t in zip(act_cls, pe_cls):
                        p2.em("scalar", a)
                        p2.em("tensor", t)
                    p2.em("vector", v_st)
                if have_mm:
                    if not have_at:
                        while half_idx[0] < 8:
                            put_half()
                    emit_kdk(s)
                    for n in range(NB):
                        t_kvd, vks = mk_kvd(s, n)
                        p2.em("tensor", t_kvd)
                        for v in vks:
                            p2.em("vector", v)
                        put_half()
                    while half_idx[0] < 16:
                        put_half()
                    R["qkv_evac"][s] = s_act.v
                    R["tr"][s] = s_tr[s % 2].v
                    R["kv_chain_dve"][s] = s_dve.v
                    if s + 1 < NG:
                        emit_wloads(s + 1, 0, 3)
                    emit_collective(s)
                    emit_qdq(s)
                    if s + 1 < NG:
                        emit_wloads(s + 1, 3, 16)
                        emit_tabs(s + 1)
                if have_at:
                    R["attn_pe"][ga] = s_pe.v
                    emit_g2(ga)

            p2.flush(nc)

            c2.close()
            sb4 = lambda n, s, d: cA.enter_context(nc.sbuf_tensor(n, s, d))
            rbv = sb4("rbv", [128, T], F32)
            vec4 = sb4("p4_vec", [1, T], F32)
            x1sl = [sb4(f"x1sl{i}", [128, T], F32) for i in range(2)]
            tsl = [sb4(f"tsl{i}", [128, T], F32) for i in range(2)]
            x2sl = [sb4(f"x2sl{i}", [128, T], F32) for i in range(2)]
            wb4 = [sb4(f"wb4{i}", [128, NCH * 128], BF16) for i in range(2)]

            p4 = Prog()
            R4 = dict(w_pe={}, dve1={}, pool2={}, x2w={})

            v_a1 = s_act.inc()

            def a_rstd(e, v1=v_a1):
                e.activation(vec4[0:1, 0:T], rmsacc[:], AF.Ln,
                             bias=epssb[0:1, :],
                             scale=1.0 / 4096.0).then_inc(s_act.h, 1)
                e.wait_ge(s_act.h, v1)
                e.activation(rmsacc[0:1, 0:T], vec4[0:1, 0:T], AF.Exp,
                             scale=-0.5).then_inc(s_act.h, 1)
            p4.em("scalar", a_rstd)
            v_d_rstd = s_act.inc()

            def t_rb(e, vd=v_d_rstd):
                e.wait_ge(s_act.h, vd)
                e.matmul(pmm[0][:, 0:512], ones1x128[:],
                         rmsacc[0:1, 0:512], start=True, stop=True)
                e.matmul(pmm[1][:, 0:512], ones1x128[:],
                         rmsacc[0:1, 512:T],
                         start=True, stop=True).then_inc(s_pe.h, 1)
            p4.em("tensor", t_rb)
            v_p_rb = s_pe.inc()

            def v_rb(e, vp=v_p_rb):
                e.wait_ge(s_pe.h, vp)
                e.tensor_copy(rbv[:, 0:512], pmm[0][:, 0:512])
                e.tensor_copy(rbv[:, 512:T],
                              pmm[1][:, 0:512]).then_inc(s_dve.h, 1)
            p4.em("vector", v_rb)
            v_rb_done = s_dve.inc()
            if debug:
                def gp_d2(e, v=v_rb_done):
                    e.wait_ge(s_dve.h, v)
                    e.dma_start(out=dbg["attn"].ap(),
                                in_=attn[:]).then_inc(s_dbg.h, 16)
                    e.dma_start(out=dbg["rstd"].ap(),
                                in_=rmsacc[0:1, 0:T]).then_inc(s_dbg.h, 16)
                p4.em("gpsimd", gp_d2)
                s_dbg.inc(32)
            for oc in range(32):
                par = oc % 2

                def gp_w4(e, oc=oc, par=par):
                    if oc >= 2:
                        e.wait_ge(s_pe.h, R4["w_pe"][oc - 2])
                    e.dma_start(out=wb4[par][:],
                                in_=woutd.ap()[oc]).then_inc(s_w4[par].h, 16)
                p4.em("gpsimd", gp_w4)
                s_w4[par].inc(16)
                vw4 = s_w4[par].v

                def sy_x1(e, oc=oc, par=par):
                    if oc == 0:
                        e.wait_ge(s_x1w.h, 16 * NT)
                    if oc >= 2:
                        e.wait_ge(s_dve.h, R4["pool2"][oc - 2])
                    for n in range(NT):
                        e.dma_start(
                            out=x1sl[par][:, n * TT:(n + 1) * TT],
                            in_=x1d.ap()[n][:, oc * TT:(oc + 1) * TT]
                        ).then_inc(s_x1r[par].h, 16)
                p4.em("sync", sy_x1)
                s_x1r[par].inc(16 * NT)
                vx1 = s_x1r[par].v

                b0, b1 = pmm[2 * par], pmm[2 * par + 1]

                def t_out(e, oc=oc, par=par, b0=b0, b1=b1, vw=vw4,
                          vrb=v_rb_done):
                    e.wait_ge(s_w4[par].h, vw)
                    if oc < 2:
                        e.wait_ge(s_dve.h, vrb)
                    if oc >= 2:
                        e.wait_ge(s_dve.h, R4["dve1"][oc - 2])
                    mm = None
                    for c in range(NCH):
                        for t2, bank in ((0, b0), (1, b1)):
                            mm = e.matmul(
                                bank[:, 0:512],
                                wb4[par][:, c * 128:(c + 1) * 128],
                                attn[:, c * T + t2 * 512:c * T + t2 * 512 + 512],
                                start=(c == 0), stop=(c == NCH - 1))
                    mm.then_inc(s_pe.h, 1)
                p4.em("tensor", t_out)
                R4["w_pe"][oc] = s_pe.inc()

                v12 = [s_dve.inc() for _ in range(2)]

                def v_o1(e, oc=oc, par=par, b0=b0, b1=b1,
                         vp=R4["w_pe"][oc], vx=vx1, v12=v12):
                    e.wait_ge(s_pe.h, vp)
                    mm = None
                    for t2, bank in ((0, b0), (1, b1)):
                        mm = e.tensor_tensor(
                            tsl[par][:, t2 * 512:(t2 + 1) * 512],
                            bank[:, 0:512], rbv[:, t2 * 512:(t2 + 1) * 512],
                            ALU.mult)
                    mm.then_inc(s_dve.h, 1)
                    e.wait_ge(s_dve.h, v12[0])
                    e.wait_ge(s_x1r[par].h, vx)
                    if oc >= 2:
                        e.wait_ge(s_x2w[par].h, R4["x2w"][oc - 2])
                    e.tensor_tensor(x2sl[par][:], tsl[par][:], x1sl[par][:],
                                    ALU.add).then_inc(s_dve.h, 1)
                p4.em("vector", v_o1)
                R4["dve1"][oc] = v12[0]
                R4["pool2"][oc] = v12[1]

                def sy_x2(e, oc=oc, par=par, vp=v12[1]):
                    e.wait_ge(s_dve.h, vp)
                    e.dma_start(out=x2d.ap()[oc],
                                in_=x2sl[par][:]).then_inc(s_x2w[par].h, 16)
                p4.em("sync", sy_x2)
                s_x2w[par].inc(16)
                R4["x2w"][oc] = s_x2w[par].v



            def sy_fin4(e):
                e.wait_ge(s_x2w[0].h, 16 * 16)
                e.wait_ge(s_x2w[1].h, 16 * 16)
            p4.em("sync", sy_fin4)

            p4.flush(nc)

        with contextlib.ExitStack() as c4:
            p4b = Prog()
            tb4 = mk_ffn_bufs(c4, "p4b_")
            tb4["sx"], tb4["sxb"] = s_x2r, s_x2rb
            emit_ab_loads(p4b, tb4,
                          [(s_x2w[0], 16 * 16), (s_x2w[1], 16 * 16)])

            def x2src_f(t0):
                return [(tb4["xf"][:, c * TT:(c + 1) * TT],
                         x2d.ap()[c][:, t0 * TT:(t0 + 1) * TT])
                        for c in range(NCH)]

            def x2src_b(t0):
                return [(tb4["bufB"][t0 % 2][:, c * TT:(c + 1) * TT],
                         x2d.ap()[c][:, t0 * TT:(t0 + 1) * TT])
                        for c in range(NCH)]

            emit_ffn_ln(
                p4b, tb4,
                x_f32_src=x2src_f,
                x_b16_src=x2src_b,
                res_dst=lambda t0: yd.ap()[t0],
                res_sem=s_y,
                pre_waits=[(s_x2w[0], 16 * 16), (s_x2w[1], 16 * 16)] +
                          ([(s_dbg, s_dbg.v)] if debug else []),
                second_ln_dest=None)

            def gp_fin(e, v=s_y.v, vd=s_dbg.v):
                e.wait_ge(s_y.h, v)
                if vd:
                    e.wait_ge(s_dbg.h, vd)
            p4b.em("gpsimd", gp_fin)
            p4b.flush(nc)

    return nc


def _prep_shared(L0_attn_qkv_w, L0_attn_out_w, L0_attn_gate_w, L0_attn_norm_w,
                 L0_ffn_A, L0_ffn_B, L0_ffn_Wr, L0_ffn_br):
    f32 = np.float32

    def wtiles(w, no):
        return np.ascontiguousarray(
            w.reshape(no, 128, NCH, 128).transpose(0, 3, 2, 1).reshape(
                no, 128, NCH * 128)).astype(f32)

    d = {}
    d["wqkv"] = wtiles(np.asarray(L0_attn_qkv_w), 96)
    d["wgate"] = wtiles(np.asarray(L0_attn_gate_w), 32)
    d["wout"] = wtiles(np.asarray(L0_attn_out_w), 32)
    A_flat = np.asarray(L0_ffn_A).reshape(128, 4096)
    d["Aw"] = np.ascontiguousarray(
        A_flat.T.reshape(NCH, 128, 128).transpose(1, 0, 2).reshape(
            128, NCH * 128)).astype(f32)
    B_flat = np.asarray(L0_ffn_B).transpose(0, 2, 1).reshape(128, 4096)
    d["Bw"] = np.ascontiguousarray(B_flat).astype(f32)
    Wr = np.asarray(L0_ffn_Wr)
    d["Wrw"] = np.ascontiguousarray(
        Wr.T.reshape(NCH, 128, 8).transpose(1, 0, 2).reshape(
            128, NCH * 8)).astype(f32)
    d["nww"] = np.ascontiguousarray(
        np.asarray(L0_attn_norm_w).reshape(NCH, 128).T).astype(f32)
    d["brw"] = np.asarray(L0_ffn_br).reshape(8, 1).astype(f32)
    d["kdw"] = _KDH
    d["qdbw"] = np.ascontiguousarray(
        _QDB.reshape(NG, HPG, 128, 256).transpose(0, 2, 1, 3).reshape(
            NG, 128, HPG * 256))
    d["ddTw"] = _DDT
    d["Ew"] = _E8.astype(np.float32)
    d["onesw"] = np.ones((128, 1), dtype=BF)
    d["negonesw"] = np.full((128, 1), -1.0, dtype=BF)
    d["ones1x128w"] = np.ones((1, 128), dtype=f32)
    d["ones1x8w"] = np.ones((1, 8), dtype=f32)
    d["ones8w"] = np.ones((8, 1), dtype=f32)
    d["epsw"] = np.full((1, 1), EPS, dtype=f32)
    return d


def _prep_x(x, core):
    b, g = core // 2, core % 2
    xs = np.asarray(x)[b, g * T:(g + 1) * T, :]
    t = xs.T.reshape(NCH, 128, NT, TT).transpose(2, 1, 0, 3)
    return np.ascontiguousarray(t.reshape(NT, 128, NCH * TT)).astype(np.float32)


def _unprep_y(y):
    t = np.asarray(y).reshape(NT, 128, NCH, TT).transpose(2, 1, 0, 3)
    return t.reshape(4096, T).T


def kernel(L0_attn_hidden_states, L0_attn_qkv_w, L0_attn_out_w,
           L0_attn_gate_w, L0_attn_norm_w, L0_ffn_A, L0_ffn_B, L0_ffn_Wr,
           L0_ffn_br):
    shared = _prep_shared(L0_attn_qkv_w, L0_attn_out_w, L0_attn_gate_w,
                          L0_attn_norm_w, L0_ffn_A, L0_ffn_B, L0_ffn_Wr,
                          L0_ffn_br)
    in_maps = []
    for c in range(8):
        m = dict(shared)
        m["x0"] = _prep_x(L0_attn_hidden_states, c)
        m["m0w"] = np.full((128, 1), float(c % 2), np.float32)
        in_maps.append(m)
    nc = build_nc()
    res = run_bass_kernel_spmd(nc, in_maps, core_ids=list(range(8)))
    out = np.zeros((4, 2048, 4096), dtype=np.float32)
    for c in range(8):
        b, g = c // 2, c % 2
        out[b, g * T:(g + 1) * T, :] = _unprep_y(res.results[c]["y"])
    return out


if __name__ == "__main__":
    nc = build_nc()
    n_inst = sum(len(b.instructions) for b in nc.m.functions[0].blocks)
    print("built ok, instructions:", n_inst)
